# version 18 (final)
# baseline (speedup 1.0000x reference)
import math
import numpy as np
import concourse.bass as bass
import concourse.mybir as mybir
from concourse.bass_utils import run_bass_kernel_spmd

F32 = mybir.dt.float32
BF16 = mybir.dt.bfloat16
AF = mybir.ActivationFunctionType
ALU = mybir.AluOpType
AX = mybir.AxisListType

T = 4096
D = 1024
NT = T // 128
DC = D // 128


class Sem:
    def __init__(self, nc, stack, name):
        self.h = stack.enter_context(nc.semaphore(name))
        self.v = 0

    def inc(self, d=1):
        self.v += d
        return self.v


class Plan:
    ENGINES = ("tensor", "scalar", "vector", "gpsimd", "sync")

    def __init__(self):
        self.q = {e: [] for e in self.ENGINES}

    def add(self, eng, fn):
        self.q[eng].append(fn)

    def run(self, nc):
        with nc.Block() as block:
            @block.tensor
            def _(e):
                for f in self.q["tensor"]:
                    f(e)

            @block.scalar
            def _(e):
                for f in self.q["scalar"]:
                    f(e)

            @block.vector
            def _(e):
                for f in self.q["vector"]:
                    f(e)

            @block.gpsimd
            def _(e):
                for f in self.q["gpsimd"]:
                    f(e)

            @block.sync
            def _(e):
                for f in self.q["sync"]:
                    f(e)


def wait(plan, eng, sem, val):
    plan.add(eng, lambda e, s=sem.h, v=val: e.wait_ge(s, v))


def _ap(x):
    a = getattr(x, "ap", None)
    return x.ap() if callable(a) else x


def pe_mark(plan, mm):
    v = mm.inc()
    last = plan.q["tensor"].pop()
    plan.add("tensor", lambda e, f=last, h=mm.h: f(e).then_inc(h, 1))
    return v


def emit_derive_xT(nc, plan, ctx, x_f32_hbm, xbf_hbm, xT_tiles, done_sem,
                   dma_eng="gpsimd", tr_eng="sync", pre_wait=None):
    cast = ctx["mksem"]("cast")
    x_ap = _ap(x_f32_hbm)
    xbf_ap = _ap(xbf_hbm)
    if pre_wait is not None:
        wait(plan, dma_eng, *pre_wait)
    slab = T // 4
    for i in range(4):
        v = cast.inc(16)
        plan.add(dma_eng, lambda e, i=i, v=v, slab=slab: e.dma_start(
            xbf_ap[i * slab:(i + 1) * slab, :], x_ap[i * slab:(i + 1) * slab, :]
        ).then_inc(cast.h, 16))
    cast_done = cast.v
    wait(plan, tr_eng, cast, cast_done)
    vals = []
    for c in range(DC):
        v = done_sem.inc(16)
        plan.add(tr_eng, lambda e, c=c, v=v: e.dma_start(
            xT_tiles[c][0:128, 0:T], xbf_ap[:, c * 128:(c + 1) * 128], transpose=True
        ).then_inc(done_sem.h, 16))
        vals.append(v)
    return vals


def emit_bias_broadcast(nc, plan, ctx, b_hbm, bbc_sb, ncols, ready_sem):
    brow = ctx["brow_sb"]
    ones1 = ctx["ones1"]
    ld = ctx["mksem"]("brow")
    mm = ctx["mm_sem"]
    v = ld.inc(16)
    plan.add("gpsimd", lambda e, v=v: e.dma_start(
        brow[0:1, 0:ncols], _ap(b_hbm).rearrange("(o d) -> o d", o=1)).then_inc(ld.h, 16))
    wait(plan, "tensor", ld, v)
    wait(plan, "tensor", ctx["init_sem"], ctx["init_ones"])
    ps = ctx["ps"]
    for j in range(ncols // 512):
        plan.add("tensor", lambda e, j=j: e.matmul(
            ps[4][:, 0:512] if j == 0 else ps[5][:, 0:512],
            ones1[0:1, :], brow[0:1, j * 512:(j + 1) * 512],
            start=True, stop=True))
    pe_mark(plan, mm)
    wait(plan, "scalar", mm, mm.v)
    for j in range(ncols // 512):
        rv = ready_sem.inc()
        plan.add("scalar", lambda e, j=j, rv=rv: e.copy(
            bbc_sb[:, j * 512:(j + 1) * 512],
            (ctx["ps"][4] if j == 0 else ctx["ps"][5])[:, 0:512]
        ).then_inc(ready_sem.h, 1))


def emit_ln_epilogue(nc, plan, ctx, y_sb, x1_sb, eps=1e-5):
    lnv = ctx["ln_v"]; lns = ctx["ln_s"]
    par = ctx["ln_par"][0]
    ctx["ln_par"][0] = 1 - par
    st = ctx["ln_st"][par]
    mean = st[:, 12:13]; var = st[:, 13:14]
    lnt = st[:, 14:15]; rs = st[:, 15:16]; nmrs = st[:, 16:17]
    ps_ = ctx.get(f"ln_prev_s{par}", 0)
    pv_ = ctx.get(f"ln_prev_f{par}", 0)
    if ps_:
        wait(plan, "vector", lns, ps_)
    wait(plan, "vector", lnv, max(pv_, lnv.v))
    for half in range(2):
        lnv.inc()
        plan.add("vector", lambda e, half=half: e.bn_stats(
            st[:, 6 * half:6 * half + 6],
            y_sb[:, 512 * half:512 * half + 512]).then_inc(lnv.h, 1))
    wait(plan, "vector", lnv, lnv.v)
    v2 = lnv.inc()
    plan.add("vector", lambda e: e.bn_aggr(st[:, 12:14], st[:, 0:12]).then_inc(lnv.h, 1))
    wait(plan, "scalar", lnv, v2)
    v2s = lns.inc()
    plan.add("scalar", lambda e: e.activation(lnt, var, AF.Ln, bias=ctx["eps_ln"][:]).then_inc(lns.h, 1))
    wait(plan, "scalar", lns, v2s)
    v3 = lns.inc()
    plan.add("scalar", lambda e: e.activation(rs, lnt, AF.Exp, scale=-0.5).then_inc(lns.h, 1))
    wait(plan, "vector", lns, v3)
    vd = lnv.inc()
    plan.add("vector", lambda e: e.scalar_tensor_tensor(
        nmrs, mean, -1.0, rs, ALU.mult, ALU.mult).then_inc(lnv.h, 1))
    wait(plan, "vector", lnv, vd)
    vf = lnv.inc()
    plan.add("vector", lambda e: e.tensor_scalar(
        x1_sb[:], y_sb[:], rs, nmrs, ALU.mult, ALU.add).then_inc(lnv.h, 1))
    ctx[f"ln_prev_s{par}"] = v3
    ctx[f"ln_prev_f{par}"] = vf
    return vf


def mk_common(nc, stack, plan, pool=None):
    ctx = {}
    ctx["_pool"] = pool
    ctx["_off"] = [0]
    ctx["dma_list"] = []

    def salloc(name_, shape, dtype):
        if pool is None:
            return stack.enter_context(nc.sbuf_tensor(name_, shape, dtype))
        nbytes = shape[1] * mybir.dt.size(dtype)
        nbytes = (nbytes + 63) // 64 * 64
        off = ctx["_off"][0]
        ctx["_off"][0] = off + nbytes
        assert ctx["_off"][0] <= pool.shape[1], (
            f"pool overflow: {name_} needs {nbytes}B at {off}, pool {pool.shape[1]}")
        return pool[0:shape[0], off:off + shape[1] * mybir.dt.size(dtype)].bitcast(dtype)

    ctx["salloc"] = salloc
    ctx["pool_reset"] = lambda: ctx["_off"].__setitem__(0, 0)
    ctx["psB"] = [stack.enter_context(nc.psum_tensor(f"psB{i}", [128, 1024], F32))
                  for i in range(4)]
    ctx["ps"] = [ctx["psB"][i // 2][:, (i % 2) * 512:(i % 2) * 512 + 512]
                 for i in range(8)]
    ctx["ones1"] = stack.enter_context(nc.sbuf_tensor("ones1", [1, 128], BF16))
    ctx["ones128"] = stack.enter_context(nc.sbuf_tensor("ones128", [128, 1], BF16))
    ctx["brow_sb"] = stack.enter_context(nc.sbuf_tensor("brow", [1, 1024], BF16))
    ctx["ln_st"] = [stack.enter_context(nc.sbuf_tensor(f"ln_st{i}", [128, 18], F32)) for i in range(2)]
    ctx["ln_par"] = [0]
    ctx["eps_ln"] = stack.enter_context(nc.sbuf_tensor("eps_ln", [128, 1], F32))
    for nm_ in ("ld_sem", "mm_sem", "ln_v", "ln_s", "init_sem"):
        ctx[nm_] = Sem(nc, stack, nm_)
    _n = [0]
    def _mksem(tag="s"):
        _n[0] += 1
        s = Sem(nc, stack, f"{tag}{_n[0]}")
        ctx["dma_list"].append(s)
        return s
    ctx["mksem"] = _mksem
    def _mksem_nobar(tag="s"):
        _n[0] += 1
        return Sem(nc, stack, f"{tag}{_n[0]}")
    ctx["mksem_nobar"] = _mksem_nobar
    iv = ctx["init_sem"].inc(1)
    plan.add("vector", lambda e: e.memset(ctx["ones1"][:], 1.0))
    plan.add("vector", lambda e: e.memset(ctx["ones128"][:], 1.0))
    plan.add("vector", lambda e, v=iv: e.memset(ctx["eps_ln"][:], 1e-5).then_inc(ctx["init_sem"].h, 1))
    ctx["init_ones"] = iv
    return ctx


def build_stage4():
    nc = bass.Bass("TRN2", target_bir_lowering=False)
    x3 = nc.dram_tensor("x3", [T, D], F32, kind="ExternalInput").ap()
    wfr = nc.dram_tensor("wfr", [D, D], F32, kind="ExternalInput").ap()
    wfi = nc.dram_tensor("wfi", [D, D], F32, kind="ExternalInput").ap()
    bias = nc.dram_tensor("bias", [D], F32, kind="ExternalInput").ap()
    fr = nc.dram_tensor("fr", [D, D], F32, kind="ExternalInput").ap()
    fi = nc.dram_tensor("fi", [D, D], F32, kind="ExternalInput").ap()
    fineg = nc.dram_tensor("fineg", [D, D], F32, kind="ExternalInput").ap()
    gr = nc.dram_tensor("gr", [D, D], F32, kind="ExternalInput").ap()
    gineg = nc.dram_tensor("gineg", [D, D], F32, kind="ExternalInput").ap()
    out = nc.dram_tensor("out", [T, D], F32, kind="ExternalOutput").ap()
    x3bf = nc.dram_tensor("x3bf", [T, D], BF16, kind="Internal").ap()

    import contextlib
    with contextlib.ExitStack() as stack:
        plan = Plan()
        ctx = mk_common(nc, stack, plan)
        emit_stage4(nc, stack, plan, ctx, x3, x3bf, wfr, wfi, bias,
                    fr, fi, fineg, gr, gineg, out)
        plan.run(nc)
    return nc


def emit_stage4(nc, stack, plan, ctx, x3, x3bf, wfr, wfi, bias,
                fr, fi, fineg, gr, gineg, out, name="s4"):
    sb = lambda n, shp, dt: ctx["salloc"](name + n, shp, dt)
    cA = [sb(f"cA{j}", [128, 4096], BF16) for j in range(DC)]
    wA = [sb(f"wA{j}", [128, 2048], BF16) for j in range(DC)]
    aT = [sb(f"aT{j}", [128, 2048], BF16) for j in range(DC)]
    mS = [sb(f"mS{j}", [128, 1024], BF16) for j in range(DC)]
    bbc = sb("bbc", [128, 1024], F32)
    ysb = [sb(f"y{j}", [128, 1024], F32) for j in range(2)]
    osb = [sb(f"o{j}", [128, 1024], F32) for j in range(2)]
    xres = [sb(f"xr{j}", [128, 1024], F32) for j in range(2)]

    mm = ctx["mm_sem"]
    ld = ctx["mksem"](name + "ld")
    gld = ctx["mksem"](name + "gld")
    stm = [ctx["mksem"](name + "stA"), ctx["mksem"](name + "stB")]
    ev = Sem(nc, stack, name + "ev")
    xt = Sem(nc, stack, name + "xt")
    bb = Sem(nc, stack, name + "bb")

    def cast_load(dst_tiles, src_ap, col0, sem, eng="gpsimd"):
        for j in range(DC):
            sem.inc(16)
            plan.add(eng, lambda e, j=j: e.dma_start(
                dst_tiles[j][:, col0:col0 + D], src_ap[j * 128:(j + 1) * 128, :]
            ).then_inc(sem.h, 16))

    cast_load(cA, fr, 0, ld)
    cast_load(cA, fi, 1024, ld)
    cast_load(cA, fineg, 2048, ld)
    cast_load(wA, wfr, 0, ld)
    cast_load(wA, wfi, 1024, ld)
    consts_done = ld.v

    emit_bias_broadcast(nc, plan, ctx, bias, bbc, D, bb)
    bb_done = bb.v

    wait(plan, "tensor", ld, consts_done)
    wait(plan, "tensor", ctx["init_sem"], ctx["init_ones"])
    s1_ev = []
    for kt in range(DC):
        for half, (t0, t1) in enumerate(((0, 2048), (1024, 0))):
            for dch in range(2):
                bank = ctx["ps"][(kt * 4 + half * 2 + dch) % 4]
                for j in range(DC):
                    if half == 0:
                        ops = [(wA[j][:, kt * 128:kt * 128 + 128], cA[j][:, dch * 512:dch * 512 + 512]),
                               (wA[j][:, 1024 + kt * 128:1024 + kt * 128 + 128], cA[j][:, 2048 + dch * 512:2048 + dch * 512 + 512])]
                    else:
                        ops = [(wA[j][:, 1024 + kt * 128:1024 + kt * 128 + 128], cA[j][:, dch * 512:dch * 512 + 512]),
                               (wA[j][:, kt * 128:kt * 128 + 128], cA[j][:, 1024 + dch * 512:1024 + dch * 512 + 512])]
                    for oi, (l_, r_) in enumerate(ops):
                        st_ = (j == 0 and oi == 0)
                        sp_ = (j == DC - 1 and oi == 1)
                        plan.add("tensor", lambda e, b=bank, l_=l_, r_=r_, st_=st_, sp_=sp_:
                                 e.matmul(b[:, 0:512], l_, r_, start=st_, stop=sp_))
                pe_mark(plan, mm)
                col0 = half * 1024 + dch * 512
                wait(plan, "scalar", mm, mm.v)
                v = ev.inc()
                plan.add("scalar", lambda e, kt=kt, col0=col0, b=bank, v=v:
                         e.copy(aT[kt][:, col0:col0 + 512], b[:, 0:512]).then_inc(ev.h, 1))
                s1_ev.append(v)
                if len(s1_ev) >= 4:
                    wait(plan, "tensor", ev, s1_ev[-4])
    s1_done = ev.v

    wait(plan, "gpsimd", mm, mm.v)
    cast_load(wA, gr, 0, gld)
    cast_load(wA, gineg, 1024, gld)
    g_all = gld.v

    wait(plan, "sync", mm, mm.v)
    emit_derive_xT(nc, plan, ctx, x3, x3bf, cA, xt, pre_wait=None)
    xt_all = xt.v

    wait(plan, "tensor", gld, g_all)
    wait(plan, "tensor", ev, s1_done)
    s2_ev = []
    for dt_ in range(DC):
        for och in range(2):
            bank = ctx["ps"][(dt_ * 2 + och) % 4]
            n_mm = 0
            for k in range(DC):
                for oi, (l_, r_) in enumerate((
                        (aT[k][:, dt_ * 128:dt_ * 128 + 128], wA[k][:, och * 512:och * 512 + 512]),
                        (aT[k][:, 1024 + dt_ * 128:1024 + dt_ * 128 + 128], wA[k][:, 1024 + och * 512:1024 + och * 512 + 512]))):
                    st_ = (k == 0 and oi == 0)
                    sp_ = (k == DC - 1 and oi == 1)
                    plan.add("tensor", lambda e, b=bank, l_=l_, r_=r_, st_=st_, sp_=sp_:
                             e.matmul(b[:, 0:512], l_, r_, start=st_, stop=sp_))
            pe_mark(plan, mm)
            wait(plan, "scalar", mm, mm.v)
            v = ev.inc()
            plan.add("scalar", lambda e, dt_=dt_, och=och, b=bank, v=v:
                     e.copy(mS[dt_][:, och * 512:och * 512 + 512], b[:, 0:512]).then_inc(ev.h, 1))
            s2_ev.append(v)
            if len(s2_ev) >= 4:
                wait(plan, "tensor", ev, s2_ev[-4])
    m_done = ev.v

    wait(plan, "tensor", ev, m_done)
    wait(plan, "tensor", xt, xt_all)
    wait(plan, "tensor", bb, bb_done)
    res_sem = [ctx["mksem"](name + "rA"), ctx["mksem"](name + "rB")]
    vv_of, ln_of, st_of = {}, {}, {}
    for nt in range(NT):
        pb = nt % 2
        banks = (ctx["ps"][4 + 2 * pb], ctx["ps"][5 + 2 * pb])
        if nt >= 2:
            wait(plan, "sync", ctx["ln_v"], vv_of[nt - 2])
        v = res_sem[pb].inc(16)
        plan.add("sync", lambda e, nt=nt, pb=pb, v=v: e.dma_start(
            xres[pb][:], x3[nt * 128:(nt + 1) * 128, :]).then_inc(res_sem[pb].h, 16))
        if nt >= 2:
            wait(plan, "tensor", ctx["ln_v"], vv_of[nt - 2])
        for j in range(DC):
            st_ = (j == 0)
            sp_ = (j == DC - 1)
            l_ = cA[j][:, nt * 128:(nt + 1) * 128]
            for och in range(2):
                r_ = mS[j][:, och * 512:och * 512 + 512]
                plan.add("tensor", lambda e, b=banks[och], l_=l_, r_=r_, st_=st_, sp_=sp_:
                         e.matmul(b[:, 0:512], l_, r_, start=st_, stop=sp_))
        mm_here = pe_mark(plan, mm)
        wait(plan, "vector", mm, mm_here)
        wait(plan, "vector", res_sem[pb], v)
        vpre = ctx["ln_v"].inc(2)
        for och in range(2):
            plan.add("vector", lambda e, b=banks[och], och=och, pb=pb:
                     e.tensor_add(ysb[pb][:, och * 512:och * 512 + 512], b[:, 0:512],
                                  bbc[:, och * 512:och * 512 + 512]).then_inc(ctx["ln_v"].h, 1))
        wait(plan, "vector", ctx["ln_v"], vpre)
        vv = ctx["ln_v"].inc()
        plan.add("vector", lambda e, pb=pb, v=vv: e.tensor_add(
            ysb[pb][:], ysb[pb][:], xres[pb][:]).then_inc(ctx["ln_v"].h, 1))
        vv_of[nt] = vv
        wait(plan, "scalar", ctx["ln_v"], vv)
        if nt >= 2:
            wait(plan, "vector", stm[pb], st_of[nt - 2])
        lnfin = emit_ln_epilogue(nc, plan, ctx, ysb[pb], osb[pb])
        ln_of[nt] = lnfin
        wait(plan, "gpsimd", ctx["ln_v"], lnfin)
        sv = stm[pb].inc(16)
        st_of[nt] = sv
        plan.add("gpsimd", lambda e, nt=nt, pb=pb: e.dma_start(
            out[nt * 128:(nt + 1) * 128, :], osb[pb][:]).then_inc(stm[pb].h, 16))
    wait(plan, "gpsimd", stm[0], stm[0].v)
    wait(plan, "gpsimd", stm[1], stm[1].v)
    return




def emit_barrier(nc, plan, ctx):
    for eng in Plan.ENGINES:
        wait(plan, eng, ctx["mm_sem"], ctx["mm_sem"].v)
        wait(plan, eng, ctx["ln_s"], ctx["ln_s"].v)
        wait(plan, eng, ctx["ln_v"], ctx["ln_v"].v)
        for s in ctx["dma_list"]:
            if s.v:
                wait(plan, eng, s, s.v)


def build_stage1():
    nc = bass.Bass("TRN2", target_bir_lowering=False)
    x = nc.dram_tensor("x", [T, D], F32, kind="ExternalInput").ap()
    wqkv = nc.dram_tensor("wqkv", [D, 3 * D], F32, kind="ExternalInput").ap()
    wproj = nc.dram_tensor("wproj", [D, D], F32, kind="ExternalInput").ap()
    bproj = nc.dram_tensor("bproj", [D], F32, kind="ExternalInput").ap()
    masks = nc.dram_tensor("masks", [128, 2048], BF16, kind="ExternalInput").ap()
    x1 = nc.dram_tensor("x1", [T, D], F32, kind="ExternalOutput").ap()
    xbf = nc.dram_tensor("xbf", [T, D], BF16, kind="Internal").ap()
    ahbm = nc.dram_tensor("ahbm", [D, T], BF16, kind="Internal").ap()

    import contextlib
    with contextlib.ExitStack() as stack:
        plan = Plan()
        ctx = mk_common(nc, stack, plan)
        emit_stage1(nc, stack, plan, ctx, x, xbf, wqkv, wproj, bproj, masks,
                    ahbm, x1)
        plan.run(nc)
    return nc


def emit_stage1(nc, stack, plan, ctx, x, xbf, wqkv, wproj, bproj, masks_h,
                ahbm, x1, name="s1"):
    sb = lambda n, shp, dt: ctx["salloc"](name + n, shp, dt)
    mksem = ctx["mksem"]
    NQB = T // 512
    arena = [sb(f"ar{c}", [128, T], BF16) for c in range(DC)]
    wg = [sb(f"wg{c}", [128, 768], BF16) for c in range(DC)]
    qTg = [sb(f"qT{p}", [128, T], BF16) for p in range(2)]
    kTg = [sb(f"kT{p}", [128, T], BF16) for p in range(2)]
    vext = sb("vext", [128, NT * 4 * 65], BF16)
    pbuf = sb("pbuf", [128, 2048], BF16)
    cb = sb("cb", [128, 16], F32)
    nbs = sb("nbs", [1, 16], F32)
    nbs_bf = sb("nbsb", [1, 2], BF16)
    msk = sb("msk", [128, 2048], BF16)
    apair = [sb(f"ap{p}", [128, 512], BF16) for p in range(2)]
    rrow = sb("rrow", [1, 1024], BF16)
    rbc = sb("rbc", [128, 512], F32)
    wpj = [sb(f"wp{c}", [128, 1024], BF16) for c in range(DC)]
    bbc = sb("bbc", [128, 1024], F32)
    ysb = [sb(f"y{p}", [128, 1024], F32) for p in range(2)]
    osb = [sb(f"o{p}", [128, 1024], F32) for p in range(2)]
    xres = [sb(f"xr{p}", [128, 1024], F32) for p in range(2)]

    ps = ctx["ps"]
    mm = ctx["mm_sem"]; lnv = ctx["ln_v"]; lns = ctx["ln_s"]
    xt = mksem(name + "xt")
    wld = mksem(name + "wld")
    pld = mksem(name + "pld")
    ast = [mksem(name + f"as{p}") for p in range(2)]
    atl = mksem(name + "atl")
    res2 = [mksem(name + "res2A"), mksem(name + "res2B")]
    stm = [mksem(name + f"st{p}") for p in range(2)]
    bb = Sem(nc, stack, name + "bb")

    emit_derive_xT(nc, plan, ctx, x, xbf, arena, xt)
    xt_all = xt.v
    for c in range(DC):
        pld.inc(16)
        plan.add("gpsimd", lambda e, c=c: e.dma_start(
            wpj[c][:], _ap(wproj)[c * 128:(c + 1) * 128, :]).then_inc(pld.h, 16))
    pld.inc(16)
    plan.add("gpsimd", lambda e: e.dma_start(msk[:], _ap(masks_h)).then_inc(pld.h, 16))
    pld_all = pld.v
    emit_bias_broadcast(nc, plan, ctx, bproj, bbc, D, bb)
    bb_done = bb.v
    lnv.inc()
    plan.add("vector", lambda e: e.memset(
        vext[:].rearrange("p (s c) -> p s c", c=65)[:, :, 64:65], 1.0).then_inc(lnv.h, 1))
    vext_ones = lnv.v

    projmark = {}
    tile12 = 0
    st_of = {}
    for g in range(4):
        pf = ctx.pop(name + "pending_fin", None)
        if pf is not None:
            pf()
        if g >= 1:
            wait(plan, "gpsimd", mm, projmark[g - 1])
        for c in range(DC):
            for oi, (src0, w_) in enumerate(((g * 256, 256), (D + g * 256, 256), (2 * D + g * 256, 256))):
                wld.inc(16)
                plan.add("gpsimd", lambda e, c=c, src0=src0, oi=oi: e.dma_start(
                    wg[c][:, oi * 256:(oi + 1) * 256],
                    _ap(wqkv)[c * 128:(c + 1) * 128, src0:src0 + 256]).then_inc(wld.h, 16))
        pcc = ctx.get("precast_chunks")
        if pcc:
            pcc.pop(0)()
        wld_here = wld.v
        wait(plan, "tensor", wld, wld_here)
        if g >= 1:
            wait(plan, "tensor", lns, lns.v)
            wait(plan, "tensor", lnv, lnv.v)
        if g == 0:
            wait(plan, "tensor", xt, xt_all)
            wait(plan, "tensor", ctx["init_sem"], ctx["init_ones"])
        evac = {}
        idx = 0
        for role, dstl, coloff, scale in (("q", qTg, 0, 0.125), ("k", kTg, 256, 1.0)):
            for pt in range(2):
                for nsb in range(T // 512):
                    bank = ps[idx % 2]
                    if idx >= 2:
                        wait(plan, "tensor", lns, evac[idx - 2])
                    for c in range(DC):
                        plan.add("tensor", lambda e, c=c, pt=pt, nsb=nsb, coloff=coloff, bank=bank:
                                 e.matmul(bank[:, 0:512],
                                          wg[c][0:128, coloff + pt * 128:coloff + (pt + 1) * 128],
                                          arena[c][0:128, nsb * 512:(nsb + 1) * 512],
                                          start=(c == 0), stop=(c == DC - 1)))
                    mk = pe_mark(plan, mm)
                    wait(plan, "scalar", mm, mk)
                    lns.inc()
                    plan.add("scalar", lambda e, dstl=dstl, pt=pt, nsb=nsb, bank=bank, scale=scale:
                             e.activation(dstl[pt][:, nsb * 512:(nsb + 1) * 512],
                                          bank[:, 0:512], AF.Copy, scale=float(scale)).then_inc(lns.h, 1))
                    evac[idx] = lns.v
                    idx += 1
        vidx = 0
        vevac = {}
        for nt in range(NT):
            bank = ps[2 + nt % 2]
            if vidx >= 2:
                wait(plan, "tensor", lns, vevac[vidx - 2])
            for c in range(DC):
                plan.add("tensor", lambda e, c=c, nt=nt, bank=bank:
                         e.matmul(bank[:, 0:256],
                                  arena[c][0:128, nt * 128:(nt + 1) * 128],
                                  wg[c][0:128, 512:768],
                                  start=(c == 0), stop=(c == DC - 1)))
            mk = pe_mark(plan, mm)
            wait(plan, "scalar", mm, mk)
            if g == 0 and nt == 0:
                wait(plan, "scalar", lnv, vext_ones)
            for hg in range(4):
                lns.inc()
                plan.add("scalar", lambda e, nt=nt, hg=hg, bank=bank: e.copy(
                    vext[:, (nt * 4 + hg) * 65:(nt * 4 + hg) * 65 + 64],
                    bank[:, hg * 64:(hg + 1) * 64]).then_inc(lns.h, 1))
            vevac[vidx] = lns.v
            vidx += 1
        projmark[g] = mm.v
        proj_evac_all = lns.v

        wait(plan, "scalar", lns, proj_evac_all)
        for pt in range(2):
            for sl, srcl in ((0, qTg), (1, kTg)):
                first = True
                for half in range(max(T // 2048, 1)):
                    w_ = min(2048, T)
                    wait(plan, "scalar", mm, ctx.get(name + "bndmm", 0))
                    sqmark = lns.inc()
                    plan.add("scalar", lambda e, pt=pt, srcl=srcl, half=half, w_=w_: e.activation(
                        pbuf[:, 0:w_], srcl[pt][:, half * w_:(half + 1) * w_],
                        AF.Square).then_inc(lns.h, 1))
                    wait(plan, "tensor", lns, sqmark)
                    if first:
                        wait(plan, "tensor", lnv, ctx.get(name + "bndv", 0))
                    nhalves = max(T // 2048, 1)
                    nch = w_ // 512
                    for hh in range(2):
                        for ch in range(nch):
                            last_ = (half == nhalves - 1 and ch == nch - 1)
                            plan.add("tensor", lambda e, hh=hh, ch=ch, first=first, last_=last_:
                                     e.matmul((ps[2] if hh == 0 else ps[7])[0:1, 0:512],
                                              ctx["ones128"][hh * 64:hh * 64 + 64, 0:1],
                                              pbuf[hh * 64:hh * 64 + 64, ch * 512:(ch + 1) * 512],
                                              start=(first and ch == 0), stop=last_))
                    first = False
                    mk = pe_mark(plan, mm)
                    ctx[name + "bndmm"] = mk
                wait(plan, "vector", mm, mk)
                for hh in range(2):
                    h4 = 2 * pt + hh
                    lnv.inc()
                    plan.add("vector", lambda e, h4=h4, sl=sl, hh=hh: e.reduce_max(
                        nbs[0:1, 2 * h4 + sl:2 * h4 + sl + 1],
                        (ps[2] if hh == 0 else ps[7])[0:1, 0:512], axis=AX.X).then_inc(lnv.h, 1))
                ctx[name + "bndv"] = lnv.v
            p_ = 2 * g + pt
            for hh in range(2):
                bidx = 2 * (2 * pt + hh)
                wait(plan, "vector", lnv, ctx[name + "bndv"])
                lnv.inc()
                plan.add("vector", lambda e, bidx=bidx: e.tensor_mul(
                    nbs[0:1, 8 + bidx:9 + bidx], nbs[0:1, bidx:bidx + 1],
                    nbs[0:1, bidx + 1:bidx + 2]).then_inc(lnv.h, 1))
            bidx0 = 2 * (2 * pt)
            wait(plan, "vector", lnv, lnv.v)
            lnv.inc()
            plan.add("vector", lambda e, bidx0=bidx0: e.tensor_max(
                nbs[0:1, 8 + bidx0:9 + bidx0], nbs[0:1, 8 + bidx0:9 + bidx0],
                nbs[0:1, 10 + bidx0:11 + bidx0]).then_inc(lnv.h, 1))
            wait(plan, "scalar", lnv, lnv.v)
            lns.inc()
            plan.add("scalar", lambda e, bidx0=bidx0: e.activation(
                nbs[0:1, 9 + bidx0:10 + bidx0], nbs[0:1, 8 + bidx0:9 + bidx0],
                AF.Ln).then_inc(lns.h, 1))
            wait(plan, "scalar", lns, lns.v)
            lns.inc()
            plan.add("scalar", lambda e, bidx0=bidx0: e.activation(
                nbs[0:1, 8 + bidx0:9 + bidx0], nbs[0:1, 9 + bidx0:10 + bidx0],
                AF.Exp, scale=0.5).then_inc(lns.h, 1))
            wait(plan, "vector", lns, lns.v)
            wait(plan, "vector", mm, mm.v)
            lnv.inc()
            plan.add("vector", lambda e, bidx0=bidx0: e.tensor_scalar_mul(
                nbs_bf[0:1, 0:1], nbs[0:1, 8 + bidx0:9 + bidx0], -1.0).then_inc(lnv.h, 1))
            wait(plan, "tensor", lnv, lnv.v)
            plan.add("tensor", lambda e: e.matmul(
                ps[7][0:128, 0:1], ctx["ones1"][0:1, :], nbs_bf[0:1, 0:1],
                start=True, stop=True))
            mk = pe_mark(plan, mm)
            wait(plan, "scalar", mm, mk)
            lns.inc()
            plan.add("scalar", lambda e, p_=p_: e.copy(
                cb[:, p_:p_ + 1], ps[7][0:128, 0:1]).then_inc(lns.h, 1))
        cb_done = lns.v

        wait(plan, "tensor", lns, cb_done)
        if g == 0:
            wait(plan, "tensor", pld, pld_all)
            wait(plan, "vector", pld, pld_all)
        for pt in range(2):
            pair = 2 * g + pt
            for qb in range(NQB):
                J = 4 * qb + 4
                expm = {}
                avm = {}
                maskm = {}
                pb = tile12 % 2
                wait(plan, "tensor", lns, ctx.get(name + "exp_last", 0))
                jpos = {}

                def emit_S(j):
                    ji = len(jpos)
                    jpos[j] = ji
                    SS = ctx["psB"][2 + ji % 2]
                    if ji >= 2:
                        wait(plan, "tensor", lns, expm[ji - 2])
                    plan.add("tensor", lambda e, pt=pt, j=j, qb=qb, SS=SS:
                             e.matmul(SS[:, 0:512], kTg[pt][0:64, j * 128:(j + 1) * 128],
                                      qTg[pt][0:64, qb * 512:(qb + 1) * 512],
                                      start=True, stop=True, tile_position=(0, 0)))
                    plan.add("tensor", lambda e, pt=pt, j=j, qb=qb, SS=SS:
                             e.matmul(SS[:, 512:1024], kTg[pt][64:128, j * 128:(j + 1) * 128],
                                      qTg[pt][64:128, qb * 512:(qb + 1) * 512],
                                      start=True, stop=True, tile_position=(64, 0)))
                    mmS[j] = pe_mark(plan, mm)

                def emit_expmask(j):
                    ji = jpos[j]
                    SS = ctx["psB"][2 + ji % 2]
                    wait(plan, "scalar", mm, mmS[j])
                    if ji >= 2:
                        wait(plan, "scalar", mm, avm[ji - 2])
                    p_ = 2 * g + pt
                    lns.inc()
                    plan.add("scalar", lambda e, ji=ji, SS=SS, p_=p_: e.activation(
                        pbuf[:, (ji % 2) * 1024:(ji % 2) * 1024 + 1024],
                        SS[:, 0:1024], AF.Exp, bias=cb[:, p_:p_ + 1]).then_inc(lns.h, 1))
                    expm[ji] = lns.v
                    if j >= 4 * qb:
                        r_off = (j - 4 * qb) * 512
                        wait(plan, "vector", lns, expm[ji])
                        for hh in range(2):
                            lnv.inc()
                            plan.add("vector", lambda e, hh=hh, ji=ji, r_off=r_off: e.tensor_mul(
                                pbuf[:, (ji % 2) * 1024 + hh * 512:(ji % 2) * 1024 + hh * 512 + 512],
                                pbuf[:, (ji % 2) * 1024 + hh * 512:(ji % 2) * 1024 + hh * 512 + 512],
                                msk[:, r_off:r_off + 512]).then_inc(lnv.h, 1))
                        maskm[ji] = lnv.v

                def emit_av(j, first, last):
                    ji = jpos[j]
                    if j >= 4 * qb:
                        wait(plan, "tensor", lnv, maskm[ji])
                    else:
                        wait(plan, "tensor", lns, expm[ji])
                    for hh, attb in ((0, ps[0]), (1, ps[1])):
                        hg = 2 * pt + hh
                        plan.add("tensor", lambda e, hh=hh, j=j, ji=ji, hg=hg, attb=attb,
                                 first=first, last=last:
                                 e.matmul(attb[0:65, 0:512],
                                          vext[:, (j * 4 + hg) * 65:(j * 4 + hg) * 65 + 65],
                                          pbuf[:, (ji % 2) * 1024 + hh * 512:
                                               (ji % 2) * 1024 + hh * 512 + 512],
                                          start=first, stop=last))
                    avm[ji] = pe_mark(plan, mm)

                mmS = {}
                jorder = list(range(4 * qb, J)) + list(range(4 * qb))
                emit_S(jorder[0])
                pf = ctx.pop(name + "pending_fin", None)
                if pf is not None:
                    pf()
                for ji, j in enumerate(jorder):
                    emit_expmask(j)
                    if ji + 1 < J:
                        emit_S(jorder[ji + 1])
                    if ji == 0:
                        wait(plan, "tensor", lnv, ctx.get(name + "finp", 0))
                    emit_av(j, first=(ji == 0), last=(ji == J - 1))
                ctx[name + "exp_last"] = expm[J - 1]
                wait(plan, "vector", mm, avm[J - 1])
                wait(plan, "vector", mm, ctx.get(name + "rrm", 0))
                for hh, attb in ((0, ps[0]), (1, ps[1])):
                    lnv.inc()
                    def _recip(e, hh=hh, attb=attb):
                        with nc.allow_low_precision(reason="softmax 1/l to bf16 matmul operand"):
                            return e.reciprocal(rrow[0:1, hh * 512:hh * 512 + 512],
                                                attb[64:65, 0:512])
                    plan.add("vector", lambda e, f=_recip: f(e).then_inc(lnv.h, 1))
                rcopy_done = lnv.v

                def part2(pair=pair, qb=qb, pb=pb, t12=tile12, rcopy_done=rcopy_done):
                    wait(plan, "tensor", lnv, rcopy_done)
                    wait(plan, "tensor", lns, ctx.get(name + "rbs", 0))
                    for hh in range(2):
                        plan.add("tensor", lambda e, hh=hh: e.matmul(
                            ps[2][hh * 64:hh * 64 + 64, 0:512],
                            ctx["ones1"][0:1, hh * 64:hh * 64 + 64],
                            rrow[0:1, hh * 512:hh * 512 + 512],
                            start=True, stop=True))
                    rbm = pe_mark(plan, mm)
                    ctx[name + "rrm"] = rbm
                    wait(plan, "scalar", mm, rbm)
                    wait(plan, "scalar", lnv, ctx.get(name + "rbv", 0))
                    lns.inc()
                    plan.add("scalar", lambda e: e.copy(rbc[:], ps[2][:, 0:512]).then_inc(lns.h, 1))
                    ctx[name + "rbs"] = lns.v
                    wait(plan, "vector", lns, lns.v)
                    if t12 >= 2:
                        wait(plan, "vector", ast[pb], st_of.get(t12 - 2, 0))
                    for hh, attb in ((0, ps[0]), (1, ps[1])):
                        lnv.inc()
                        plan.add("vector", lambda e, hh=hh, attb=attb, pb=pb: e.tensor_mul(
                            apair[pb][hh * 64:hh * 64 + 64, 0:512],
                            attb[0:64, 0:512],
                            rbc[hh * 64:hh * 64 + 64, 0:512]).then_inc(lnv.h, 1))
                    ctx[name + "rbv"] = lnv.v
                    ctx[name + "finp"] = lnv.v
                    wait(plan, "gpsimd", lnv, lnv.v)
                    sv = ast[pb].inc(16)
                    st_of[t12] = sv
                    plan.add("gpsimd", lambda e, pair=pair, qb=qb, pb=pb: e.dma_start(
                        _ap(ahbm)[pair * 128:(pair + 1) * 128, qb * 512:(qb + 1) * 512],
                        apair[pb][:, 0:512]).then_inc(ast[pb].h, 16))

                ctx[name + "pending_fin"] = part2
                tile12 += 1

    pf = ctx.pop(name + "pending_fin", None)
    if pf is not None:
        pf()
    wait(plan, "sync", ast[0], ast[0].v)
    wait(plan, "sync", ast[1], ast[1].v)
    wait(plan, "sync", mm, projmark[3])
    for c in range(DC):
        atl.inc(16)
        plan.add("sync", lambda e, c=c: e.dma_start(
            arena[c][0:128, 0:T], _ap(ahbm)[c * 128:(c + 1) * 128, 0:T]
        ).then_inc(atl.h, 16))
    atl_all = atl.v

    wait(plan, "tensor", atl, atl_all)
    wait(plan, "tensor", bb, bb_done)
    vv_of, ln_of2, st_of2 = {}, {}, {}
    for nt in range(NT):
        pb = nt % 2
        banks = (ps[4 + 2 * pb], ps[5 + 2 * pb])
        if nt >= 2:
            wait(plan, "sync", lnv, vv_of[nt - 2])
        rv = res2[pb].inc(16)
        plan.add("sync", lambda e, nt=nt, pb=pb: e.dma_start(
            xres[pb][:], _ap(x)[nt * 128:(nt + 1) * 128, :]).then_inc(res2[pb].h, 16))
        if nt >= 2:
            wait(plan, "tensor", lnv, vv_of[nt - 2])
        for c in range(DC):
            for och in range(2):
                plan.add("tensor", lambda e, c=c, nt=nt, och=och, b=banks[och]:
                         e.matmul(b[:, 0:512], arena[c][0:128, nt * 128:(nt + 1) * 128],
                                  wpj[c][0:128, och * 512:och * 512 + 512],
                                  start=(c == 0), stop=(c == DC - 1)))
        mm_here = pe_mark(plan, mm)
        wait(plan, "vector", mm, mm_here)
        wait(plan, "vector", res2[pb], rv)
        if nt == 0:
            wait(plan, "vector", bb, bb_done)
        vpre = lnv.inc(2)
        for och in range(2):
            plan.add("vector", lambda e, b=banks[och], och=och, pb=pb:
                     e.tensor_add(ysb[pb][:, och * 512:och * 512 + 512], b[:, 0:512],
                                  bbc[:, och * 512:och * 512 + 512]).then_inc(lnv.h, 1))
        wait(plan, "vector", lnv, vpre)
        vv = lnv.inc()
        plan.add("vector", lambda e, pb=pb: e.tensor_add(
            ysb[pb][:], ysb[pb][:], xres[pb][:]).then_inc(lnv.h, 1))
        vv_of[nt] = vv
        wait(plan, "scalar", lnv, vv)
        if nt >= 2:
            wait(plan, "vector", stm[pb], st_of2[nt - 2])
        lnfin = emit_ln_epilogue(nc, plan, ctx, ysb[pb], osb[pb])
        ln_of2[nt] = lnfin
        wait(plan, "gpsimd", lnv, lnfin)
        sv = stm[pb].inc(16)
        st_of2[nt] = sv
        plan.add("gpsimd", lambda e, nt=nt, pb=pb: e.dma_start(
            _ap(x1)[nt * 128:(nt + 1) * 128, :], osb[pb][:]).then_inc(stm[pb].h, 16))
    wait(plan, "gpsimd", stm[0], stm[0].v)
    wait(plan, "gpsimd", stm[1], stm[1].v)
    return



def build_stage2():
    nc = bass.Bass("TRN2", target_bir_lowering=False)
    x1 = nc.dram_tensor("x1", [T, D], F32, kind="ExternalInput").ap()
    wrt = nc.dram_tensor("wrt", [D, 8], F32, kind="ExternalInput").ap()
    w1 = nc.dram_tensor("w1", [8 * 4096, D], F32, kind="ExternalInput").ap()
    w2 = nc.dram_tensor("w2", [8 * D, 2048], F32, kind="ExternalInput").ap()
    x2 = nc.dram_tensor("x2", [T, D], F32, kind="ExternalOutput").ap()
    x1bf = nc.dram_tensor("x1bf", [T, D], BF16, kind="Internal").ap()
    w1bf = nc.dram_tensor("w1bf", [8 * 4096, D], BF16, kind="Internal").ap()
    w2bf = nc.dram_tensor("w2bf", [8 * D, 2048], BF16, kind="Internal").ap()

    import contextlib
    with contextlib.ExitStack() as stack:
        plan = Plan()
        ctx = mk_common(nc, stack, plan)
        emit_stage2(nc, stack, plan, ctx, x1, wrt, w1, w2, x1bf, w1bf,
                    w2bf, x2)
        plan.run(nc)
    return nc


def emit_weight_precasts(nc, plan, ctx, w1, w2, w1bf, w2bf, wqkv, wqkvbf,
                         wout, woutbf):
    mksem = ctx["mksem_nobar"]
    caste = [mksem(f"pce{e}") for e in range(8)]
    c3 = mksem("pc3")
    for e in range(8):
        caste[e].inc(32)
    c3.inc(64)
    c3_final = c3.v

    def chunk(es, with_c3=False):
        def emit():
            for e in es:
                plan.add("gpsimd", lambda e_, e=e: e_.dma_start(
                    _ap(w1bf)[e * 4096:(e + 1) * 4096, :],
                    _ap(w1)[e * 4096:(e + 1) * 4096, :]).then_inc(caste[e].h, 16))
                plan.add("gpsimd", lambda e_, e=e: e_.dma_start(
                    _ap(w2bf)[e * D:(e + 1) * D, :],
                    _ap(w2)[e * D:(e + 1) * D, :]).then_inc(caste[e].h, 16))
            if with_c3:
                for i in range(3):
                    plan.add("gpsimd", lambda e_, i=i: e_.dma_start(
                        _ap(wqkvbf)[i * D:(i + 1) * D, :],
                        _ap(wqkv)[i * D:(i + 1) * D, :]).then_inc(c3.h, 16))
                plan.add("gpsimd", lambda e_: e_.dma_start(
                    _ap(woutbf), _ap(wout)).then_inc(c3.h, 16))
        return emit

    chunks = [chunk([0, 1]), chunk([2, 3]), chunk([4, 5]), chunk([6, 7], True)]
    return {"caste": caste, "c3": (c3, c3_final), "chunks": chunks}


def emit_stage2(nc, stack, plan, ctx, x1, wrt, w1, w2, x1bf, w1bf, w2bf,
                x2, name="s2", ext_casts=None):
    E = 8
    QT = min(1024, T)
    NQ = T // QT
    NTQ = QT // 128
    NT2 = QT // 256
    sb = lambda n, shp, dt: ctx["salloc"](name + n, shp, dt)
    mksem = ctx["mksem"]
    x1T = [[sb(f"xT{p}_{c}", [128, QT], BF16) for c in range(DC)] for p in range(2)]
    w1tv = [[sb(f"w1v{b}_{c}", [128, 1024], BF16) for c in range(DC)] for b in range(2)]
    w1tg = [[sb(f"w1g{b}_{c}", [128, 1024], BF16) for c in range(DC)] for b in range(2)]
    w2t = [[sb(f"w2t{b}_{f}", [128, 1024], BF16) for f in range(DC)] for b in range(2)]
    wrT = sb("wrT", [128, 64], BF16)
    acta = [[sb(f"act{p}_{f}", [128, 512], BF16) for f in range(DC)] for p in range(2)]
    sg = [sb(f"sg{p}", [128, 512], F32) for p in range(2)]
    rwq = sb("rwq", [128, 8 * NTQ], F32)
    lg = sb("lg", [128, 8 * NTQ], F32)
    rt = sb("rt", [128, 40], F32)
    yac = [sb(f"ya{t}", [128, 1024], F32) for t in range(NTQ)]
    osb = [sb(f"o{p}", [128, 1024], F32) for p in range(2)]
    xres = [sb("xr0", [128, 1024], F32)]

    ps = ctx["ps"]
    mm = ctx["mm_sem"]; lnv = ctx["ln_v"]; lns = ctx["ln_s"]
    cast = mksem(name + "cast")
    caste = [mksem(name + f"ce{e}") for e in range(E)]
    wrld = mksem(name + "wrld")
    xtq = mksem(name + "xtq")
    wld = [mksem(name + f"wld{b}") for b in range(2)]
    res = mksem(name + "res")
    stm = [mksem(name + f"st{p}") for p in range(2)]

    def vop(fn):
        lnv.inc()
        plan.add("vector", lambda e_, fn=fn: fn(e_).then_inc(lnv.h, 1))
        wait(plan, "vector", lnv, lnv.v)

    slab = T // 4
    for i in range(4):
        cast.inc(16)
        plan.add("gpsimd", lambda e_, i=i, slab=slab: e_.dma_start(
            x1bf[i * slab:(i + 1) * slab, :], x1[i * slab:(i + 1) * slab, :]).then_inc(cast.h, 16))
    cast_all = cast.v
    if ext_casts is not None:
        caste = ext_casts["caste"]
    else:
        for e in range(E):
            caste[e].inc(32)
            plan.add("gpsimd", lambda e_, e=e: e_.dma_start(
                w1bf[e * 4096:(e + 1) * 4096, :], w1[e * 4096:(e + 1) * 4096, :]).then_inc(caste[e].h, 16))
            plan.add("gpsimd", lambda e_, e=e: e_.dma_start(
                w2bf[e * D:(e + 1) * D, :], w2[e * D:(e + 1) * D, :]).then_inc(caste[e].h, 16))

    for c in range(DC):
        wrld.inc(16)
        plan.add("gpsimd", lambda e_, c=c: e_.dma_start(
            wrT[:, c * 8:(c + 1) * 8], _ap(wrt)[c * 128:(c + 1) * 128, :]
        ).then_inc(wrld.h, 16))
    wrld_all = wrld.v

    wait(plan, "tensor", ctx["init_sem"], ctx["init_ones"])
    q_hend = {}
    buffree = [0, 0]
    actm_h = {}
    yst_h = {}
    lastY = [0, 0]
    gfs = 0
    gy = 0
    rnd = 0
    st_of = {}
    tile_base = 0

    def emit_x1T_loads(q):
        if q >= NQ:
            return
        qb2 = q % 2
        if q >= 2:
            wait(plan, "sync", mm, q_hend.get(q - 2, 0))
        if q == 0:
            wait(plan, "sync", cast, cast_all)
        xtq.inc(16 * DC)
        for c in range(DC):
            plan.add("sync", lambda e_, q=q, c=c, qb2=qb2: e_.dma_start(
                x1T[qb2][c][0:128, 0:QT],
                x1bf[q * QT:(q + 1) * QT, c * 128:(c + 1) * 128],
                transpose=True).then_inc(xtq.h, 16))
        return xtq.v

    def emit_router(q, xv):
        qb2 = q % 2
        wait(plan, "tensor", xtq, xv)
        if q == 0:
            wait(plan, "tensor", wrld, wrld_all)
        wait(plan, "tensor", lnv, actm_h.get(gfs - 1, 0))
        wait(plan, "scalar", lnv, ctx.get(name + "rtv", 0))
        for ntq in range(NTQ):
            for c in range(DC):
                plan.add("tensor", lambda e_, ntq=ntq, c=c, qb2=qb2: e_.matmul(
                    ps[0][:, ntq * 8:(ntq + 1) * 8],
                    x1T[qb2][c][0:128, ntq * 128:(ntq + 1) * 128],
                    wrT[:, c * 8:(c + 1) * 8],
                    start=(ntq == 0 and c == 0), stop=(ntq == NTQ - 1 and c == DC - 1)))
        mk = pe_mark(plan, mm)
        wait(plan, "scalar", mm, mk)
        lns.inc()
        plan.add("scalar", lambda e_: e_.copy(lg[:], ps[0][:, 0:8 * NTQ]).then_inc(lns.h, 1))
        lg_done = lns.v
        ctx[name + "lgd"] = lg_done
        wait(plan, "vector", lns, lg_done)
        for ntq in range(NTQ):
            lgs = lg[:, ntq * 8:(ntq + 1) * 8]
            m1 = rt[:, 0:1]; eq1 = rt[:, 1:9]; mskd = rt[:, 9:17]
            m2 = rt[:, 17:18]; eq2 = rt[:, 24:32]; dd = rt[:, 18:19]
            ex = rt[:, 19:20]; tt = rt[:, 20:21]; w1v = rt[:, 21:22]
            w2v = rt[:, 22:23]; t2 = rt[:, 32:40]
            rws = rwq[:, ntq * 8:(ntq + 1) * 8]

            def vop(fn):
                lnv.inc()
                plan.add("vector", lambda e_, fn=fn: fn(e_).then_inc(lnv.h, 1))
                wait(plan, "vector", lnv, lnv.v)
            vop(lambda e_, lgs=lgs: e_.reduce_max(m1, lgs, axis=AX.X))
            vop(lambda e_, lgs=lgs: e_.tensor_single_scalar(eq1, lgs, m1, ALU.is_equal))
            vop(lambda e_, lgs=lgs: e_.scalar_tensor_tensor(mskd, eq1, -1e30, lgs, ALU.mult, ALU.add))
            vop(lambda e_: e_.reduce_max(m2, mskd, axis=AX.X))
            vop(lambda e_: e_.tensor_single_scalar(eq2, mskd, m2, ALU.is_equal))
            vop(lambda e_: e_.tensor_sub(dd, m2, m1))
            wait(plan, "scalar", lnv, lnv.v)
            lns.inc()
            plan.add("scalar", lambda e_, dd=dd, ex=ex: e_.activation(
                ex, dd, AF.Exp).then_inc(lns.h, 1))
            wait(plan, "vector", lns, lns.v)
            vop(lambda e_: e_.tensor_scalar_add(tt, ex, 1.0))
            vop(lambda e_: e_.reciprocal(w1v, tt))
            vop(lambda e_: e_.tensor_scalar(w2v, w1v, -1.0, 1.0, ALU.mult, ALU.add))
            vop(lambda e_: e_.tensor_single_scalar(t2, eq1, w1v, ALU.mult))
            vop(lambda e_, rws=rws: e_.scalar_tensor_tensor(rws, eq2, w2v, t2, ALU.mult, ALU.add))
            ctx[name + "rtv"] = lnv.v
        return lg_done

    def emit_loads(q, e, fh):
        b = (e * 2 + fh) % 2
        wait(plan, "sync", caste[e], caste[e].v)
        wait(plan, "sync", mm, buffree[b])
        wld[b].inc(16 * 3 * DC)
        for c in range(DC):
            plan.add("sync", lambda e_, e=e, fh=fh, c=c, b=b: e_.dma_start(
                w1tv[b][c][:], w1bf[e * 4096 + fh * 1024:e * 4096 + (fh + 1) * 1024,
                                   c * 128:(c + 1) * 128],
                transpose=True).then_inc(wld[b].h, 16))
            plan.add("sync", lambda e_, e=e, fh=fh, c=c, b=b: e_.dma_start(
                w1tg[b][c][:], w1bf[e * 4096 + 2048 + fh * 1024:
                                    e * 4096 + 2048 + (fh + 1) * 1024,
                                    c * 128:(c + 1) * 128],
                transpose=True).then_inc(wld[b].h, 16))
            plan.add("sync", lambda e_, e=e, fh=fh, c=c, b=b: e_.dma_start(
                w2t[b][c][:], w2bf[e * D:(e + 1) * D,
                                   fh * 1024 + c * 128:fh * 1024 + (c + 1) * 128],
                transpose=True).then_inc(wld[b].h, 16))
        return wld[b].v

    def emit_H(q, e, fh, ntw, wv, rw_done, qfirst):
        nonlocal gfs, rnd
        qb2 = q % 2
        b = (e * 2 + fh) % 2
        par = rnd % 2
        if ntw == 0:
            wait(plan, "tensor", wld[b], wv)
        if qfirst and ntw == 0:
            wait(plan, "tensor", lns, rw_done)
        for fs in range(DC):
            pv = ps[0 + 2 * (fs % 2)]
            pg = ps[1 + 2 * (fs % 2)]
            if gfs >= 2:
                wait(plan, "tensor", lnv, actm_h.get(gfs - 2, 0))
            for c in range(DC):
                plan.add("tensor", lambda e_, c=c, fs=fs, ntw=ntw, b=b, pv=pv, qb2=qb2:
                         e_.matmul(pv[:, 0:512],
                                   w1tv[b][c][0:128, fs * 128:(fs + 1) * 128],
                                   x1T[qb2][c][0:128, ntw * 512:(ntw + 1) * 512],
                                   start=(c == 0), stop=(c == DC - 1)))
            for c in range(DC):
                plan.add("tensor", lambda e_, c=c, fs=fs, ntw=ntw, b=b, pg=pg, qb2=qb2:
                         e_.matmul(pg[:, 0:512],
                                   w1tg[b][c][0:128, fs * 128:(fs + 1) * 128],
                                   x1T[qb2][c][0:128, ntw * 512:(ntw + 1) * 512],
                                   start=(c == 0), stop=(c == DC - 1)))
            hmk = pe_mark(plan, mm)
            wait(plan, "scalar", mm, hmk)
            if gfs >= 2:
                wait(plan, "scalar", lnv, actm_h.get(gfs - 2, 0))
            lns.inc()
            plan.add("scalar", lambda e_, fs=fs, pg=pg: e_.activation(
                sg[fs % 2][:], pg[:, 0:512], AF.Silu).then_inc(lns.h, 1))
            smk = lns.v
            wait(plan, "vector", lns, smk)
            if lastY[par]:
                wait(plan, "vector", mm, lastY[par])
            lnv.inc()
            plan.add("vector", lambda e_, fs=fs, pv=pv, par=par: e_.tensor_mul(
                acta[par][fs][:], pv[:, 0:512], sg[fs % 2][:]).then_inc(lnv.h, 1))
            actm_h[gfs] = lnv.v
            gfs += 1
        rnd += 1
        return par, gfs - 1

    def make_Y(q, e, fh, ntw, par, gfs_last, qfirst):
        b = (e * 2 + fh) % 2

        def emit_Y():
            nonlocal gy
            wait(plan, "tensor", lnv, actm_h[gfs_last])
            first = qfirst
            for nsub in range(4):
                tile = ntw * 4 + nsub
                pairb = (ps[4], ps[5]) if nsub % 2 == 0 else (ps[6], ps[7])
                if gy >= 2:
                    wait(plan, "tensor", lnv, yst_h.get(gy - 2, 0))
                for fs in range(DC):
                    for cc in range(2):
                        plan.add("tensor", lambda e_, fs=fs, nsub=nsub, cc=cc, b=b, par=par, yb=pairb[cc]:
                                 e_.matmul(yb[:, 0:512],
                                           acta[par][fs][0:128, nsub * 128:(nsub + 1) * 128],
                                           w2t[b][fs][0:128, cc * 512:(cc + 1) * 512],
                                           start=(fs == 0), stop=(fs == DC - 1)))
                ymk = pe_mark(plan, mm)
                wait(plan, "vector", mm, ymk)
                for cc in range(2):
                    lnv.inc()
                    if first:
                        plan.add("vector", lambda e_, tile=tile, cc=cc, e=e, yb=pairb[cc]:
                                 e_.tensor_single_scalar(
                                     yac[tile][:, cc * 512:(cc + 1) * 512],
                                     yb[:, 0:512],
                                     rwq[:, tile * 8 + e:tile * 8 + e + 1],
                                     ALU.mult).then_inc(lnv.h, 1))
                    else:
                        plan.add("vector", lambda e_, tile=tile, cc=cc, e=e, yb=pairb[cc]:
                                 e_.scalar_tensor_tensor(
                                     yac[tile][:, cc * 512:(cc + 1) * 512],
                                     yb[:, 0:512],
                                     rwq[:, tile * 8 + e:tile * 8 + e + 1],
                                     yac[tile][:, cc * 512:(cc + 1) * 512],
                                     ALU.mult, ALU.add).then_inc(lnv.h, 1))
                yst_h[gy] = lnv.v
                gy += 1
            lastY[par] = mm.v
            buffree[b] = mm.v
        return emit_Y

    def make_finalize(q):
        def one_tile(ntq):
            def emit_fin_tile():
                t = q * NTQ + ntq
                pb = t % 2
                rv = res.inc(16)
                plan.add("gpsimd", lambda e_, t=t: e_.dma_start(
                    xres[0][:], x1[t * 128:(t + 1) * 128, :]).then_inc(res.h, 16))
                wait(plan, "vector", res, rv)
                lnv.inc()
                plan.add("vector", lambda e_, ntq=ntq: e_.tensor_add(
                    yac[ntq][:], yac[ntq][:], xres[0][:]).then_inc(lnv.h, 1))
                vv = lnv.v
                wait(plan, "scalar", lnv, vv)
                if t >= 2:
                    wait(plan, "vector", stm[pb], st_of[t - 2])
                lnfin = emit_ln_epilogue(nc, plan, ctx, yac[ntq], osb[pb])
                wait(plan, "gpsimd", lnv, lnfin)
                sv = stm[pb].inc(16)
                st_of[t] = sv
                plan.add("gpsimd", lambda e_, t=t, pb=pb: e_.dma_start(
                    x2[t * 128:(t + 1) * 128, :], osb[pb][:]).then_inc(stm[pb].h, 16))
            return emit_fin_tile
        return [one_tile(ntq) for ntq in range(NTQ)]

    xv = emit_x1T_loads(0)
    xv_next = None
    pending_fin = []
    for q in range(NQ):
        xv_next = emit_x1T_loads(q + 1)
        rw_done = emit_router(q, xv)
        xv = xv_next
        pending_Y = None
        ef = [(e, fh) for e in range(E) for fh in range(2)]
        if q % 2 == 1:
            ef = ef[::-1]
        for efi, (e, fh) in enumerate(ef):
            b = (e * 2 + fh) % 2
            if ctx.get(name + f"buf{b}") != (e, fh):
                emit_loads(q, e, fh)
                ctx[name + f"buf{b}"] = (e, fh)
            wv = wld[b].v
            for ntw in range(QT // 512):
                for _ in range(4):
                    if pending_fin:
                        pending_fin.pop(0)()
                par, gl = emit_H(q, e, fh, ntw, wv, rw_done, efi == 0)
                if pending_Y is not None:
                    pending_Y()
                pending_Y = make_Y(q, e, fh, ntw, par, gl, efi == 0)
        pending_Y()
        q_hend[q] = mm.v
        pending_fin = make_finalize(q)
    for f in pending_fin:
        f()
    wait(plan, "gpsimd", stm[0], stm[0].v)
    wait(plan, "gpsimd", stm[1], stm[1].v)
    return


def _run_stage2(x1, Wr, W1, W2):
    nc = build_stage2()
    in_maps = [dict(x1=np.ascontiguousarray(x1[b]), wrt=np.ascontiguousarray(Wr.T),
                    w1=np.ascontiguousarray(W1.reshape(8 * 4096, D)),
                    w2=np.ascontiguousarray(W2.reshape(8 * D, 2048)))
               for b in range(x1.shape[0])]
    res = run_bass_kernel_spmd(nc, in_maps, core_ids=list(range(x1.shape[0])))
    return np.stack([r["x2"] for r in res.results], 0)


def build_stage3():
    nc = bass.Bass("TRN2", target_bir_lowering=False)
    x2 = nc.dram_tensor("x2", [T, D], F32, kind="ExternalInput").ap()
    wqkv = nc.dram_tensor("wqkv", [3 * D, D], F32, kind="ExternalInput").ap()
    wout = nc.dram_tensor("wout", [D, D], F32, kind="ExternalInput").ap()
    bout = nc.dram_tensor("bout", [D], F32, kind="ExternalInput").ap()
    x3 = nc.dram_tensor("x3", [T, D], F32, kind="ExternalOutput").ap()
    x2bf = nc.dram_tensor("x2bf", [T, D], BF16, kind="Internal").ap()
    wqkvbf = nc.dram_tensor("wqkvbf", [3 * D, D], BF16, kind="Internal").ap()
    woutbf = nc.dram_tensor("woutbf", [D, D], BF16, kind="Internal").ap()

    import contextlib
    with contextlib.ExitStack() as stack:
        plan = Plan()
        ctx = mk_common(nc, stack, plan)
        emit_stage3(nc, stack, plan, ctx, x2, x2bf, wqkv, wqkvbf, wout, woutbf,
                    bout, x3)
        plan.run(nc)
    return nc


def emit_stage3(nc, stack, plan, ctx, x2, x2bf, wqkv, wqkvbf, wout, woutbf,
                bout, x3, name="s3", ext_casts=None):
    EPS = 1e-6
    sb = lambda n, shp, dt: ctx["salloc"](name + n, shp, dt)
    mksem = ctx["mksem"]
    NS = T // 512
    kvw = [sb(f"kvw{c}", [128, 2048], BF16) for c in range(DC)]
    woT = [sb(f"wo{c}", [128, 1024], BF16) for c in range(DC)]
    kvacc = [sb(f"kva{c}", [128, 1024], F32) for c in range(DC)]
    kvbf = [sb(f"kvb{c}", [128, 1024], BF16) for c in range(DC)]
    x2Ta = [sb(f"x2a{p}", [128, 1024], BF16) for p in range(2)]
    x2Tb = [sb(f"x2b{p}", [128, 4096], BF16) for p in range(2)]
    kbf = [sb(f"kbf{i}", [128, 1024], BF16) for i in range(4)]
    vbf = [sb(f"vbf{i}", [128, 1024], BF16) for i in range(4)]
    qT = [[sb(f"qT{p}_{c}", [128, 512], BF16) for c in range(DC)] for p in range(2)]
    aT = [sb(f"aT{c}", [128, 512], BF16) for c in range(DC)]
    ksumT = sb("ksumT", [128, 8], BF16)
    zsb = sb("z", [128, 8], F32)
    bbc = sb("bbc", [128, 1024], F32)
    ysb = [sb(f"y{p}", [128, 1024], F32) for p in range(2)]
    osb = [sb(f"o{p}", [128, 1024], F32) for p in range(2)]
    xres = [sb(f"xr{p}", [128, 1024], F32) for p in range(2)]

    ps = ctx["ps"]
    mm = ctx["mm_sem"]; lnv = ctx["ln_v"]; lns = ctx["ln_s"]
    cast = mksem(name + "cast")
    wld = mksem(name + "wld")
    qld = mksem(name + "qld")
    xta = [mksem(name + f"xta{p}") for p in range(2)]
    xtb = [mksem(name + f"xtb{p}") for p in range(2)]
    res = mksem(name + "res")
    stm = [mksem(name + f"st{p}") for p in range(2)]
    bb = Sem(nc, stack, name + "bb")

    slab = T // 4
    for i in range(4):
        cast.inc(16)
        plan.add("gpsimd", lambda e, i=i, slab=slab: e.dma_start(
            x2bf[i * slab:(i + 1) * slab, :], x2[i * slab:(i + 1) * slab, :]).then_inc(cast.h, 16))
    if ext_casts is None:
        for i in range(3):
            cast.inc(16)
            plan.add("gpsimd", lambda e, i=i: e.dma_start(
                wqkvbf[i * D:(i + 1) * D, :], wqkv[i * D:(i + 1) * D, :]).then_inc(cast.h, 16))
        cast.inc(16)
        plan.add("gpsimd", lambda e: e.dma_start(woutbf, wout).then_inc(cast.h, 16))
        wsem, wval = cast, cast.v
    else:
        wsem, wval = ext_casts["c3"]
    cast_all = cast.v

    wait(plan, "sync", wsem, wval)
    for c in range(DC):
        wld.inc(32)
        plan.add("sync", lambda e, c=c: e.dma_start(
            kvw[c][0:128, 0:2048], wqkvbf[D:3 * D, c * 128:(c + 1) * 128],
            transpose=True).then_inc(wld.h, 16))
        plan.add("sync", lambda e, c=c: e.dma_start(
            woT[c][:], woutbf[:, c * 128:(c + 1) * 128], transpose=True).then_inc(wld.h, 16))
    wld_all = wld.v

    emit_bias_broadcast(nc, plan, ctx, bout, bbc, D, bb)
    bb_done = bb.v

    for c in range(DC):
        lnv.inc()
        plan.add("vector", lambda e, c=c: e.memset(kvacc[c][:], 0.0).then_inc(lnv.h, 1))
    kvz_done = lnv.v

    wait(plan, "tensor", wld, wld_all)
    wait(plan, "tensor", ctx["init_sem"], ctx["init_ones"])
    xta_free = {}
    evac_of = {}
    eps_of = {}
    kvmm_of = {}
    kvadd_of = {}
    projmm_of = {}
    NG = NT // 4
    for g in range(NG):
        for i in range(4):
            nt = 4 * g + i
            pb = nt % 2
            if nt >= 2:
                wait(plan, "sync", mm, xta_free[nt - 2])
            if nt == 0:
                wait(plan, "sync", cast, cast_all)
            xv = xta[pb].inc(16 * DC)
            for c in range(DC):
                plan.add("sync", lambda e, nt=nt, c=c, pb=pb: e.dma_start(
                    x2Ta[pb][0:128, c * 128:(c + 1) * 128],
                    x2bf[nt * 128:(nt + 1) * 128, c * 128:(c + 1) * 128],
                    transpose=True).then_inc(xta[pb].h, 16))
            wait(plan, "tensor", xta[pb], xv)
            if nt >= 1:
                wait(plan, "tensor", lns, evac_of[nt - 1])
            if g >= 1 and i == 0:
                wait(plan, "tensor", lnv, kvadd_of[g - 1])
            for c in range(DC):
                st_ = (c == 0)
                sp_ = (c == DC - 1)
                for tgt, col in ((ps[0], 0), (ps[1], 512), (ps[2], 1024), (ps[3], 1536)):
                    plan.add("tensor", lambda e, c=c, st_=st_, sp_=sp_, pb=pb, tgt=tgt, col=col:
                             e.matmul(tgt[:, 0:512],
                                      x2Ta[pb][0:128, c * 128:(c + 1) * 128],
                                      kvw[c][0:128, col:col + 512],
                                      start=st_, stop=sp_))
            mmk = pe_mark(plan, mm)
            xta_free[nt] = mmk
            projmm_of[nt] = mmk
            wait(plan, "scalar", mm, mmk)
            if g >= 1:
                wait(plan, "scalar", mm, kvmm_of[g - 1])
            for cc in range(2):
                lns.inc()
                plan.add("scalar", lambda e, cc=cc, i=i: e.activation(
                    kbf[i][:, cc * 512:cc * 512 + 512], ps[0 + cc][:, 0:512],
                    AF.Relu).then_inc(lns.h, 1))
                lns.inc()
                plan.add("scalar", lambda e, cc=cc, i=i: e.copy(
                    vbf[i][:, cc * 512:cc * 512 + 512], ps[2 + cc][:, 0:512]).then_inc(lns.h, 1))
            evac_of[nt] = lns.v
            wait(plan, "vector", lns, evac_of[nt])
            lnv.inc()
            plan.add("vector", lambda e, i=i: e.tensor_scalar_add(
                kbf[i][:], kbf[i][:], EPS).then_inc(lnv.h, 1))
            eps_of[nt] = lnv.v
        wait(plan, "tensor", lnv, eps_of[4 * g + 3])
        if g == 0:
            wait(plan, "tensor", lnv, kvz_done)
        kvadd_prev = kvadd_of.get(g - 1, 0)
        add_targets = []
        for dc_ in range(DC):
            pair = (ps[4], ps[5]) if dc_ % 2 == 0 else (ps[2], ps[3])
            if dc_ >= 2:
                wait(plan, "tensor", lnv, add_targets[dc_ - 2])
            for i in range(4):
                nt = 4 * g + i
                st_ = (i == 0)
                sp_ = (i == 3)
                for eh in range(2):
                    plan.add("tensor", lambda e, dc_=dc_, i=i, eh=eh, st_=st_, sp_=sp_, pair=pair:
                             e.matmul(pair[eh][:, 0:512],
                                      kbf[i][0:128, dc_ * 128:(dc_ + 1) * 128],
                                      vbf[i][0:128, eh * 512:eh * 512 + 512],
                                      start=st_, stop=sp_))
                plan.add("tensor", lambda e, dc_=dc_, i=i, g=g, st_=st_, sp_=sp_:
                         e.matmul(ps[7][0:128, dc_:dc_ + 1],
                                  kbf[i][0:128, dc_ * 128:(dc_ + 1) * 128],
                                  ctx["ones128"][0:128, 0:1],
                                  start=(g == 0 and i == 0), stop=(g == NG - 1 and i == 3)))
            mmv = pe_mark(plan, mm)
            wait(plan, "vector", mm, mmv)
            for eh in range(2):
                lnv.inc()
                plan.add("vector", lambda e, dc_=dc_, eh=eh, pair=pair: e.tensor_add(
                    kvacc[dc_][:, eh * 512:eh * 512 + 512],
                    kvacc[dc_][:, eh * 512:eh * 512 + 512],
                    pair[eh][:, 0:512]).then_inc(lnv.h, 1))
            add_targets.append(lnv.v)
        kvmm_of[g] = mm.v
        kvadd_of[g] = lnv.v

    wait(plan, "scalar", lnv, kvadd_of[NG - 1])
    wait(plan, "scalar", mm, mm.v)
    for c in range(DC):
        lns.inc()
        plan.add("scalar", lambda e, c=c: e.copy(kvbf[c][:], kvacc[c][:]).then_inc(lns.h, 1))
    lns.inc()
    plan.add("scalar", lambda e: e.copy(ksumT[:], ps[7][0:128, 0:8]).then_inc(lns.h, 1))
    kvbf_done = lns.v

    wait(plan, "sync", mm, kvmm_of[NG - 1])
    for c in range(DC):
        qld.inc(16)
        plan.add("sync", lambda e, c=c: e.dma_start(
            kvw[c][0:128, 0:1024], wqkvbf[0:D, c * 128:(c + 1) * 128],
            transpose=True).then_inc(qld.h, 16))
    qld_all = qld.v

    wait(plan, "tensor", qld, qld_all)
    wait(plan, "tensor", lns, kvbf_done)
    qproj_free = {}
    ln_of = {}
    st_of = {}
    tile_idx = 0
    z_done_of = {}
    qready_of = {}
    bprojmm = {0: 0}

    def emit_A(ns):
        if ns >= NS:
            return
        pb = ns % 2
        if ns >= 2:
            wait(plan, "sync", mm, qproj_free[ns - 2])
        xv = xtb[pb].inc(16 * DC)
        for c in range(DC):
            plan.add("sync", lambda e, ns=ns, c=c, pb=pb: e.dma_start(
                x2Tb[pb][0:128, c * 512:(c + 1) * 512],
                x2bf[ns * 512:(ns + 1) * 512, c * 128:(c + 1) * 128],
                transpose=True).then_inc(xtb[pb].h, 16))
        wait(plan, "tensor", xtb[pb], xv)
        wait(plan, "tensor", lns, ctx.get(name + "qrelu", 0))
        qevac = {}
        for dq in range(DC):
            bank = ps[dq % 2]
            if dq >= 2:
                wait(plan, "tensor", lnv, qevac[dq - 2])
            for c in range(DC):
                plan.add("tensor", lambda e, dq=dq, c=c, pb=pb, bank=bank:
                         e.matmul(bank[:, 0:512],
                                  kvw[c][0:128, dq * 128:(dq + 1) * 128],
                                  x2Tb[pb][0:128, c * 512:(c + 1) * 512],
                                  start=(c == 0), stop=(c == DC - 1)))
            mmq = pe_mark(plan, mm)
            wait(plan, "scalar", mm, mmq)
            wait(plan, "scalar", mm, bprojmm.get(ns - 1, 0))
            lns.inc()
            plan.add("scalar", lambda e, dq=dq, pb=pb, bank=bank: e.activation(
                qT[pb][dq][:], bank[:, 0:512], AF.Relu).then_inc(lns.h, 1))
            wait(plan, "vector", lns, lns.v)
            lnv.inc()
            plan.add("vector", lambda e, dq=dq, pb=pb: e.tensor_scalar_add(
                qT[pb][dq][:], qT[pb][dq][:], EPS).then_inc(lnv.h, 1))
            qevac[dq] = lnv.v
        ctx[name + "qrelu"] = lns.v
        qproj_free[ns] = mm.v
        qready_of[ns] = lnv.v

    def emit_B(ns):
        nonlocal tile_idx
        pb = ns % 2
        wait(plan, "tensor", lnv, qready_of[ns])
        wait(plan, "tensor", lnv, ln_of.get(tile_idx - 1, 0))
        for nsub in range(4):
            for c in range(DC):
                plan.add("tensor", lambda e, nsub=nsub, c=c, pb=pb:
                         e.matmul(ps[2][0:128, nsub:nsub + 1],
                                  qT[pb][c][0:128, nsub * 128:(nsub + 1) * 128],
                                  ksumT[0:128, c:c + 1],
                                  start=(c == 0), stop=(c == DC - 1)))
        mmz = pe_mark(plan, mm)
        wait(plan, "vector", mm, mmz)
        wait(plan, "vector", lnv, lnv.v)
        lnv.inc()
        plan.add("vector", lambda e: e.tensor_scalar_add(
            zsb[:, 0:4], ps[2][0:128, 0:4], EPS).then_inc(lnv.h, 1))
        wait(plan, "vector", lnv, lnv.v)
        lnv.inc()
        plan.add("vector", lambda e: e.reciprocal(
            zsb[:, 4:8], zsb[:, 0:4]).then_inc(lnv.h, 1))
        z_done = lnv.v
        z_done_of[ns] = z_done
        aevac = {}
        for et in range(DC):
            bank = ps[3 + et % 2]
            if et >= 2:
                wait(plan, "tensor", lns, aevac[et - 2])
            for c in range(DC):
                plan.add("tensor", lambda e, et=et, c=c, pb=pb, bank=bank:
                         e.matmul(bank[:, 0:512],
                                  kvbf[c][0:128, et * 128:(et + 1) * 128],
                                  qT[pb][c][0:128, 0:512],
                                  start=(c == 0), stop=(c == DC - 1)))
            mma = pe_mark(plan, mm)
            wait(plan, "scalar", mm, mma)
            wait(plan, "scalar", mm, bprojmm.get(ns, 0))
            lns.inc()
            plan.add("scalar", lambda e, et=et, bank=bank: e.copy(
                aT[et][:], bank[:, 0:512]).then_inc(lns.h, 1))
            aevac[et] = lns.v
        a_ready = lns.v
        wait(plan, "tensor", lns, a_ready)
        for nsub in range(4):
            pb2 = tile_idx % 2
            pbanks = (ps[5], ps[6]) if nsub % 2 == 0 else (ps[7], ps[2])
            if nsub % 2 == 1:
                wait(plan, "tensor", lnv, z_done)
            if tile_idx >= 2:
                wait(plan, "tensor", lnv, ln_of[tile_idx - 2])
            for et in range(DC):
                st_ = (et == 0)
                sp_ = (et == DC - 1)
                for cc in range(2):
                    plan.add("tensor", lambda e, nsub=nsub, cc=cc, et=et, st_=st_, sp_=sp_, pbanks=pbanks:
                             e.matmul(pbanks[cc][:, 0:512],
                                      aT[et][0:128, nsub * 128:(nsub + 1) * 128],
                                      woT[et][0:128, cc * 512:cc * 512 + 512],
                                      start=st_, stop=sp_))
            mmp = pe_mark(plan, mm)
            rv = res.inc(16)
            plan.add("gpsimd", lambda e, ns=ns, nsub=nsub, pb2=pb2: e.dma_start(
                xres[pb2][:], x2[ns * 512 + nsub * 128:ns * 512 + (nsub + 1) * 128, :]
            ).then_inc(res.h, 16))
            wait(plan, "vector", mm, mmp)
            wait(plan, "vector", bb, bb_done)
            wait(plan, "vector", lnv, z_done)
            wait(plan, "vector", res, rv)
            for cc in range(2):
                lnv.inc()
                plan.add("vector", lambda e, cc=cc, nsub=nsub, pb2=pb2, pbanks=pbanks:
                         e.scalar_tensor_tensor(
                             ysb[pb2][:, cc * 512:cc * 512 + 512], pbanks[cc][:, 0:512],
                             zsb[:, 4 + nsub:5 + nsub], bbc[:, cc * 512:cc * 512 + 512],
                             ALU.mult, ALU.add).then_inc(lnv.h, 1))
            wait(plan, "vector", lnv, lnv.v)
            vv = lnv.inc()
            plan.add("vector", lambda e, pb2=pb2: e.tensor_add(
                ysb[pb2][:], ysb[pb2][:], xres[pb2][:]).then_inc(lnv.h, 1))
            wait(plan, "scalar", lnv, vv)
            if tile_idx >= 2:
                wait(plan, "vector", stm[pb2], st_of[tile_idx - 2])
            lnfin = emit_ln_epilogue(nc, plan, ctx, ysb[pb2], osb[pb2])
            ln_of[tile_idx] = lnfin
            wait(plan, "gpsimd", lnv, lnfin)
            sv = stm[pb2].inc(16)
            st_of[tile_idx] = sv
            plan.add("gpsimd", lambda e, ns=ns, nsub=nsub, pb2=pb2: e.dma_start(
                x3[ns * 512 + nsub * 128:ns * 512 + (nsub + 1) * 128, :],
                osb[pb2][:]).then_inc(stm[pb2].h, 16))
            tile_idx += 1
        bprojmm[ns + 1] = mm.v

    emit_A(0)
    for ns in range(NS):
        emit_A(ns + 1)
        emit_B(ns)
    wait(plan, "gpsimd", stm[0], stm[0].v)
    wait(plan, "gpsimd", stm[1], stm[1].v)
    return


    wait(plan, "gpsimd", stm[1], stm[1].v)
    return


def _dft_consts():
    j = np.arange(D)
    ang = 2.0 * np.pi * np.outer(j, j) / D
    fr = np.cos(ang).astype(np.float32)
    fi = (-np.sin(ang)).astype(np.float32)
    gr = (np.cos(ang) / D).astype(np.float32)
    gi = (np.sin(ang) / D).astype(np.float32)
    return fr, fi, gr, gi


def _run_stage4(x3, wfr, wfi, bias):
    fr, fi, gr, gi = _dft_consts()
    nc = build_stage4()
    in_maps = []
    for b in range(x3.shape[0]):
        in_maps.append(dict(x3=np.ascontiguousarray(x3[b]), wfr=wfr, wfi=wfi,
                            bias=bias, fr=fr, fi=fi, fineg=-fi, gr=gr, gineg=-gi))
    res = run_bass_kernel_spmd(nc, in_maps, core_ids=list(range(x3.shape[0])))
    return np.stack([r["out"] for r in res.results], 0)


def _mk_masks():
    import ml_dtypes
    m = np.zeros((128, 2048), dtype=np.float32)
    for ri, r in enumerate((0, 128, 256, 384)):
        k = np.arange(128)[:, None]
        q = np.arange(512)[None, :]
        m[:, ri * 512:(ri + 1) * 512] = ((k + r) <= q).astype(np.float32)
    return m.astype(ml_dtypes.bfloat16)


def _run_stage1(x, wqkv, wproj, bproj):
    nc = build_stage1()
    masks = _mk_masks()
    in_maps = [dict(x=np.ascontiguousarray(x[b]), wqkv=wqkv, wproj=wproj,
                    bproj=bproj, masks=masks) for b in range(x.shape[0])]
    res = run_bass_kernel_spmd(nc, in_maps, core_ids=list(range(x.shape[0])))
    return np.stack([r["x1"] for r in res.results], 0)


def _run_stage3(x2, wqkv, wout, bout):
    nc = build_stage3()
    in_maps = [dict(x2=np.ascontiguousarray(x2[b]), wqkv=wqkv, wout=wout, bout=bout)
               for b in range(x2.shape[0])]
    res = run_bass_kernel_spmd(nc, in_maps, core_ids=list(range(x2.shape[0])))
    return np.stack([r["x3"] for r in res.results], 0)


POOL_BYTES = 201728


def build_fused():
    nc = bass.Bass("TRN2", target_bir_lowering=False)
    x = nc.dram_tensor("x", [T, D], F32, kind="ExternalInput").ap()
    wqkv0 = nc.dram_tensor("wqkv0", [D, 3 * D], F32, kind="ExternalInput").ap()
    wproj0 = nc.dram_tensor("wproj0", [D, D], F32, kind="ExternalInput").ap()
    bproj0 = nc.dram_tensor("bproj0", [D], F32, kind="ExternalInput").ap()
    masks = nc.dram_tensor("masks", [128, 2048], BF16, kind="ExternalInput").ap()
    wrt = nc.dram_tensor("wrt", [D, 8], F32, kind="ExternalInput").ap()
    w1 = nc.dram_tensor("w1", [8 * 4096, D], F32, kind="ExternalInput").ap()
    w2 = nc.dram_tensor("w2", [8 * D, 2048], F32, kind="ExternalInput").ap()
    wqkv1 = nc.dram_tensor("wqkv1", [3 * D, D], F32, kind="ExternalInput").ap()
    wout1 = nc.dram_tensor("wout1", [D, D], F32, kind="ExternalInput").ap()
    bout1 = nc.dram_tensor("bout1", [D], F32, kind="ExternalInput").ap()
    wfr = nc.dram_tensor("wfr", [D, D], F32, kind="ExternalInput").ap()
    wfi = nc.dram_tensor("wfi", [D, D], F32, kind="ExternalInput").ap()
    fbias = nc.dram_tensor("fbias", [D], F32, kind="ExternalInput").ap()
    fr = nc.dram_tensor("fr", [D, D], F32, kind="ExternalInput").ap()
    fi = nc.dram_tensor("fi", [D, D], F32, kind="ExternalInput").ap()
    fineg = nc.dram_tensor("fineg", [D, D], F32, kind="ExternalInput").ap()
    gr = nc.dram_tensor("gr", [D, D], F32, kind="ExternalInput").ap()
    gineg = nc.dram_tensor("gineg", [D, D], F32, kind="ExternalInput").ap()
    out = nc.dram_tensor("out", [T, D], F32, kind="ExternalOutput").ap()
    x1 = nc.dram_tensor("x1", [T, D], F32, kind="Internal").ap()
    x2 = nc.dram_tensor("x2", [T, D], F32, kind="Internal").ap()
    x3 = nc.dram_tensor("x3", [T, D], F32, kind="Internal").ap()
    xbf = nc.dram_tensor("xbf", [T, D], BF16, kind="Internal").ap()
    ahbm = nc.dram_tensor("ahbm", [D, T], BF16, kind="Internal").ap()
    x1bf = nc.dram_tensor("x1bf", [T, D], BF16, kind="Internal").ap()
    w1bf = nc.dram_tensor("w1bf", [8 * 4096, D], BF16, kind="Internal").ap()
    w2bf = nc.dram_tensor("w2bf", [8 * D, 2048], BF16, kind="Internal").ap()
    x2bf = nc.dram_tensor("x2bf", [T, D], BF16, kind="Internal").ap()
    wqkvbf = nc.dram_tensor("wqkvbf", [3 * D, D], BF16, kind="Internal").ap()
    woutbf = nc.dram_tensor("woutbf", [D, D], BF16, kind="Internal").ap()
    x3bf = nc.dram_tensor("x3bf", [T, D], BF16, kind="Internal").ap()

    import contextlib
    with contextlib.ExitStack() as stack:
        pool = stack.enter_context(nc.sbuf_tensor("pool", [128, POOL_BYTES], mybir.dt.uint8))
        plan = Plan()
        ctx = mk_common(nc, stack, plan, pool=pool)
        ext = emit_weight_precasts(nc, plan, ctx, w1, w2, w1bf, w2bf,
                                   wqkv1, wqkvbf, wout1, woutbf)
        ctx["precast_chunks"] = list(ext["chunks"])
        emit_stage1(nc, stack, plan, ctx, x, xbf, wqkv0, wproj0, bproj0,
                    masks, ahbm, x1)
        assert not ctx["precast_chunks"]
        emit_barrier(nc, plan, ctx)
        ctx["pool_reset"]()
        emit_stage2(nc, stack, plan, ctx, x1, wrt, w1, w2, x1bf, w1bf, w2bf,
                    x2, ext_casts=ext)
        emit_barrier(nc, plan, ctx)
        ctx["pool_reset"]()
        emit_stage3(nc, stack, plan, ctx, x2, x2bf, wqkv1, wqkvbf, wout1,
                    woutbf, bout1, x3, ext_casts=ext)
        emit_barrier(nc, plan, ctx)
        ctx["pool_reset"]()
        emit_stage4(nc, stack, plan, ctx, x3, x3bf, wfr, wfi, fbias,
                    fr, fi, fineg, gr, gineg, out)
        plan.run(nc)
    return nc


def kernel(L0_attn_x, L0_attn_w_qkv, L0_attn_w_proj, L0_attn_b_proj,
           L0_ffn_Wr, L0_ffn_W1, L0_ffn_W2,
           L1_attn_W_qkv, L1_attn_W_out, L1_attn_b_out,
           L1_ffn_w_freq_real, L1_ffn_w_freq_imag, L1_ffn_bias):
    B = L0_attn_x.shape[0]
    cores = list(range(B))
    x = np.ascontiguousarray(L0_attn_x, dtype=np.float32)
    masks = _mk_masks()
    fr, fi, gr, gi = _dft_consts()
    w1r = np.ascontiguousarray(L0_ffn_W1.reshape(8 * 4096, D))
    w2r = np.ascontiguousarray(L0_ffn_W2.reshape(8 * D, 2048))
    wrt = np.ascontiguousarray(L0_ffn_Wr.T)

    nc = build_fused()
    im = [dict(x=np.ascontiguousarray(x[b]), wqkv0=L0_attn_w_qkv,
               wproj0=L0_attn_w_proj, bproj0=L0_attn_b_proj, masks=masks,
               wrt=wrt, w1=w1r, w2=w2r,
               wqkv1=L1_attn_W_qkv, wout1=L1_attn_W_out, bout1=L1_attn_b_out,
               wfr=L1_ffn_w_freq_real, wfi=L1_ffn_w_freq_imag,
               fbias=L1_ffn_bias, fr=fr, fi=fi, fineg=-fi, gr=gr, gineg=-gi)
          for b in cores]
    res = run_bass_kernel_spmd(nc, im, core_ids=cores)
    out = np.stack([res.results[b]["out"] for b in cores], 0)
    return out.astype(np.float32)


def kernel_unfused(L0_attn_x, L0_attn_w_qkv, L0_attn_w_proj, L0_attn_b_proj,
           L0_ffn_Wr, L0_ffn_W1, L0_ffn_W2,
           L1_attn_W_qkv, L1_attn_W_out, L1_attn_b_out,
           L1_ffn_w_freq_real, L1_ffn_w_freq_imag, L1_ffn_bias):
    B = L0_attn_x.shape[0]
    cores = list(range(B))
    x = np.ascontiguousarray(L0_attn_x, dtype=np.float32)
    masks = _mk_masks()
    fr, fi, gr, gi = _dft_consts()

    nc1 = build_stage1()
    im = [dict(x=np.ascontiguousarray(x[b]), wqkv=L0_attn_w_qkv,
               wproj=L0_attn_w_proj, bproj=L0_attn_b_proj, masks=masks)
          for b in cores]
    r1 = run_bass_kernel_spmd(nc1, im, core_ids=cores)
    x1 = [r1.results[b]["x1"] for b in cores]

    nc2 = build_stage2()
    w1r = np.ascontiguousarray(L0_ffn_W1.reshape(8 * 4096, D))
    w2r = np.ascontiguousarray(L0_ffn_W2.reshape(8 * D, 2048))
    wrt = np.ascontiguousarray(L0_ffn_Wr.T)
    im = [dict(x1=x1[b], wrt=wrt, w1=w1r, w2=w2r) for b in cores]
    r2 = run_bass_kernel_spmd(nc2, im, core_ids=cores)
    x2 = [r2.results[b]["x2"] for b in cores]

    nc3 = build_stage3()
    im = [dict(x2=x2[b], wqkv=L1_attn_W_qkv, wout=L1_attn_W_out,
               bout=L1_attn_b_out) for b in cores]
    r3 = run_bass_kernel_spmd(nc3, im, core_ids=cores)
    x3 = [r3.results[b]["x3"] for b in cores]

    nc4 = build_stage4()
    im = [dict(x3=x3[b], wfr=L1_ffn_w_freq_real, wfi=L1_ffn_w_freq_imag,
               bias=L1_ffn_bias, fr=fr, fi=fi, fineg=-fi, gr=gr, gineg=-gi)
          for b in cores]
    r4 = run_bass_kernel_spmd(nc4, im, core_ids=cores)
    out = np.stack([r4.results[b]["out"] for b in cores], 0)
    return out.astype(np.float32)


# version 20
# speedup vs baseline: 1.0907x; 1.0907x over previous
import math
import numpy as np
import concourse.bass as bass
import concourse.mybir as mybir
from concourse.bass_utils import run_bass_kernel_spmd

F32 = mybir.dt.float32
BF16 = mybir.dt.bfloat16
AF = mybir.ActivationFunctionType
ALU = mybir.AluOpType
AX = mybir.AxisListType

T = 4096
D = 1024
NT = T // 128
DC = D // 128


class Sem:
    def __init__(self, nc, stack, name):
        self.h = stack.enter_context(nc.semaphore(name))
        self.v = 0

    def inc(self, d=1):
        self.v += d
        return self.v


class Plan:
    ENGINES = ("tensor", "scalar", "vector", "gpsimd", "sync")

    def __init__(self):
        self.q = {e: [] for e in self.ENGINES}

    def add(self, eng, fn):
        self.q[eng].append(fn)

    def run(self, nc):
        with nc.Block() as block:
            @block.tensor
            def _(e):
                for f in self.q["tensor"]:
                    f(e)

            @block.scalar
            def _(e):
                for f in self.q["scalar"]:
                    f(e)

            @block.vector
            def _(e):
                for f in self.q["vector"]:
                    f(e)

            @block.gpsimd
            def _(e):
                for f in self.q["gpsimd"]:
                    f(e)

            @block.sync
            def _(e):
                for f in self.q["sync"]:
                    f(e)


def wait(plan, eng, sem, val):
    plan.add(eng, lambda e, s=sem.h, v=val: e.wait_ge(s, v))


def _ap(x):
    a = getattr(x, "ap", None)
    return x.ap() if callable(a) else x


def pe_mark(plan, mm):
    v = mm.inc()
    last = plan.q["tensor"].pop()
    plan.add("tensor", lambda e, f=last, h=mm.h: f(e).then_inc(h, 1))
    return v


def emit_derive_xT(nc, plan, ctx, x_f32_hbm, xbf_hbm, xT_tiles, done_sem,
                   dma_eng="gpsimd", tr_eng="sync", pre_wait=None):
    cast = ctx["mksem"]("cast")
    x_ap = _ap(x_f32_hbm)
    xbf_ap = _ap(xbf_hbm)
    if pre_wait is not None:
        wait(plan, dma_eng, *pre_wait)
    slab = T // 4
    for i in range(4):
        v = cast.inc(16)
        plan.add(dma_eng, lambda e, i=i, v=v, slab=slab: e.dma_start(
            xbf_ap[i * slab:(i + 1) * slab, :], x_ap[i * slab:(i + 1) * slab, :]
        ).then_inc(cast.h, 16))
    cast_done = cast.v
    wait(plan, tr_eng, cast, cast_done)
    vals = []
    for c in range(DC):
        v = done_sem.inc(16)
        plan.add(tr_eng, lambda e, c=c, v=v: e.dma_start(
            xT_tiles[c][0:128, 0:T], xbf_ap[:, c * 128:(c + 1) * 128], transpose=True
        ).then_inc(done_sem.h, 16))
        vals.append(v)
    return vals


def emit_bias_broadcast(nc, plan, ctx, b_hbm, bbc_sb, ncols, ready_sem):
    brow = ctx["brow_sb"]
    ones1 = ctx["ones1"]
    ld = ctx["mksem"]("brow")
    mm = ctx["mm_sem"]
    v = ld.inc(16)
    plan.add("gpsimd", lambda e, v=v: e.dma_start(
        brow[0:1, 0:ncols], _ap(b_hbm).rearrange("(o d) -> o d", o=1)).then_inc(ld.h, 16))
    wait(plan, "tensor", ld, v)
    wait(plan, "tensor", ctx["init_sem"], ctx["init_ones"])
    ps = ctx["ps"]
    for j in range(ncols // 512):
        plan.add("tensor", lambda e, j=j: e.matmul(
            ps[4][:, 0:512] if j == 0 else ps[5][:, 0:512],
            ones1[0:1, :], brow[0:1, j * 512:(j + 1) * 512],
            start=True, stop=True))
    pe_mark(plan, mm)
    wait(plan, "scalar", mm, mm.v)
    for j in range(ncols // 512):
        rv = ready_sem.inc()
        plan.add("scalar", lambda e, j=j, rv=rv: e.copy(
            bbc_sb[:, j * 512:(j + 1) * 512],
            (ctx["ps"][4] if j == 0 else ctx["ps"][5])[:, 0:512]
        ).then_inc(ready_sem.h, 1))


def emit_ln_epilogue(nc, plan, ctx, y_sb, x1_sb, eps=1e-5):
    lnv = ctx["ln_v"]; lns = ctx["ln_s"]
    par = ctx["ln_par"][0]
    ctx["ln_par"][0] = 1 - par
    st = ctx["ln_st"][par]
    mean = st[:, 12:13]; var = st[:, 13:14]
    lnt = st[:, 14:15]; rs = st[:, 15:16]; nmrs = st[:, 16:17]
    ps_ = ctx.get(f"ln_prev_s{par}", 0)
    pv_ = ctx.get(f"ln_prev_f{par}", 0)
    if ps_:
        wait(plan, "vector", lns, ps_)
    wait(plan, "vector", lnv, max(pv_, lnv.v))
    for half in range(2):
        lnv.inc()
        plan.add("vector", lambda e, half=half: e.bn_stats(
            st[:, 6 * half:6 * half + 6],
            y_sb[:, 512 * half:512 * half + 512]).then_inc(lnv.h, 1))
    wait(plan, "vector", lnv, lnv.v)
    v2 = lnv.inc()
    plan.add("vector", lambda e: e.bn_aggr(st[:, 12:14], st[:, 0:12]).then_inc(lnv.h, 1))
    wait(plan, "scalar", lnv, v2)
    v2s = lns.inc()
    plan.add("scalar", lambda e: e.activation(lnt, var, AF.Ln, bias=ctx["eps_ln"][:]).then_inc(lns.h, 1))
    wait(plan, "scalar", lns, v2s)
    v3 = lns.inc()
    plan.add("scalar", lambda e: e.activation(rs, lnt, AF.Exp, scale=-0.5).then_inc(lns.h, 1))
    wait(plan, "vector", lns, v3)
    vd = lnv.inc()
    plan.add("vector", lambda e: e.scalar_tensor_tensor(
        nmrs, mean, -1.0, rs, ALU.mult, ALU.mult).then_inc(lnv.h, 1))
    wait(plan, "vector", lnv, vd)
    vf = lnv.inc()
    plan.add("vector", lambda e: e.tensor_scalar(
        x1_sb[:], y_sb[:], rs, nmrs, ALU.mult, ALU.add).then_inc(lnv.h, 1))
    ctx[f"ln_prev_s{par}"] = v3
    ctx[f"ln_prev_f{par}"] = vf
    return vf


def mk_common(nc, stack, plan, pool=None):
    ctx = {}
    ctx["_pool"] = pool
    ctx["_off"] = [0]
    ctx["dma_list"] = []

    def salloc(name_, shape, dtype):
        if pool is None:
            return stack.enter_context(nc.sbuf_tensor(name_, shape, dtype))
        nbytes = shape[1] * mybir.dt.size(dtype)
        nbytes = (nbytes + 63) // 64 * 64
        off = ctx["_off"][0]
        ctx["_off"][0] = off + nbytes
        assert ctx["_off"][0] <= pool.shape[1], (
            f"pool overflow: {name_} needs {nbytes}B at {off}, pool {pool.shape[1]}")
        return pool[0:shape[0], off:off + shape[1] * mybir.dt.size(dtype)].bitcast(dtype)

    ctx["salloc"] = salloc
    ctx["pool_reset"] = lambda: ctx["_off"].__setitem__(0, 0)
    ctx["psB"] = [stack.enter_context(nc.psum_tensor(f"psB{i}", [128, 1024], F32))
                  for i in range(4)]
    ctx["ps"] = [ctx["psB"][i // 2][:, (i % 2) * 512:(i % 2) * 512 + 512]
                 for i in range(8)]
    ctx["ones1"] = stack.enter_context(nc.sbuf_tensor("ones1", [1, 128], BF16))
    ctx["ones128"] = stack.enter_context(nc.sbuf_tensor("ones128", [128, 1], BF16))
    ctx["brow_sb"] = stack.enter_context(nc.sbuf_tensor("brow", [1, 1024], BF16))
    ctx["ln_st"] = [stack.enter_context(nc.sbuf_tensor(f"ln_st{i}", [128, 18], F32)) for i in range(2)]
    ctx["ln_par"] = [0]
    ctx["eps_ln"] = stack.enter_context(nc.sbuf_tensor("eps_ln", [128, 1], F32))
    for nm_ in ("ld_sem", "mm_sem", "ln_v", "ln_s", "init_sem"):
        ctx[nm_] = Sem(nc, stack, nm_)
    _n = [0]
    def _mksem(tag="s"):
        _n[0] += 1
        s = Sem(nc, stack, f"{tag}{_n[0]}")
        ctx["dma_list"].append(s)
        return s
    ctx["mksem"] = _mksem
    def _mksem_nobar(tag="s"):
        _n[0] += 1
        return Sem(nc, stack, f"{tag}{_n[0]}")
    ctx["mksem_nobar"] = _mksem_nobar
    iv = ctx["init_sem"].inc(1)
    plan.add("vector", lambda e: e.memset(ctx["ones1"][:], 1.0))
    plan.add("vector", lambda e: e.memset(ctx["ones128"][:], 1.0))
    plan.add("vector", lambda e, v=iv: e.memset(ctx["eps_ln"][:], 1e-5).then_inc(ctx["init_sem"].h, 1))
    ctx["init_ones"] = iv
    return ctx


def build_stage4():
    nc = bass.Bass("TRN2", target_bir_lowering=False)
    x3 = nc.dram_tensor("x3", [T, D], F32, kind="ExternalInput").ap()
    wfr = nc.dram_tensor("wfr", [D, D], F32, kind="ExternalInput").ap()
    wfi = nc.dram_tensor("wfi", [D, D], F32, kind="ExternalInput").ap()
    bias = nc.dram_tensor("bias", [D], F32, kind="ExternalInput").ap()
    fr = nc.dram_tensor("fr", [D, D], F32, kind="ExternalInput").ap()
    fi = nc.dram_tensor("fi", [D, D], F32, kind="ExternalInput").ap()
    fineg = nc.dram_tensor("fineg", [D, D], F32, kind="ExternalInput").ap()
    gr = nc.dram_tensor("gr", [D, D], F32, kind="ExternalInput").ap()
    gineg = nc.dram_tensor("gineg", [D, D], F32, kind="ExternalInput").ap()
    out = nc.dram_tensor("out", [T, D], F32, kind="ExternalOutput").ap()
    x3bf = nc.dram_tensor("x3bf", [T, D], BF16, kind="Internal").ap()

    import contextlib
    with contextlib.ExitStack() as stack:
        plan = Plan()
        ctx = mk_common(nc, stack, plan)
        emit_stage4(nc, stack, plan, ctx, x3, x3bf, wfr, wfi, bias,
                    fr, fi, fineg, gr, gineg, out)
        plan.run(nc)
    return nc


def emit_stage4(nc, stack, plan, ctx, x3, x3bf, wfr, wfi, bias,
                fr, fi, fineg, gr, gineg, out, name="s4"):
    sb = lambda n, shp, dt: ctx["salloc"](name + n, shp, dt)
    cA = [sb(f"cA{j}", [128, 4096], BF16) for j in range(DC)]
    wA = [sb(f"wA{j}", [128, 2048], BF16) for j in range(DC)]
    aT = [sb(f"aT{j}", [128, 2048], BF16) for j in range(DC)]
    mS = [sb(f"mS{j}", [128, 1024], BF16) for j in range(DC)]
    bbc = sb("bbc", [128, 1024], F32)
    ysb = [sb(f"y{j}", [128, 1024], F32) for j in range(2)]
    osb = [sb(f"o{j}", [128, 1024], F32) for j in range(2)]
    xres = [sb(f"xr{j}", [128, 1024], F32) for j in range(2)]

    mm = ctx["mm_sem"]
    ld = ctx["mksem"](name + "ld")
    gld = ctx["mksem"](name + "gld")
    stm = [ctx["mksem"](name + "stA"), ctx["mksem"](name + "stB")]
    ev = Sem(nc, stack, name + "ev")
    xt = Sem(nc, stack, name + "xt")
    bb = Sem(nc, stack, name + "bb")

    def cast_load(dst_tiles, src_ap, col0, sem, eng="gpsimd"):
        for j in range(DC):
            sem.inc(16)
            plan.add(eng, lambda e, j=j: e.dma_start(
                dst_tiles[j][:, col0:col0 + D], src_ap[j * 128:(j + 1) * 128, :]
            ).then_inc(sem.h, 16))

    cast_load(cA, fr, 0, ld)
    cast_load(cA, fi, 1024, ld)
    cast_load(cA, fineg, 2048, ld)
    cast_load(wA, wfr, 0, ld)
    cast_load(wA, wfi, 1024, ld)
    consts_done = ld.v

    emit_bias_broadcast(nc, plan, ctx, bias, bbc, D, bb)
    bb_done = bb.v

    wait(plan, "tensor", ld, consts_done)
    wait(plan, "tensor", ctx["init_sem"], ctx["init_ones"])
    s1_ev = []
    for kt in range(DC):
        for half, (t0, t1) in enumerate(((0, 2048), (1024, 0))):
            for dch in range(2):
                bank = ctx["ps"][(kt * 4 + half * 2 + dch) % 4]
                for j in range(DC):
                    if half == 0:
                        ops = [(wA[j][:, kt * 128:kt * 128 + 128], cA[j][:, dch * 512:dch * 512 + 512]),
                               (wA[j][:, 1024 + kt * 128:1024 + kt * 128 + 128], cA[j][:, 2048 + dch * 512:2048 + dch * 512 + 512])]
                    else:
                        ops = [(wA[j][:, 1024 + kt * 128:1024 + kt * 128 + 128], cA[j][:, dch * 512:dch * 512 + 512]),
                               (wA[j][:, kt * 128:kt * 128 + 128], cA[j][:, 1024 + dch * 512:1024 + dch * 512 + 512])]
                    for oi, (l_, r_) in enumerate(ops):
                        st_ = (j == 0 and oi == 0)
                        sp_ = (j == DC - 1 and oi == 1)
                        plan.add("tensor", lambda e, b=bank, l_=l_, r_=r_, st_=st_, sp_=sp_:
                                 e.matmul(b[:, 0:512], l_, r_, start=st_, stop=sp_))
                pe_mark(plan, mm)
                col0 = half * 1024 + dch * 512
                wait(plan, "scalar", mm, mm.v)
                v = ev.inc()
                plan.add("scalar", lambda e, kt=kt, col0=col0, b=bank, v=v:
                         e.copy(aT[kt][:, col0:col0 + 512], b[:, 0:512]).then_inc(ev.h, 1))
                s1_ev.append(v)
                if len(s1_ev) >= 4:
                    wait(plan, "tensor", ev, s1_ev[-4])
    s1_done = ev.v

    wait(plan, "gpsimd", mm, mm.v)
    cast_load(wA, gr, 0, gld)
    cast_load(wA, gineg, 1024, gld)
    g_all = gld.v

    wait(plan, "sync", mm, mm.v)
    emit_derive_xT(nc, plan, ctx, x3, x3bf, cA, xt, pre_wait=None)
    xt_all = xt.v

    wait(plan, "tensor", gld, g_all)
    wait(plan, "tensor", ev, s1_done)
    s2_ev = []
    for dt_ in range(DC):
        for och in range(2):
            bank = ctx["ps"][(dt_ * 2 + och) % 4]
            n_mm = 0
            for k in range(DC):
                for oi, (l_, r_) in enumerate((
                        (aT[k][:, dt_ * 128:dt_ * 128 + 128], wA[k][:, och * 512:och * 512 + 512]),
                        (aT[k][:, 1024 + dt_ * 128:1024 + dt_ * 128 + 128], wA[k][:, 1024 + och * 512:1024 + och * 512 + 512]))):
                    st_ = (k == 0 and oi == 0)
                    sp_ = (k == DC - 1 and oi == 1)
                    plan.add("tensor", lambda e, b=bank, l_=l_, r_=r_, st_=st_, sp_=sp_:
                             e.matmul(b[:, 0:512], l_, r_, start=st_, stop=sp_))
            pe_mark(plan, mm)
            wait(plan, "scalar", mm, mm.v)
            v = ev.inc()
            plan.add("scalar", lambda e, dt_=dt_, och=och, b=bank, v=v:
                     e.copy(mS[dt_][:, och * 512:och * 512 + 512], b[:, 0:512]).then_inc(ev.h, 1))
            s2_ev.append(v)
            if len(s2_ev) >= 4:
                wait(plan, "tensor", ev, s2_ev[-4])
    m_done = ev.v

    wait(plan, "tensor", ev, m_done)
    wait(plan, "tensor", xt, xt_all)
    wait(plan, "tensor", bb, bb_done)
    res_sem = [ctx["mksem"](name + "rA"), ctx["mksem"](name + "rB")]
    vv_of, ln_of, st_of = {}, {}, {}
    for nt in range(NT):
        pb = nt % 2
        banks = (ctx["ps"][4 + 2 * pb], ctx["ps"][5 + 2 * pb])
        if nt >= 2:
            wait(plan, "sync", ctx["ln_v"], vv_of[nt - 2])
        v = res_sem[pb].inc(16)
        plan.add("sync", lambda e, nt=nt, pb=pb, v=v: e.dma_start(
            xres[pb][:], x3[nt * 128:(nt + 1) * 128, :]).then_inc(res_sem[pb].h, 16))
        if nt >= 2:
            wait(plan, "tensor", ctx["ln_v"], vv_of[nt - 2])
        for j in range(DC):
            st_ = (j == 0)
            sp_ = (j == DC - 1)
            l_ = cA[j][:, nt * 128:(nt + 1) * 128]
            for och in range(2):
                r_ = mS[j][:, och * 512:och * 512 + 512]
                plan.add("tensor", lambda e, b=banks[och], l_=l_, r_=r_, st_=st_, sp_=sp_:
                         e.matmul(b[:, 0:512], l_, r_, start=st_, stop=sp_))
        mm_here = pe_mark(plan, mm)
        wait(plan, "vector", mm, mm_here)
        wait(plan, "vector", res_sem[pb], v)
        vpre = ctx["ln_v"].inc(2)
        for och in range(2):
            plan.add("vector", lambda e, b=banks[och], och=och, pb=pb:
                     e.tensor_add(ysb[pb][:, och * 512:och * 512 + 512], b[:, 0:512],
                                  bbc[:, och * 512:och * 512 + 512]).then_inc(ctx["ln_v"].h, 1))
        wait(plan, "vector", ctx["ln_v"], vpre)
        vv = ctx["ln_v"].inc()
        plan.add("vector", lambda e, pb=pb, v=vv: e.tensor_add(
            ysb[pb][:], ysb[pb][:], xres[pb][:]).then_inc(ctx["ln_v"].h, 1))
        vv_of[nt] = vv
        wait(plan, "scalar", ctx["ln_v"], vv)
        if nt >= 2:
            wait(plan, "vector", stm[pb], st_of[nt - 2])
        lnfin = emit_ln_epilogue(nc, plan, ctx, ysb[pb], osb[pb])
        ln_of[nt] = lnfin
        wait(plan, "gpsimd", ctx["ln_v"], lnfin)
        sv = stm[pb].inc(16)
        st_of[nt] = sv
        plan.add("gpsimd", lambda e, nt=nt, pb=pb: e.dma_start(
            out[nt * 128:(nt + 1) * 128, :], osb[pb][:]).then_inc(stm[pb].h, 16))
    wait(plan, "gpsimd", stm[0], stm[0].v)
    wait(plan, "gpsimd", stm[1], stm[1].v)
    return




def emit_barrier(nc, plan, ctx):
    for eng in Plan.ENGINES:
        wait(plan, eng, ctx["mm_sem"], ctx["mm_sem"].v)
        wait(plan, eng, ctx["ln_s"], ctx["ln_s"].v)
        wait(plan, eng, ctx["ln_v"], ctx["ln_v"].v)
        for s in ctx["dma_list"]:
            if s.v:
                wait(plan, eng, s, s.v)


def build_stage1():
    nc = bass.Bass("TRN2", target_bir_lowering=False)
    x = nc.dram_tensor("x", [T, D], F32, kind="ExternalInput").ap()
    wqkv = nc.dram_tensor("wqkv", [D, 3 * D], F32, kind="ExternalInput").ap()
    wproj = nc.dram_tensor("wproj", [D, D], F32, kind="ExternalInput").ap()
    bproj = nc.dram_tensor("bproj", [D], F32, kind="ExternalInput").ap()
    masks = nc.dram_tensor("masks", [128, 2048], BF16, kind="ExternalInput").ap()
    x1 = nc.dram_tensor("x1", [T, D], F32, kind="ExternalOutput").ap()
    xbf = nc.dram_tensor("xbf", [T, D], BF16, kind="Internal").ap()
    ahbm = nc.dram_tensor("ahbm", [D, T], BF16, kind="Internal").ap()

    import contextlib
    with contextlib.ExitStack() as stack:
        plan = Plan()
        ctx = mk_common(nc, stack, plan)
        emit_stage1(nc, stack, plan, ctx, x, xbf, wqkv, wproj, bproj, masks,
                    ahbm, x1)
        plan.run(nc)
    return nc


def emit_stage1(nc, stack, plan, ctx, x, xbf, wqkv, wproj, bproj, masks_h,
                ahbm, x1, name="s1"):
    sb = lambda n, shp, dt: ctx["salloc"](name + n, shp, dt)
    mksem = ctx["mksem"]
    NQB = T // 512
    arena = [sb(f"ar{c}", [128, T], BF16) for c in range(DC)]
    wg = [sb(f"wg{c}", [128, 768], BF16) for c in range(DC)]
    qTg = [sb(f"qT{p}", [128, T], BF16) for p in range(2)]
    kTg = [sb(f"kT{p}", [128, T], BF16) for p in range(2)]
    vext = sb("vext", [128, NT * 4 * 65], BF16)
    pbuf = sb("pbuf", [128, 2048], BF16)
    cb = sb("cb", [128, 16], F32)
    nbs = sb("nbs", [1, 16], F32)
    nbs_bf = sb("nbsb", [1, 2], BF16)
    msk = sb("msk", [128, 2048], BF16)
    apair = [sb(f"ap{p}", [128, 512], BF16) for p in range(2)]
    rrow = sb("rrow", [1, 1024], BF16)
    rbc = sb("rbc", [128, 512], F32)
    wpj = [sb(f"wp{c}", [128, 1024], BF16) for c in range(DC)]
    bbc = sb("bbc", [128, 1024], F32)
    ysb = [sb(f"y{p}", [128, 1024], F32) for p in range(2)]
    osb = [sb(f"o{p}", [128, 1024], F32) for p in range(2)]
    xres = [sb(f"xr{p}", [128, 1024], F32) for p in range(2)]

    ps = ctx["ps"]
    mm = ctx["mm_sem"]; lnv = ctx["ln_v"]; lns = ctx["ln_s"]
    xt = mksem(name + "xt")
    wld = mksem(name + "wld")
    pld = mksem(name + "pld")
    ast = [mksem(name + f"as{p}") for p in range(2)]
    atl = mksem(name + "atl")
    res2 = [mksem(name + "res2A"), mksem(name + "res2B")]
    stm = [mksem(name + f"st{p}") for p in range(2)]
    bb = Sem(nc, stack, name + "bb")

    emit_derive_xT(nc, plan, ctx, x, xbf, arena, xt)
    xt_all = xt.v
    pld.inc(16)
    plan.add("sync", lambda e: e.dma_start(msk[:], _ap(masks_h)).then_inc(pld.h, 16))
    pld_all = pld.v
    wpld = mksem(name + "wpld")

    def emit_wpj_loads():
        for c in range(DC):
            wpld.inc(16)
            plan.add("gpsimd", lambda e, c=c: e.dma_start(
                wpj[c][:], _ap(wproj)[c * 128:(c + 1) * 128, :]).then_inc(wpld.h, 16))
    emit_bias_broadcast(nc, plan, ctx, bproj, bbc, D, bb)
    bb_done = bb.v
    lnv.inc()
    plan.add("vector", lambda e: e.memset(
        vext[:].rearrange("p (s c) -> p s c", c=65)[:, :, 64:65], 1.0).then_inc(lnv.h, 1))
    vext_ones = lnv.v

    projmark = {}
    tile12 = 0
    st_of = {}
    for g in range(4):
        pf = ctx.pop(name + "pending_fin", None)
        if pf is not None:
            pf()
        if g >= 1:
            wait(plan, "gpsimd", mm, projmark[g - 1])
        for c in range(DC):
            for oi, (src0, w_) in enumerate(((g * 256, 256), (D + g * 256, 256), (2 * D + g * 256, 256))):
                wld.inc(16)
                plan.add("gpsimd", lambda e, c=c, src0=src0, oi=oi: e.dma_start(
                    wg[c][:, oi * 256:(oi + 1) * 256],
                    _ap(wqkv)[c * 128:(c + 1) * 128, src0:src0 + 256]).then_inc(wld.h, 16))
        if g == 0:
            emit_wpj_loads()
        pcc = ctx.get("precast_chunks")
        if pcc:
            pcc.pop(0)()
        wld_here = wld.v
        wait(plan, "tensor", wld, wld_here)
        if g >= 1:
            wait(plan, "tensor", lns, lns.v)
            wait(plan, "tensor", lnv, lnv.v)
        if g == 0:
            wait(plan, "tensor", xt, xt_all)
            wait(plan, "tensor", ctx["init_sem"], ctx["init_ones"])
        evac = {}
        idx = 0
        for role, dstl, coloff, scale in (("q", qTg, 0, 0.125), ("k", kTg, 256, 1.0)):
            for pt in range(2):
                for nsb in range(T // 512):
                    bank = ps[idx % 2]
                    if idx >= 2:
                        wait(plan, "tensor", lns, evac[idx - 2])
                    for c in range(DC):
                        plan.add("tensor", lambda e, c=c, pt=pt, nsb=nsb, coloff=coloff, bank=bank:
                                 e.matmul(bank[:, 0:512],
                                          wg[c][0:128, coloff + pt * 128:coloff + (pt + 1) * 128],
                                          arena[c][0:128, nsb * 512:(nsb + 1) * 512],
                                          start=(c == 0), stop=(c == DC - 1)))
                    mk = pe_mark(plan, mm)
                    wait(plan, "scalar", mm, mk)
                    lns.inc()
                    plan.add("scalar", lambda e, dstl=dstl, pt=pt, nsb=nsb, bank=bank, scale=scale:
                             e.activation(dstl[pt][:, nsb * 512:(nsb + 1) * 512],
                                          bank[:, 0:512], AF.Copy, scale=float(scale)).then_inc(lns.h, 1))
                    evac[idx] = lns.v
                    idx += 1
        vidx = 0
        vevac = {}
        for nt in range(NT):
            bank = ps[2 + nt % 2]
            if vidx >= 2:
                wait(plan, "tensor", lns, vevac[vidx - 2])
            for c in range(DC):
                plan.add("tensor", lambda e, c=c, nt=nt, bank=bank:
                         e.matmul(bank[:, 0:256],
                                  arena[c][0:128, nt * 128:(nt + 1) * 128],
                                  wg[c][0:128, 512:768],
                                  start=(c == 0), stop=(c == DC - 1)))
            mk = pe_mark(plan, mm)
            wait(plan, "scalar", mm, mk)
            if g == 0 and nt == 0:
                wait(plan, "scalar", lnv, vext_ones)
            for hg in range(4):
                lns.inc()
                plan.add("scalar", lambda e, nt=nt, hg=hg, bank=bank: e.copy(
                    vext[:, (nt * 4 + hg) * 65:(nt * 4 + hg) * 65 + 64],
                    bank[:, hg * 64:(hg + 1) * 64]).then_inc(lns.h, 1))
            vevac[vidx] = lns.v
            vidx += 1
        projmark[g] = mm.v
        proj_evac_all = lns.v

        wait(plan, "scalar", lns, proj_evac_all)
        for pt in range(2):
            for sl, srcl in ((0, qTg), (1, kTg)):
                first = True
                for half in range(max(T // 2048, 1)):
                    w_ = min(2048, T)
                    wait(plan, "scalar", mm, ctx.get(name + "bndmm", 0))
                    sqmark = lns.inc()
                    plan.add("scalar", lambda e, pt=pt, srcl=srcl, half=half, w_=w_: e.activation(
                        pbuf[:, 0:w_], srcl[pt][:, half * w_:(half + 1) * w_],
                        AF.Square).then_inc(lns.h, 1))
                    wait(plan, "tensor", lns, sqmark)
                    if first:
                        wait(plan, "tensor", lnv, ctx.get(name + "bndv", 0))
                    nhalves = max(T // 2048, 1)
                    nch = w_ // 512
                    for hh in range(2):
                        for ch in range(nch):
                            last_ = (half == nhalves - 1 and ch == nch - 1)
                            plan.add("tensor", lambda e, hh=hh, ch=ch, first=first, last_=last_:
                                     e.matmul((ps[2] if hh == 0 else ps[7])[0:1, 0:512],
                                              ctx["ones128"][hh * 64:hh * 64 + 64, 0:1],
                                              pbuf[hh * 64:hh * 64 + 64, ch * 512:(ch + 1) * 512],
                                              start=(first and ch == 0), stop=last_))
                    first = False
                    mk = pe_mark(plan, mm)
                    ctx[name + "bndmm"] = mk
                wait(plan, "vector", mm, mk)
                for hh in range(2):
                    h4 = 2 * pt + hh
                    lnv.inc()
                    plan.add("vector", lambda e, h4=h4, sl=sl, hh=hh: e.reduce_max(
                        nbs[0:1, 2 * h4 + sl:2 * h4 + sl + 1],
                        (ps[2] if hh == 0 else ps[7])[0:1, 0:512], axis=AX.X).then_inc(lnv.h, 1))
                ctx[name + "bndv"] = lnv.v
            p_ = 2 * g + pt
            for hh in range(2):
                bidx = 2 * (2 * pt + hh)
                wait(plan, "vector", lnv, ctx[name + "bndv"])
                lnv.inc()
                plan.add("vector", lambda e, bidx=bidx: e.tensor_mul(
                    nbs[0:1, 8 + bidx:9 + bidx], nbs[0:1, bidx:bidx + 1],
                    nbs[0:1, bidx + 1:bidx + 2]).then_inc(lnv.h, 1))
            bidx0 = 2 * (2 * pt)
            wait(plan, "vector", lnv, lnv.v)
            lnv.inc()
            plan.add("vector", lambda e, bidx0=bidx0: e.tensor_max(
                nbs[0:1, 8 + bidx0:9 + bidx0], nbs[0:1, 8 + bidx0:9 + bidx0],
                nbs[0:1, 10 + bidx0:11 + bidx0]).then_inc(lnv.h, 1))
            wait(plan, "scalar", lnv, lnv.v)
            lns.inc()
            plan.add("scalar", lambda e, bidx0=bidx0: e.activation(
                nbs[0:1, 9 + bidx0:10 + bidx0], nbs[0:1, 8 + bidx0:9 + bidx0],
                AF.Ln).then_inc(lns.h, 1))
            wait(plan, "scalar", lns, lns.v)
            lns.inc()
            plan.add("scalar", lambda e, bidx0=bidx0: e.activation(
                nbs[0:1, 8 + bidx0:9 + bidx0], nbs[0:1, 9 + bidx0:10 + bidx0],
                AF.Exp, scale=0.5).then_inc(lns.h, 1))
            wait(plan, "vector", lns, lns.v)
            wait(plan, "vector", mm, mm.v)
            lnv.inc()
            plan.add("vector", lambda e, bidx0=bidx0: e.tensor_scalar_mul(
                nbs_bf[0:1, 0:1], nbs[0:1, 8 + bidx0:9 + bidx0], -1.0).then_inc(lnv.h, 1))
            wait(plan, "tensor", lnv, lnv.v)
            plan.add("tensor", lambda e: e.matmul(
                ps[7][0:128, 0:1], ctx["ones1"][0:1, :], nbs_bf[0:1, 0:1],
                start=True, stop=True))
            mk = pe_mark(plan, mm)
            wait(plan, "scalar", mm, mk)
            lns.inc()
            plan.add("scalar", lambda e, p_=p_: e.copy(
                cb[:, p_:p_ + 1], ps[7][0:128, 0:1]).then_inc(lns.h, 1))
        cb_done = lns.v

        wait(plan, "tensor", lns, cb_done)
        if g == 0:
            wait(plan, "tensor", pld, pld_all)
            wait(plan, "vector", pld, pld_all)
        for pt in range(2):
            pair = 2 * g + pt
            for qb in range(NQB):
                J = 4 * qb + 4
                expm = {}
                avm = {}
                maskm = {}
                pb = tile12 % 2
                wait(plan, "tensor", lns, ctx.get(name + "exp_last", 0))
                jpos = {}

                def emit_S(j):
                    ji = len(jpos)
                    jpos[j] = ji
                    SS = ctx["psB"][2 + ji % 2]
                    if ji >= 2:
                        wait(plan, "tensor", lns, expm[ji - 2])
                    plan.add("tensor", lambda e, pt=pt, j=j, qb=qb, SS=SS:
                             e.matmul(SS[:, 0:512], kTg[pt][0:64, j * 128:(j + 1) * 128],
                                      qTg[pt][0:64, qb * 512:(qb + 1) * 512],
                                      start=True, stop=True, tile_position=(0, 0)))
                    plan.add("tensor", lambda e, pt=pt, j=j, qb=qb, SS=SS:
                             e.matmul(SS[:, 512:1024], kTg[pt][64:128, j * 128:(j + 1) * 128],
                                      qTg[pt][64:128, qb * 512:(qb + 1) * 512],
                                      start=True, stop=True, tile_position=(64, 0)))
                    mmS[j] = pe_mark(plan, mm)

                def emit_expmask(j):
                    ji = jpos[j]
                    SS = ctx["psB"][2 + ji % 2]
                    wait(plan, "scalar", mm, mmS[j])
                    if ji >= 2:
                        wait(plan, "scalar", mm, avm[ji - 2])
                    p_ = 2 * g + pt
                    lns.inc()
                    plan.add("scalar", lambda e, ji=ji, SS=SS, p_=p_: e.activation(
                        pbuf[:, (ji % 2) * 1024:(ji % 2) * 1024 + 1024],
                        SS[:, 0:1024], AF.Exp, bias=cb[:, p_:p_ + 1]).then_inc(lns.h, 1))
                    expm[ji] = lns.v
                    if j >= 4 * qb:
                        r_off = (j - 4 * qb) * 512
                        wait(plan, "vector", lns, expm[ji])
                        for hh in range(2):
                            lnv.inc()
                            plan.add("vector", lambda e, hh=hh, ji=ji, r_off=r_off: e.tensor_mul(
                                pbuf[:, (ji % 2) * 1024 + hh * 512:(ji % 2) * 1024 + hh * 512 + 512],
                                pbuf[:, (ji % 2) * 1024 + hh * 512:(ji % 2) * 1024 + hh * 512 + 512],
                                msk[:, r_off:r_off + 512]).then_inc(lnv.h, 1))
                        maskm[ji] = lnv.v

                def emit_av(j, first, last):
                    ji = jpos[j]
                    if j >= 4 * qb:
                        wait(plan, "tensor", lnv, maskm[ji])
                    else:
                        wait(plan, "tensor", lns, expm[ji])
                    for hh, attb in ((0, ps[0]), (1, ps[1])):
                        hg = 2 * pt + hh
                        plan.add("tensor", lambda e, hh=hh, j=j, ji=ji, hg=hg, attb=attb,
                                 first=first, last=last:
                                 e.matmul(attb[0:65, 0:512],
                                          vext[:, (j * 4 + hg) * 65:(j * 4 + hg) * 65 + 65],
                                          pbuf[:, (ji % 2) * 1024 + hh * 512:
                                               (ji % 2) * 1024 + hh * 512 + 512],
                                          start=first, stop=last))
                    avm[ji] = pe_mark(plan, mm)

                mmS = {}
                jorder = list(range(4 * qb, J)) + list(range(4 * qb))
                emit_S(jorder[0])
                pf = ctx.pop(name + "pending_fin", None)
                if pf is not None:
                    pf()
                for ji, j in enumerate(jorder):
                    emit_expmask(j)
                    if ji + 1 < J:
                        emit_S(jorder[ji + 1])
                    if ji == 0:
                        wait(plan, "tensor", lnv, ctx.get(name + "finp", 0))
                    emit_av(j, first=(ji == 0), last=(ji == J - 1))
                ctx[name + "exp_last"] = expm[J - 1]
                wait(plan, "vector", mm, avm[J - 1])
                wait(plan, "vector", mm, ctx.get(name + "rrm", 0))
                for hh, attb in ((0, ps[0]), (1, ps[1])):
                    lnv.inc()
                    def _recip(e, hh=hh, attb=attb):
                        with nc.allow_low_precision(reason="softmax 1/l to bf16 matmul operand"):
                            return e.reciprocal(rrow[0:1, hh * 512:hh * 512 + 512],
                                                attb[64:65, 0:512])
                    plan.add("vector", lambda e, f=_recip: f(e).then_inc(lnv.h, 1))
                rcopy_done = lnv.v

                def part2(pair=pair, qb=qb, pb=pb, t12=tile12, rcopy_done=rcopy_done):
                    wait(plan, "tensor", lnv, rcopy_done)
                    wait(plan, "tensor", lns, ctx.get(name + "rbs", 0))
                    for hh in range(2):
                        plan.add("tensor", lambda e, hh=hh: e.matmul(
                            ps[2][hh * 64:hh * 64 + 64, 0:512],
                            ctx["ones1"][0:1, hh * 64:hh * 64 + 64],
                            rrow[0:1, hh * 512:hh * 512 + 512],
                            start=True, stop=True))
                    rbm = pe_mark(plan, mm)
                    ctx[name + "rrm"] = rbm
                    wait(plan, "scalar", mm, rbm)
                    wait(plan, "scalar", lnv, ctx.get(name + "rbv", 0))
                    lns.inc()
                    plan.add("scalar", lambda e: e.copy(rbc[:], ps[2][:, 0:512]).then_inc(lns.h, 1))
                    ctx[name + "rbs"] = lns.v
                    wait(plan, "vector", lns, lns.v)
                    if t12 >= 2:
                        wait(plan, "vector", ast[pb], st_of.get(t12 - 2, 0))
                    for hh, attb in ((0, ps[0]), (1, ps[1])):
                        lnv.inc()
                        plan.add("vector", lambda e, hh=hh, attb=attb, pb=pb: e.tensor_mul(
                            apair[pb][hh * 64:hh * 64 + 64, 0:512],
                            attb[0:64, 0:512],
                            rbc[hh * 64:hh * 64 + 64, 0:512]).then_inc(lnv.h, 1))
                    ctx[name + "rbv"] = lnv.v
                    ctx[name + "finp"] = lnv.v
                    wait(plan, "gpsimd", lnv, lnv.v)
                    sv = ast[pb].inc(16)
                    st_of[t12] = sv
                    plan.add("gpsimd", lambda e, pair=pair, qb=qb, pb=pb: e.dma_start(
                        _ap(ahbm)[pair * 128:(pair + 1) * 128, qb * 512:(qb + 1) * 512],
                        apair[pb][:, 0:512]).then_inc(ast[pb].h, 16))

                ctx[name + "pending_fin"] = part2
                tile12 += 1

    pf = ctx.pop(name + "pending_fin", None)
    if pf is not None:
        pf()
    wait(plan, "sync", ast[0], ast[0].v)
    wait(plan, "sync", ast[1], ast[1].v)
    wait(plan, "sync", mm, projmark[3])
    for c in range(DC):
        atl.inc(16)
        plan.add("sync", lambda e, c=c: e.dma_start(
            arena[c][0:128, 0:T], _ap(ahbm)[c * 128:(c + 1) * 128, 0:T]
        ).then_inc(atl.h, 16))
    atl_all = atl.v

    wait(plan, "tensor", atl, atl_all)
    wait(plan, "tensor", wpld, wpld.v)
    wait(plan, "tensor", bb, bb_done)
    vv_of, ln_of2, st_of2 = {}, {}, {}
    for nt in range(NT):
        pb = nt % 2
        banks = (ps[4 + 2 * pb], ps[5 + 2 * pb])
        if nt >= 2:
            wait(plan, "sync", lnv, vv_of[nt - 2])
        rv = res2[pb].inc(16)
        plan.add("sync", lambda e, nt=nt, pb=pb: e.dma_start(
            xres[pb][:], _ap(x)[nt * 128:(nt + 1) * 128, :]).then_inc(res2[pb].h, 16))
        if nt >= 2:
            wait(plan, "tensor", lnv, vv_of[nt - 2])
        for c in range(DC):
            for och in range(2):
                plan.add("tensor", lambda e, c=c, nt=nt, och=och, b=banks[och]:
                         e.matmul(b[:, 0:512], arena[c][0:128, nt * 128:(nt + 1) * 128],
                                  wpj[c][0:128, och * 512:och * 512 + 512],
                                  start=(c == 0), stop=(c == DC - 1)))
        mm_here = pe_mark(plan, mm)
        wait(plan, "vector", mm, mm_here)
        wait(plan, "vector", res2[pb], rv)
        if nt == 0:
            wait(plan, "vector", bb, bb_done)
        vpre = lnv.inc(2)
        for och in range(2):
            plan.add("vector", lambda e, b=banks[och], och=och, pb=pb:
                     e.tensor_add(ysb[pb][:, och * 512:och * 512 + 512], b[:, 0:512],
                                  bbc[:, och * 512:och * 512 + 512]).then_inc(lnv.h, 1))
        wait(plan, "vector", lnv, vpre)
        vv = lnv.inc()
        plan.add("vector", lambda e, pb=pb: e.tensor_add(
            ysb[pb][:], ysb[pb][:], xres[pb][:]).then_inc(lnv.h, 1))
        vv_of[nt] = vv
        wait(plan, "scalar", lnv, vv)
        if nt >= 2:
            wait(plan, "vector", stm[pb], st_of2[nt - 2])
        lnfin = emit_ln_epilogue(nc, plan, ctx, ysb[pb], osb[pb])
        ln_of2[nt] = lnfin
        wait(plan, "gpsimd", lnv, lnfin)
        sv = stm[pb].inc(16)
        st_of2[nt] = sv
        plan.add("gpsimd", lambda e, nt=nt, pb=pb: e.dma_start(
            _ap(x1)[nt * 128:(nt + 1) * 128, :], osb[pb][:]).then_inc(stm[pb].h, 16))
    wait(plan, "gpsimd", stm[0], stm[0].v)
    wait(plan, "gpsimd", stm[1], stm[1].v)
    return



def build_stage2():
    nc = bass.Bass("TRN2", target_bir_lowering=False)
    x1 = nc.dram_tensor("x1", [T, D], F32, kind="ExternalInput").ap()
    wrt = nc.dram_tensor("wrt", [D, 8], F32, kind="ExternalInput").ap()
    w1 = nc.dram_tensor("w1", [8 * 4096, D], F32, kind="ExternalInput").ap()
    w2 = nc.dram_tensor("w2", [8 * D, 2048], F32, kind="ExternalInput").ap()
    x2 = nc.dram_tensor("x2", [T, D], F32, kind="ExternalOutput").ap()
    x1bf = nc.dram_tensor("x1bf", [T, D], BF16, kind="Internal").ap()
    w1bf = nc.dram_tensor("w1bf", [8 * 4096, D], BF16, kind="Internal").ap()
    w2bf = nc.dram_tensor("w2bf", [8 * D, 2048], BF16, kind="Internal").ap()

    import contextlib
    with contextlib.ExitStack() as stack:
        plan = Plan()
        ctx = mk_common(nc, stack, plan)
        emit_stage2(nc, stack, plan, ctx, x1, wrt, w1, w2, x1bf, w1bf,
                    w2bf, x2)
        plan.run(nc)
    return nc


def emit_weight_precasts(nc, plan, ctx, w1, w2, w1bf, w2bf, wqkv, wqkvbf,
                         wout, woutbf):
    mksem = ctx["mksem_nobar"]
    caste = [mksem(f"pce{e}") for e in range(8)]
    c3 = mksem("pc3")
    for e in range(8):
        caste[e].inc(32)
    c3.inc(64)
    c3_final = c3.v

    def chunk(es, with_c3=False):
        def emit():
            for e in es:
                plan.add("gpsimd", lambda e_, e=e: e_.dma_start(
                    _ap(w1bf)[e * 4096:(e + 1) * 4096, :],
                    _ap(w1)[e * 4096:(e + 1) * 4096, :]).then_inc(caste[e].h, 16))
                plan.add("gpsimd", lambda e_, e=e: e_.dma_start(
                    _ap(w2bf)[e * D:(e + 1) * D, :],
                    _ap(w2)[e * D:(e + 1) * D, :]).then_inc(caste[e].h, 16))
            if with_c3:
                for i in range(3):
                    plan.add("gpsimd", lambda e_, i=i: e_.dma_start(
                        _ap(wqkvbf)[i * D:(i + 1) * D, :],
                        _ap(wqkv)[i * D:(i + 1) * D, :]).then_inc(c3.h, 16))
                plan.add("gpsimd", lambda e_: e_.dma_start(
                    _ap(woutbf), _ap(wout)).then_inc(c3.h, 16))
        return emit

    chunks = [chunk([0, 1]), chunk([2, 3]), chunk([4, 5]), chunk([6, 7], True)]
    return {"caste": caste, "c3": (c3, c3_final), "chunks": chunks}


def emit_stage2(nc, stack, plan, ctx, x1, wrt, w1, w2, x1bf, w1bf, w2bf,
                x2, name="s2", ext_casts=None):
    E = 8
    QT = min(1024, T)
    NQ = T // QT
    NTQ = QT // 128
    NT2 = QT // 256
    sb = lambda n, shp, dt: ctx["salloc"](name + n, shp, dt)
    mksem = ctx["mksem"]
    x1T = [[sb(f"xT{p}_{c}", [128, QT], BF16) for c in range(DC)] for p in range(2)]
    w1tv = [[sb(f"w1v{b}_{c}", [128, 1024], BF16) for c in range(DC)] for b in range(2)]
    w1tg = [[sb(f"w1g{b}_{c}", [128, 1024], BF16) for c in range(DC)] for b in range(2)]
    w2t = [[sb(f"w2t{b}_{f}", [128, 1024], BF16) for f in range(DC)] for b in range(2)]
    wrT = sb("wrT", [128, 64], BF16)
    acta = [[sb(f"act{p}_{f}", [128, 512], BF16) for f in range(DC)] for p in range(2)]
    sg = [sb(f"sg{p}", [128, 512], F32) for p in range(2)]
    rwq = sb("rwq", [128, 8 * NTQ], F32)
    lg = sb("lg", [128, 8 * NTQ], F32)
    rt = sb("rt", [128, 40], F32)
    yac = [sb(f"ya{t}", [128, 1024], F32) for t in range(NTQ)]
    osb = [sb(f"o{p}", [128, 1024], F32) for p in range(2)]
    xres = [sb("xr0", [128, 1024], F32)]

    ps = ctx["ps"]
    mm = ctx["mm_sem"]; lnv = ctx["ln_v"]; lns = ctx["ln_s"]
    cast = mksem(name + "cast")
    caste = [mksem(name + f"ce{e}") for e in range(E)]
    wrld = mksem(name + "wrld")
    xtq = mksem(name + "xtq")
    wld = [mksem(name + f"wld{b}") for b in range(2)]
    res = mksem(name + "res")
    stm = [mksem(name + f"st{p}") for p in range(2)]

    def vop(fn):
        lnv.inc()
        plan.add("vector", lambda e_, fn=fn: fn(e_).then_inc(lnv.h, 1))
        wait(plan, "vector", lnv, lnv.v)

    slab = T // 4
    for i in range(4):
        cast.inc(16)
        plan.add("gpsimd", lambda e_, i=i, slab=slab: e_.dma_start(
            x1bf[i * slab:(i + 1) * slab, :], x1[i * slab:(i + 1) * slab, :]).then_inc(cast.h, 16))
    cast_all = cast.v
    if ext_casts is not None:
        caste = ext_casts["caste"]
    else:
        for e in range(E):
            caste[e].inc(32)
            plan.add("gpsimd", lambda e_, e=e: e_.dma_start(
                w1bf[e * 4096:(e + 1) * 4096, :], w1[e * 4096:(e + 1) * 4096, :]).then_inc(caste[e].h, 16))
            plan.add("gpsimd", lambda e_, e=e: e_.dma_start(
                w2bf[e * D:(e + 1) * D, :], w2[e * D:(e + 1) * D, :]).then_inc(caste[e].h, 16))

    for c in range(DC):
        wrld.inc(16)
        plan.add("gpsimd", lambda e_, c=c: e_.dma_start(
            wrT[:, c * 8:(c + 1) * 8], _ap(wrt)[c * 128:(c + 1) * 128, :]
        ).then_inc(wrld.h, 16))
    wrld_all = wrld.v

    wait(plan, "tensor", ctx["init_sem"], ctx["init_ones"])
    q_hend = {}
    buffree = [0, 0]
    actm_h = {}
    yst_h = {}
    lastY = [0, 0]
    gfs = 0
    gy = 0
    rnd = 0
    st_of = {}
    tile_base = 0

    def emit_x1T_loads(q):
        if q >= NQ:
            return
        qb2 = q % 2
        if q >= 2:
            wait(plan, "sync", mm, q_hend.get(q - 2, 0))
        nslab = min(4, -(-((q + 1) * QT * 4) // T))
        wait(plan, "sync", cast, 16 * nslab)
        xtq.inc(16 * DC)
        for c in range(DC):
            plan.add("sync", lambda e_, q=q, c=c, qb2=qb2: e_.dma_start(
                x1T[qb2][c][0:128, 0:QT],
                x1bf[q * QT:(q + 1) * QT, c * 128:(c + 1) * 128],
                transpose=True).then_inc(xtq.h, 16))
        return xtq.v

    def emit_router(q, xv):
        qb2 = q % 2
        wait(plan, "tensor", xtq, xv)
        if q == 0:
            wait(plan, "tensor", wrld, wrld_all)
        wait(plan, "tensor", lnv, actm_h.get(gfs - 1, 0))
        wait(plan, "scalar", lnv, ctx.get(name + "rtv", 0))
        for ntq in range(NTQ):
            for c in range(DC):
                plan.add("tensor", lambda e_, ntq=ntq, c=c, qb2=qb2: e_.matmul(
                    ps[0][:, ntq * 8:(ntq + 1) * 8],
                    x1T[qb2][c][0:128, ntq * 128:(ntq + 1) * 128],
                    wrT[:, c * 8:(c + 1) * 8],
                    start=(ntq == 0 and c == 0), stop=(ntq == NTQ - 1 and c == DC - 1)))
        mk = pe_mark(plan, mm)
        wait(plan, "scalar", mm, mk)
        lns.inc()
        plan.add("scalar", lambda e_: e_.copy(lg[:], ps[0][:, 0:8 * NTQ]).then_inc(lns.h, 1))
        lg_done = lns.v
        ctx[name + "lgd"] = lg_done
        wait(plan, "vector", lns, lg_done)
        for ntq in range(NTQ):
            lgs = lg[:, ntq * 8:(ntq + 1) * 8]
            m1 = rt[:, 0:1]; eq1 = rt[:, 1:9]; mskd = rt[:, 9:17]
            m2 = rt[:, 17:18]; eq2 = rt[:, 24:32]; dd = rt[:, 18:19]
            ex = rt[:, 19:20]; tt = rt[:, 20:21]; w1v = rt[:, 21:22]
            w2v = rt[:, 22:23]; t2 = rt[:, 32:40]
            rws = rwq[:, ntq * 8:(ntq + 1) * 8]

            def vop(fn):
                lnv.inc()
                plan.add("vector", lambda e_, fn=fn: fn(e_).then_inc(lnv.h, 1))
                wait(plan, "vector", lnv, lnv.v)
            vop(lambda e_, lgs=lgs: e_.reduce_max(m1, lgs, axis=AX.X))
            vop(lambda e_, lgs=lgs: e_.tensor_single_scalar(eq1, lgs, m1, ALU.is_equal))
            vop(lambda e_, lgs=lgs: e_.scalar_tensor_tensor(mskd, eq1, -1e30, lgs, ALU.mult, ALU.add))
            vop(lambda e_: e_.reduce_max(m2, mskd, axis=AX.X))
            vop(lambda e_: e_.tensor_single_scalar(eq2, mskd, m2, ALU.is_equal))
            vop(lambda e_: e_.tensor_sub(dd, m2, m1))
            wait(plan, "scalar", lnv, lnv.v)
            lns.inc()
            plan.add("scalar", lambda e_, dd=dd, ex=ex: e_.activation(
                ex, dd, AF.Exp).then_inc(lns.h, 1))
            wait(plan, "vector", lns, lns.v)
            vop(lambda e_: e_.tensor_scalar_add(tt, ex, 1.0))
            vop(lambda e_: e_.reciprocal(w1v, tt))
            vop(lambda e_: e_.tensor_scalar(w2v, w1v, -1.0, 1.0, ALU.mult, ALU.add))
            vop(lambda e_: e_.tensor_single_scalar(t2, eq1, w1v, ALU.mult))
            vop(lambda e_, rws=rws: e_.scalar_tensor_tensor(rws, eq2, w2v, t2, ALU.mult, ALU.add))
            ctx[name + "rtv"] = lnv.v
        return lg_done

    def emit_loads(q, e, fh):
        b = (e * 2 + fh) % 2
        wait(plan, "sync", caste[e], caste[e].v)
        wait(plan, "sync", mm, buffree[b])
        wld[b].inc(16 * 3 * DC)
        for c in range(DC):
            plan.add("sync", lambda e_, e=e, fh=fh, c=c, b=b: e_.dma_start(
                w1tv[b][c][:], w1bf[e * 4096 + fh * 1024:e * 4096 + (fh + 1) * 1024,
                                   c * 128:(c + 1) * 128],
                transpose=True).then_inc(wld[b].h, 16))
            plan.add("sync", lambda e_, e=e, fh=fh, c=c, b=b: e_.dma_start(
                w1tg[b][c][:], w1bf[e * 4096 + 2048 + fh * 1024:
                                    e * 4096 + 2048 + (fh + 1) * 1024,
                                    c * 128:(c + 1) * 128],
                transpose=True).then_inc(wld[b].h, 16))
            plan.add("sync", lambda e_, e=e, fh=fh, c=c, b=b: e_.dma_start(
                w2t[b][c][:], w2bf[e * D:(e + 1) * D,
                                   fh * 1024 + c * 128:fh * 1024 + (c + 1) * 128],
                transpose=True).then_inc(wld[b].h, 16))
        return wld[b].v

    def emit_H(q, e, fh, ntw, wv, rw_done, qfirst):
        nonlocal gfs, rnd
        qb2 = q % 2
        b = (e * 2 + fh) % 2
        par = rnd % 2
        if ntw == 0:
            wait(plan, "tensor", wld[b], wv)
        if qfirst and ntw == 0:
            wait(plan, "tensor", lns, rw_done)
        for fs in range(DC):
            pv = ps[0 + 2 * (fs % 2)]
            pg = ps[1 + 2 * (fs % 2)]
            if gfs >= 2:
                wait(plan, "tensor", lnv, actm_h.get(gfs - 2, 0))
            for c in range(DC):
                plan.add("tensor", lambda e_, c=c, fs=fs, ntw=ntw, b=b, pv=pv, qb2=qb2:
                         e_.matmul(pv[:, 0:512],
                                   w1tv[b][c][0:128, fs * 128:(fs + 1) * 128],
                                   x1T[qb2][c][0:128, ntw * 512:(ntw + 1) * 512],
                                   start=(c == 0), stop=(c == DC - 1)))
            for c in range(DC):
                plan.add("tensor", lambda e_, c=c, fs=fs, ntw=ntw, b=b, pg=pg, qb2=qb2:
                         e_.matmul(pg[:, 0:512],
                                   w1tg[b][c][0:128, fs * 128:(fs + 1) * 128],
                                   x1T[qb2][c][0:128, ntw * 512:(ntw + 1) * 512],
                                   start=(c == 0), stop=(c == DC - 1)))
            hmk = pe_mark(plan, mm)
            wait(plan, "scalar", mm, hmk)
            if gfs >= 2:
                wait(plan, "scalar", lnv, actm_h.get(gfs - 2, 0))
            lns.inc()
            plan.add("scalar", lambda e_, fs=fs, pg=pg: e_.activation(
                sg[fs % 2][:], pg[:, 0:512], AF.Silu).then_inc(lns.h, 1))
            smk = lns.v
            wait(plan, "vector", lns, smk)
            if lastY[par]:
                wait(plan, "vector", mm, lastY[par])
            lnv.inc()
            plan.add("vector", lambda e_, fs=fs, pv=pv, par=par: e_.tensor_mul(
                acta[par][fs][:], pv[:, 0:512], sg[fs % 2][:]).then_inc(lnv.h, 1))
            actm_h[gfs] = lnv.v
            gfs += 1
        rnd += 1
        return par, gfs - 1

    def make_Y(q, e, fh, ntw, par, gfs_last, qfirst):
        b = (e * 2 + fh) % 2

        def emit_Y():
            nonlocal gy
            wait(plan, "tensor", lnv, actm_h[gfs_last])
            first = qfirst
            for nsub in range(4):
                tile = ntw * 4 + nsub
                pairb = (ps[4], ps[5]) if nsub % 2 == 0 else (ps[6], ps[7])
                if gy >= 2:
                    wait(plan, "tensor", lnv, yst_h.get(gy - 2, 0))
                for fs in range(DC):
                    for cc in range(2):
                        plan.add("tensor", lambda e_, fs=fs, nsub=nsub, cc=cc, b=b, par=par, yb=pairb[cc]:
                                 e_.matmul(yb[:, 0:512],
                                           acta[par][fs][0:128, nsub * 128:(nsub + 1) * 128],
                                           w2t[b][fs][0:128, cc * 512:(cc + 1) * 512],
                                           start=(fs == 0), stop=(fs == DC - 1)))
                ymk = pe_mark(plan, mm)
                wait(plan, "vector", mm, ymk)
                for cc in range(2):
                    lnv.inc()
                    if first:
                        plan.add("vector", lambda e_, tile=tile, cc=cc, e=e, yb=pairb[cc]:
                                 e_.tensor_single_scalar(
                                     yac[tile][:, cc * 512:(cc + 1) * 512],
                                     yb[:, 0:512],
                                     rwq[:, tile * 8 + e:tile * 8 + e + 1],
                                     ALU.mult).then_inc(lnv.h, 1))
                    else:
                        plan.add("vector", lambda e_, tile=tile, cc=cc, e=e, yb=pairb[cc]:
                                 e_.scalar_tensor_tensor(
                                     yac[tile][:, cc * 512:(cc + 1) * 512],
                                     yb[:, 0:512],
                                     rwq[:, tile * 8 + e:tile * 8 + e + 1],
                                     yac[tile][:, cc * 512:(cc + 1) * 512],
                                     ALU.mult, ALU.add).then_inc(lnv.h, 1))
                yst_h[gy] = lnv.v
                gy += 1
            lastY[par] = mm.v
            buffree[b] = mm.v
        return emit_Y

    def make_finalize(q):
        def one_tile(ntq):
            def emit_fin_tile():
                t = q * NTQ + ntq
                pb = t % 2
                rv = res.inc(16)
                plan.add("gpsimd", lambda e_, t=t: e_.dma_start(
                    xres[0][:], x1[t * 128:(t + 1) * 128, :]).then_inc(res.h, 16))
                wait(plan, "vector", res, rv)
                lnv.inc()
                plan.add("vector", lambda e_, ntq=ntq: e_.tensor_add(
                    yac[ntq][:], yac[ntq][:], xres[0][:]).then_inc(lnv.h, 1))
                vv = lnv.v
                wait(plan, "scalar", lnv, vv)
                if t >= 2:
                    wait(plan, "vector", stm[pb], st_of[t - 2])
                lnfin = emit_ln_epilogue(nc, plan, ctx, yac[ntq], osb[pb])
                wait(plan, "gpsimd", lnv, lnfin)
                sv = stm[pb].inc(16)
                st_of[t] = sv
                plan.add("gpsimd", lambda e_, t=t, pb=pb: e_.dma_start(
                    x2[t * 128:(t + 1) * 128, :], osb[pb][:]).then_inc(stm[pb].h, 16))
            return emit_fin_tile
        return [one_tile(ntq) for ntq in range(NTQ)]

    xv = emit_x1T_loads(0)
    xv_next = None
    for q in range(NQ):
        xv_next = emit_x1T_loads(q + 1)
        rw_done = emit_router(q, xv)
        xv = xv_next
        pending_Y = None
        ef = [(e, fh) for e in range(E) for fh in range(2)]
        if q % 2 == 1:
            ef = ef[::-1]
        fin = make_finalize(q)
        for efi, (e, fh) in enumerate(ef):
            b = (e * 2 + fh) % 2
            if ctx.get(name + f"buf{b}") != (e, fh):
                emit_loads(q, e, fh)
                ctx[name + f"buf{b}"] = (e, fh)
            wv = wld[b].v
            last_ef = (efi == len(ef) - 1)
            for ntw in range(QT // 512):
                par, gl = emit_H(q, e, fh, ntw, wv, rw_done, efi == 0)
                if pending_Y is not None:
                    pending_Y()
                if last_ef and ntw == 1 and QT // 512 > 1:
                    for ntq_ in range(NTQ // 2):
                        fin[ntq_]()
                pending_Y = make_Y(q, e, fh, ntw, par, gl, efi == 0)
        pending_Y()
        start_ = NTQ // 2 if QT // 512 > 1 else 0
        for ntq_ in range(start_, NTQ):
            fin[ntq_]()
        q_hend[q] = mm.v
    wait(plan, "gpsimd", stm[0], stm[0].v)
    wait(plan, "gpsimd", stm[1], stm[1].v)
    return


def _run_stage2(x1, Wr, W1, W2):
    nc = build_stage2()
    in_maps = [dict(x1=np.ascontiguousarray(x1[b]), wrt=np.ascontiguousarray(Wr.T),
                    w1=np.ascontiguousarray(W1.reshape(8 * 4096, D)),
                    w2=np.ascontiguousarray(W2.reshape(8 * D, 2048)))
               for b in range(x1.shape[0])]
    res = run_bass_kernel_spmd(nc, in_maps, core_ids=list(range(x1.shape[0])))
    return np.stack([r["x2"] for r in res.results], 0)


def build_stage3():
    nc = bass.Bass("TRN2", target_bir_lowering=False)
    x2 = nc.dram_tensor("x2", [T, D], F32, kind="ExternalInput").ap()
    wqkv = nc.dram_tensor("wqkv", [3 * D, D], F32, kind="ExternalInput").ap()
    wout = nc.dram_tensor("wout", [D, D], F32, kind="ExternalInput").ap()
    bout = nc.dram_tensor("bout", [D], F32, kind="ExternalInput").ap()
    x3 = nc.dram_tensor("x3", [T, D], F32, kind="ExternalOutput").ap()
    x2bf = nc.dram_tensor("x2bf", [T, D], BF16, kind="Internal").ap()
    wqkvbf = nc.dram_tensor("wqkvbf", [3 * D, D], BF16, kind="Internal").ap()
    woutbf = nc.dram_tensor("woutbf", [D, D], BF16, kind="Internal").ap()

    import contextlib
    with contextlib.ExitStack() as stack:
        plan = Plan()
        ctx = mk_common(nc, stack, plan)
        emit_stage3(nc, stack, plan, ctx, x2, x2bf, wqkv, wqkvbf, wout, woutbf,
                    bout, x3)
        plan.run(nc)
    return nc


def emit_stage3(nc, stack, plan, ctx, x2, x2bf, wqkv, wqkvbf, wout, woutbf,
                bout, x3, name="s3", ext_casts=None):
    EPS = 1e-6
    sb = lambda n, shp, dt: ctx["salloc"](name + n, shp, dt)
    mksem = ctx["mksem"]
    NS = T // 512
    kvw = [sb(f"kvw{c}", [128, 2048], BF16) for c in range(DC)]
    woT = [sb(f"wo{c}", [128, 1024], BF16) for c in range(DC)]
    kvacc = [sb(f"kva{c}", [128, 1024], F32) for c in range(DC)]
    kvbf = [sb(f"kvb{c}", [128, 1024], BF16) for c in range(DC)]
    x2Ta = [sb(f"x2a{p}", [128, 1024], BF16) for p in range(2)]
    x2Tb = [sb(f"x2b{p}", [128, 4096], BF16) for p in range(2)]
    kbf = [sb(f"kbf{i}", [128, 1024], BF16) for i in range(4)]
    vbf = [sb(f"vbf{i}", [128, 1024], BF16) for i in range(4)]
    qT = [[sb(f"qT{p}_{c}", [128, 512], BF16) for c in range(DC)] for p in range(2)]
    aT = [sb(f"aT{c}", [128, 512], BF16) for c in range(DC)]
    ksumT = sb("ksumT", [128, 8], BF16)
    zsb = sb("z", [128, 8], F32)
    bbc = sb("bbc", [128, 1024], F32)
    ysb = [sb(f"y{p}", [128, 1024], F32) for p in range(2)]
    osb = [sb(f"o{p}", [128, 1024], F32) for p in range(2)]
    xres = [sb(f"xr{p}", [128, 1024], F32) for p in range(2)]

    ps = ctx["ps"]
    mm = ctx["mm_sem"]; lnv = ctx["ln_v"]; lns = ctx["ln_s"]
    cast = mksem(name + "cast")
    wld = mksem(name + "wld")
    qld = mksem(name + "qld")
    xta = [mksem(name + f"xta{p}") for p in range(2)]
    xtb = [mksem(name + f"xtb{p}") for p in range(2)]
    res = mksem(name + "res")
    stm = [mksem(name + f"st{p}") for p in range(2)]
    bb = Sem(nc, stack, name + "bb")

    slab = T // 4
    for i in range(4):
        cast.inc(16)
        plan.add("gpsimd", lambda e, i=i, slab=slab: e.dma_start(
            x2bf[i * slab:(i + 1) * slab, :], x2[i * slab:(i + 1) * slab, :]).then_inc(cast.h, 16))
    if ext_casts is None:
        for i in range(3):
            cast.inc(16)
            plan.add("gpsimd", lambda e, i=i: e.dma_start(
                wqkvbf[i * D:(i + 1) * D, :], wqkv[i * D:(i + 1) * D, :]).then_inc(cast.h, 16))
        cast.inc(16)
        plan.add("gpsimd", lambda e: e.dma_start(woutbf, wout).then_inc(cast.h, 16))
        wsem, wval = cast, cast.v
    else:
        wsem, wval = ext_casts["c3"]
    cast_all = cast.v

    wait(plan, "sync", wsem, wval)
    for c in range(DC):
        wld.inc(32)
        plan.add("sync", lambda e, c=c: e.dma_start(
            kvw[c][0:128, 0:2048], wqkvbf[D:3 * D, c * 128:(c + 1) * 128],
            transpose=True).then_inc(wld.h, 16))
        plan.add("sync", lambda e, c=c: e.dma_start(
            woT[c][:], woutbf[:, c * 128:(c + 1) * 128], transpose=True).then_inc(wld.h, 16))
    wld_all = wld.v

    emit_bias_broadcast(nc, plan, ctx, bout, bbc, D, bb)
    bb_done = bb.v

    for c in range(DC):
        lnv.inc()
        plan.add("vector", lambda e, c=c: e.memset(kvacc[c][:], 0.0).then_inc(lnv.h, 1))
    kvz_done = lnv.v

    wait(plan, "tensor", wld, wld_all)
    wait(plan, "tensor", ctx["init_sem"], ctx["init_ones"])
    xta_free = {}
    evac_of = {}
    eps_of = {}
    kvmm_of = {}
    kvadd_of = {}
    projmm_of = {}
    NG = NT // 4
    for g in range(NG):
        for i in range(4):
            nt = 4 * g + i
            pb = nt % 2
            if nt >= 2:
                wait(plan, "sync", mm, xta_free[nt - 2])
            if nt % (NT // 4) == 0:
                wait(plan, "sync", cast, 16 * (nt // (NT // 4) + 1))
            xv = xta[pb].inc(16 * DC)
            for c in range(DC):
                plan.add("sync", lambda e, nt=nt, c=c, pb=pb: e.dma_start(
                    x2Ta[pb][0:128, c * 128:(c + 1) * 128],
                    x2bf[nt * 128:(nt + 1) * 128, c * 128:(c + 1) * 128],
                    transpose=True).then_inc(xta[pb].h, 16))
            wait(plan, "tensor", xta[pb], xv)
            if nt >= 1:
                wait(plan, "tensor", lns, evac_of[nt - 1])
            if g >= 1 and i == 0:
                wait(plan, "tensor", lnv, kvadd_of[g - 1])
            for c in range(DC):
                st_ = (c == 0)
                sp_ = (c == DC - 1)
                for tgt, col in ((ps[0], 0), (ps[1], 512), (ps[2], 1024), (ps[3], 1536)):
                    plan.add("tensor", lambda e, c=c, st_=st_, sp_=sp_, pb=pb, tgt=tgt, col=col:
                             e.matmul(tgt[:, 0:512],
                                      x2Ta[pb][0:128, c * 128:(c + 1) * 128],
                                      kvw[c][0:128, col:col + 512],
                                      start=st_, stop=sp_))
            mmk = pe_mark(plan, mm)
            xta_free[nt] = mmk
            projmm_of[nt] = mmk
            wait(plan, "scalar", mm, mmk)
            if g >= 1:
                wait(plan, "scalar", mm, kvmm_of[g - 1])
            for cc in range(2):
                lns.inc()
                plan.add("scalar", lambda e, cc=cc, i=i: e.activation(
                    kbf[i][:, cc * 512:cc * 512 + 512], ps[0 + cc][:, 0:512],
                    AF.Relu).then_inc(lns.h, 1))
                lns.inc()
                plan.add("scalar", lambda e, cc=cc, i=i: e.copy(
                    vbf[i][:, cc * 512:cc * 512 + 512], ps[2 + cc][:, 0:512]).then_inc(lns.h, 1))
            evac_of[nt] = lns.v
            wait(plan, "vector", lns, evac_of[nt])
            lnv.inc()
            plan.add("vector", lambda e, i=i: e.tensor_scalar_add(
                kbf[i][:], kbf[i][:], EPS).then_inc(lnv.h, 1))
            eps_of[nt] = lnv.v
        wait(plan, "tensor", lnv, eps_of[4 * g + 3])
        if g == 0:
            wait(plan, "tensor", lnv, kvz_done)
        kvadd_prev = kvadd_of.get(g - 1, 0)
        add_targets = []
        for dc_ in range(DC):
            pair = (ps[4], ps[5]) if dc_ % 2 == 0 else (ps[2], ps[3])
            if dc_ >= 2:
                wait(plan, "tensor", lnv, add_targets[dc_ - 2])
            for i in range(4):
                nt = 4 * g + i
                st_ = (i == 0)
                sp_ = (i == 3)
                for eh in range(2):
                    plan.add("tensor", lambda e, dc_=dc_, i=i, eh=eh, st_=st_, sp_=sp_, pair=pair:
                             e.matmul(pair[eh][:, 0:512],
                                      kbf[i][0:128, dc_ * 128:(dc_ + 1) * 128],
                                      vbf[i][0:128, eh * 512:eh * 512 + 512],
                                      start=st_, stop=sp_))
                plan.add("tensor", lambda e, dc_=dc_, i=i, g=g, st_=st_, sp_=sp_:
                         e.matmul(ps[7][0:128, dc_:dc_ + 1],
                                  kbf[i][0:128, dc_ * 128:(dc_ + 1) * 128],
                                  ctx["ones128"][0:128, 0:1],
                                  start=(g == 0 and i == 0), stop=(g == NG - 1 and i == 3)))
            mmv = pe_mark(plan, mm)
            wait(plan, "vector", mm, mmv)
            for eh in range(2):
                lnv.inc()
                plan.add("vector", lambda e, dc_=dc_, eh=eh, pair=pair: e.tensor_add(
                    kvacc[dc_][:, eh * 512:eh * 512 + 512],
                    kvacc[dc_][:, eh * 512:eh * 512 + 512],
                    pair[eh][:, 0:512]).then_inc(lnv.h, 1))
            add_targets.append(lnv.v)
        kvmm_of[g] = mm.v
        kvadd_of[g] = lnv.v

    wait(plan, "scalar", lnv, kvadd_of[NG - 1])
    wait(plan, "scalar", mm, mm.v)
    for c in range(DC):
        lns.inc()
        plan.add("scalar", lambda e, c=c: e.copy(kvbf[c][:], kvacc[c][:]).then_inc(lns.h, 1))
    lns.inc()
    plan.add("scalar", lambda e: e.copy(ksumT[:], ps[7][0:128, 0:8]).then_inc(lns.h, 1))
    kvbf_done = lns.v

    wait(plan, "sync", mm, kvmm_of[NG - 1])
    for c in range(DC):
        qld.inc(16)
        plan.add("sync", lambda e, c=c: e.dma_start(
            kvw[c][0:128, 0:1024], wqkvbf[0:D, c * 128:(c + 1) * 128],
            transpose=True).then_inc(qld.h, 16))
    qld_all = qld.v

    wait(plan, "tensor", qld, qld_all)
    wait(plan, "tensor", lns, kvbf_done)
    qproj_free = {}
    ln_of = {}
    st_of = {}
    tile_idx = 0
    z_done_of = {}
    qready_of = {}
    bprojmm = {0: 0}

    def emit_A(ns):
        if ns >= NS:
            return
        pb = ns % 2
        if ns >= 2:
            wait(plan, "sync", mm, qproj_free[ns - 2])
        xv = xtb[pb].inc(16 * DC)
        for c in range(DC):
            plan.add("sync", lambda e, ns=ns, c=c, pb=pb: e.dma_start(
                x2Tb[pb][0:128, c * 512:(c + 1) * 512],
                x2bf[ns * 512:(ns + 1) * 512, c * 128:(c + 1) * 128],
                transpose=True).then_inc(xtb[pb].h, 16))
        wait(plan, "tensor", xtb[pb], xv)
        wait(plan, "tensor", lns, ctx.get(name + "qrelu", 0))
        qevac = {}
        for dq in range(DC):
            bank = ps[dq % 2]
            if dq >= 2:
                wait(plan, "tensor", lnv, qevac[dq - 2])
            for c in range(DC):
                plan.add("tensor", lambda e, dq=dq, c=c, pb=pb, bank=bank:
                         e.matmul(bank[:, 0:512],
                                  kvw[c][0:128, dq * 128:(dq + 1) * 128],
                                  x2Tb[pb][0:128, c * 512:(c + 1) * 512],
                                  start=(c == 0), stop=(c == DC - 1)))
            mmq = pe_mark(plan, mm)
            wait(plan, "scalar", mm, mmq)
            wait(plan, "scalar", mm, bprojmm.get(ns - 1, 0))
            lns.inc()
            plan.add("scalar", lambda e, dq=dq, pb=pb, bank=bank: e.activation(
                qT[pb][dq][:], bank[:, 0:512], AF.Relu).then_inc(lns.h, 1))
            wait(plan, "vector", lns, lns.v)
            lnv.inc()
            plan.add("vector", lambda e, dq=dq, pb=pb: e.tensor_scalar_add(
                qT[pb][dq][:], qT[pb][dq][:], EPS).then_inc(lnv.h, 1))
            qevac[dq] = lnv.v
        ctx[name + "qrelu"] = lns.v
        qproj_free[ns] = mm.v
        qready_of[ns] = lnv.v

    def emit_B(ns):
        nonlocal tile_idx
        pb = ns % 2
        wait(plan, "tensor", lnv, qready_of[ns])
        wait(plan, "tensor", lnv, ln_of.get(tile_idx - 1, 0))
        for nsub in range(4):
            for c in range(DC):
                plan.add("tensor", lambda e, nsub=nsub, c=c, pb=pb:
                         e.matmul(ps[2][0:128, nsub:nsub + 1],
                                  qT[pb][c][0:128, nsub * 128:(nsub + 1) * 128],
                                  ksumT[0:128, c:c + 1],
                                  start=(c == 0), stop=(c == DC - 1)))
        mmz = pe_mark(plan, mm)
        wait(plan, "vector", mm, mmz)
        wait(plan, "vector", lnv, lnv.v)
        lnv.inc()
        plan.add("vector", lambda e: e.tensor_scalar_add(
            zsb[:, 0:4], ps[2][0:128, 0:4], EPS).then_inc(lnv.h, 1))
        wait(plan, "vector", lnv, lnv.v)
        lnv.inc()
        plan.add("vector", lambda e: e.reciprocal(
            zsb[:, 4:8], zsb[:, 0:4]).then_inc(lnv.h, 1))
        z_done = lnv.v
        z_done_of[ns] = z_done
        aevac = {}
        for et in range(DC):
            bank = ps[3 + et % 2]
            if et >= 2:
                wait(plan, "tensor", lns, aevac[et - 2])
            for c in range(DC):
                plan.add("tensor", lambda e, et=et, c=c, pb=pb, bank=bank:
                         e.matmul(bank[:, 0:512],
                                  kvbf[c][0:128, et * 128:(et + 1) * 128],
                                  qT[pb][c][0:128, 0:512],
                                  start=(c == 0), stop=(c == DC - 1)))
            mma = pe_mark(plan, mm)
            wait(plan, "scalar", mm, mma)
            wait(plan, "scalar", mm, bprojmm.get(ns, 0))
            lns.inc()
            plan.add("scalar", lambda e, et=et, bank=bank: e.copy(
                aT[et][:], bank[:, 0:512]).then_inc(lns.h, 1))
            aevac[et] = lns.v
        a_ready = lns.v
        wait(plan, "tensor", lns, a_ready)
        for nsub in range(4):
            pb2 = tile_idx % 2
            pbanks = (ps[5], ps[6]) if nsub % 2 == 0 else (ps[7], ps[2])
            if nsub % 2 == 1:
                wait(plan, "tensor", lnv, z_done)
            if tile_idx >= 2:
                wait(plan, "tensor", lnv, ln_of[tile_idx - 2])
            for et in range(DC):
                st_ = (et == 0)
                sp_ = (et == DC - 1)
                for cc in range(2):
                    plan.add("tensor", lambda e, nsub=nsub, cc=cc, et=et, st_=st_, sp_=sp_, pbanks=pbanks:
                             e.matmul(pbanks[cc][:, 0:512],
                                      aT[et][0:128, nsub * 128:(nsub + 1) * 128],
                                      woT[et][0:128, cc * 512:cc * 512 + 512],
                                      start=st_, stop=sp_))
            mmp = pe_mark(plan, mm)
            rv = res.inc(16)
            plan.add("gpsimd", lambda e, ns=ns, nsub=nsub, pb2=pb2: e.dma_start(
                xres[pb2][:], x2[ns * 512 + nsub * 128:ns * 512 + (nsub + 1) * 128, :]
            ).then_inc(res.h, 16))
            wait(plan, "vector", mm, mmp)
            wait(plan, "vector", bb, bb_done)
            wait(plan, "vector", lnv, z_done)
            wait(plan, "vector", res, rv)
            for cc in range(2):
                lnv.inc()
                plan.add("vector", lambda e, cc=cc, nsub=nsub, pb2=pb2, pbanks=pbanks:
                         e.scalar_tensor_tensor(
                             ysb[pb2][:, cc * 512:cc * 512 + 512], pbanks[cc][:, 0:512],
                             zsb[:, 4 + nsub:5 + nsub], bbc[:, cc * 512:cc * 512 + 512],
                             ALU.mult, ALU.add).then_inc(lnv.h, 1))
            wait(plan, "vector", lnv, lnv.v)
            vv = lnv.inc()
            plan.add("vector", lambda e, pb2=pb2: e.tensor_add(
                ysb[pb2][:], ysb[pb2][:], xres[pb2][:]).then_inc(lnv.h, 1))
            wait(plan, "scalar", lnv, vv)
            if tile_idx >= 2:
                wait(plan, "vector", stm[pb2], st_of[tile_idx - 2])
            lnfin = emit_ln_epilogue(nc, plan, ctx, ysb[pb2], osb[pb2])
            ln_of[tile_idx] = lnfin
            wait(plan, "gpsimd", lnv, lnfin)
            sv = stm[pb2].inc(16)
            st_of[tile_idx] = sv
            plan.add("gpsimd", lambda e, ns=ns, nsub=nsub, pb2=pb2: e.dma_start(
                x3[ns * 512 + nsub * 128:ns * 512 + (nsub + 1) * 128, :],
                osb[pb2][:]).then_inc(stm[pb2].h, 16))
            tile_idx += 1
        bprojmm[ns + 1] = mm.v

    emit_A(0)
    for ns in range(NS):
        emit_A(ns + 1)
        emit_B(ns)
    wait(plan, "gpsimd", stm[0], stm[0].v)
    wait(plan, "gpsimd", stm[1], stm[1].v)
    return


    wait(plan, "gpsimd", stm[1], stm[1].v)
    return


def _dft_consts():
    j = np.arange(D)
    ang = 2.0 * np.pi * np.outer(j, j) / D
    fr = np.cos(ang).astype(np.float32)
    fi = (-np.sin(ang)).astype(np.float32)
    gr = (np.cos(ang) / D).astype(np.float32)
    gi = (np.sin(ang) / D).astype(np.float32)
    return fr, fi, gr, gi


def _run_stage4(x3, wfr, wfi, bias):
    fr, fi, gr, gi = _dft_consts()
    nc = build_stage4()
    in_maps = []
    for b in range(x3.shape[0]):
        in_maps.append(dict(x3=np.ascontiguousarray(x3[b]), wfr=wfr, wfi=wfi,
                            bias=bias, fr=fr, fi=fi, fineg=-fi, gr=gr, gineg=-gi))
    res = run_bass_kernel_spmd(nc, in_maps, core_ids=list(range(x3.shape[0])))
    return np.stack([r["out"] for r in res.results], 0)


def _mk_masks():
    import ml_dtypes
    m = np.zeros((128, 2048), dtype=np.float32)
    for ri, r in enumerate((0, 128, 256, 384)):
        k = np.arange(128)[:, None]
        q = np.arange(512)[None, :]
        m[:, ri * 512:(ri + 1) * 512] = ((k + r) <= q).astype(np.float32)
    return m.astype(ml_dtypes.bfloat16)


def _run_stage1(x, wqkv, wproj, bproj):
    nc = build_stage1()
    masks = _mk_masks()
    in_maps = [dict(x=np.ascontiguousarray(x[b]), wqkv=wqkv, wproj=wproj,
                    bproj=bproj, masks=masks) for b in range(x.shape[0])]
    res = run_bass_kernel_spmd(nc, in_maps, core_ids=list(range(x.shape[0])))
    return np.stack([r["x1"] for r in res.results], 0)


def _run_stage3(x2, wqkv, wout, bout):
    nc = build_stage3()
    in_maps = [dict(x2=np.ascontiguousarray(x2[b]), wqkv=wqkv, wout=wout, bout=bout)
               for b in range(x2.shape[0])]
    res = run_bass_kernel_spmd(nc, in_maps, core_ids=list(range(x2.shape[0])))
    return np.stack([r["x3"] for r in res.results], 0)


POOL_BYTES = 201728


def build_fused():
    nc = bass.Bass("TRN2", target_bir_lowering=False)
    x = nc.dram_tensor("x", [T, D], F32, kind="ExternalInput").ap()
    wqkv0 = nc.dram_tensor("wqkv0", [D, 3 * D], F32, kind="ExternalInput").ap()
    wproj0 = nc.dram_tensor("wproj0", [D, D], F32, kind="ExternalInput").ap()
    bproj0 = nc.dram_tensor("bproj0", [D], F32, kind="ExternalInput").ap()
    masks = nc.dram_tensor("masks", [128, 2048], BF16, kind="ExternalInput").ap()
    wrt = nc.dram_tensor("wrt", [D, 8], F32, kind="ExternalInput").ap()
    w1 = nc.dram_tensor("w1", [8 * 4096, D], F32, kind="ExternalInput").ap()
    w2 = nc.dram_tensor("w2", [8 * D, 2048], F32, kind="ExternalInput").ap()
    wqkv1 = nc.dram_tensor("wqkv1", [3 * D, D], F32, kind="ExternalInput").ap()
    wout1 = nc.dram_tensor("wout1", [D, D], F32, kind="ExternalInput").ap()
    bout1 = nc.dram_tensor("bout1", [D], F32, kind="ExternalInput").ap()
    wfr = nc.dram_tensor("wfr", [D, D], F32, kind="ExternalInput").ap()
    wfi = nc.dram_tensor("wfi", [D, D], F32, kind="ExternalInput").ap()
    fbias = nc.dram_tensor("fbias", [D], F32, kind="ExternalInput").ap()
    fr = nc.dram_tensor("fr", [D, D], F32, kind="ExternalInput").ap()
    fi = nc.dram_tensor("fi", [D, D], F32, kind="ExternalInput").ap()
    fineg = nc.dram_tensor("fineg", [D, D], F32, kind="ExternalInput").ap()
    gr = nc.dram_tensor("gr", [D, D], F32, kind="ExternalInput").ap()
    gineg = nc.dram_tensor("gineg", [D, D], F32, kind="ExternalInput").ap()
    out = nc.dram_tensor("out", [T, D], F32, kind="ExternalOutput").ap()
    x1 = nc.dram_tensor("x1", [T, D], F32, kind="Internal").ap()
    x2 = nc.dram_tensor("x2", [T, D], F32, kind="Internal").ap()
    x3 = nc.dram_tensor("x3", [T, D], F32, kind="Internal").ap()
    xbf = nc.dram_tensor("xbf", [T, D], BF16, kind="Internal").ap()
    ahbm = nc.dram_tensor("ahbm", [D, T], BF16, kind="Internal").ap()
    x1bf = nc.dram_tensor("x1bf", [T, D], BF16, kind="Internal").ap()
    w1bf = nc.dram_tensor("w1bf", [8 * 4096, D], BF16, kind="Internal").ap()
    w2bf = nc.dram_tensor("w2bf", [8 * D, 2048], BF16, kind="Internal").ap()
    x2bf = nc.dram_tensor("x2bf", [T, D], BF16, kind="Internal").ap()
    wqkvbf = nc.dram_tensor("wqkvbf", [3 * D, D], BF16, kind="Internal").ap()
    woutbf = nc.dram_tensor("woutbf", [D, D], BF16, kind="Internal").ap()
    x3bf = nc.dram_tensor("x3bf", [T, D], BF16, kind="Internal").ap()

    import contextlib
    with contextlib.ExitStack() as stack:
        pool = stack.enter_context(nc.sbuf_tensor("pool", [128, POOL_BYTES], mybir.dt.uint8))
        plan = Plan()
        ctx = mk_common(nc, stack, plan, pool=pool)
        ext = emit_weight_precasts(nc, plan, ctx, w1, w2, w1bf, w2bf,
                                   wqkv1, wqkvbf, wout1, woutbf)
        ctx["precast_chunks"] = list(ext["chunks"])
        emit_stage1(nc, stack, plan, ctx, x, xbf, wqkv0, wproj0, bproj0,
                    masks, ahbm, x1)
        assert not ctx["precast_chunks"]
        emit_barrier(nc, plan, ctx)
        ctx["pool_reset"]()
        emit_stage2(nc, stack, plan, ctx, x1, wrt, w1, w2, x1bf, w1bf, w2bf,
                    x2, ext_casts=ext)
        emit_barrier(nc, plan, ctx)
        ctx["pool_reset"]()
        emit_stage3(nc, stack, plan, ctx, x2, x2bf, wqkv1, wqkvbf, wout1,
                    woutbf, bout1, x3, ext_casts=ext)
        emit_barrier(nc, plan, ctx)
        ctx["pool_reset"]()
        emit_stage4(nc, stack, plan, ctx, x3, x3bf, wfr, wfi, fbias,
                    fr, fi, fineg, gr, gineg, out)
        plan.run(nc)
    return nc


def kernel(L0_attn_x, L0_attn_w_qkv, L0_attn_w_proj, L0_attn_b_proj,
           L0_ffn_Wr, L0_ffn_W1, L0_ffn_W2,
           L1_attn_W_qkv, L1_attn_W_out, L1_attn_b_out,
           L1_ffn_w_freq_real, L1_ffn_w_freq_imag, L1_ffn_bias):
    B = L0_attn_x.shape[0]
    cores = list(range(B))
    x = np.ascontiguousarray(L0_attn_x, dtype=np.float32)
    masks = _mk_masks()
    fr, fi, gr, gi = _dft_consts()
    w1r = np.ascontiguousarray(L0_ffn_W1.reshape(8 * 4096, D))
    w2r = np.ascontiguousarray(L0_ffn_W2.reshape(8 * D, 2048))
    wrt = np.ascontiguousarray(L0_ffn_Wr.T)

    nc = build_fused()
    im = [dict(x=np.ascontiguousarray(x[b]), wqkv0=L0_attn_w_qkv,
               wproj0=L0_attn_w_proj, bproj0=L0_attn_b_proj, masks=masks,
               wrt=wrt, w1=w1r, w2=w2r,
               wqkv1=L1_attn_W_qkv, wout1=L1_attn_W_out, bout1=L1_attn_b_out,
               wfr=L1_ffn_w_freq_real, wfi=L1_ffn_w_freq_imag,
               fbias=L1_ffn_bias, fr=fr, fi=fi, fineg=-fi, gr=gr, gineg=-gi)
          for b in cores]
    res = run_bass_kernel_spmd(nc, im, core_ids=cores)
    out = np.stack([res.results[b]["out"] for b in cores], 0)
    return out.astype(np.float32)


def kernel_unfused(L0_attn_x, L0_attn_w_qkv, L0_attn_w_proj, L0_attn_b_proj,
           L0_ffn_Wr, L0_ffn_W1, L0_ffn_W2,
           L1_attn_W_qkv, L1_attn_W_out, L1_attn_b_out,
           L1_ffn_w_freq_real, L1_ffn_w_freq_imag, L1_ffn_bias):
    B = L0_attn_x.shape[0]
    cores = list(range(B))
    x = np.ascontiguousarray(L0_attn_x, dtype=np.float32)
    masks = _mk_masks()
    fr, fi, gr, gi = _dft_consts()

    nc1 = build_stage1()
    im = [dict(x=np.ascontiguousarray(x[b]), wqkv=L0_attn_w_qkv,
               wproj=L0_attn_w_proj, bproj=L0_attn_b_proj, masks=masks)
          for b in cores]
    r1 = run_bass_kernel_spmd(nc1, im, core_ids=cores)
    x1 = [r1.results[b]["x1"] for b in cores]

    nc2 = build_stage2()
    w1r = np.ascontiguousarray(L0_ffn_W1.reshape(8 * 4096, D))
    w2r = np.ascontiguousarray(L0_ffn_W2.reshape(8 * D, 2048))
    wrt = np.ascontiguousarray(L0_ffn_Wr.T)
    im = [dict(x1=x1[b], wrt=wrt, w1=w1r, w2=w2r) for b in cores]
    r2 = run_bass_kernel_spmd(nc2, im, core_ids=cores)
    x2 = [r2.results[b]["x2"] for b in cores]

    nc3 = build_stage3()
    im = [dict(x2=x2[b], wqkv=L1_attn_W_qkv, wout=L1_attn_W_out,
               bout=L1_attn_b_out) for b in cores]
    r3 = run_bass_kernel_spmd(nc3, im, core_ids=cores)
    x3 = [r3.results[b]["x3"] for b in cores]

    nc4 = build_stage4()
    im = [dict(x3=x3[b], wfr=L1_ffn_w_freq_real, wfi=L1_ffn_w_freq_imag,
               bias=L1_ffn_bias, fr=fr, fi=fi, fineg=-fi, gr=gr, gineg=-gi)
          for b in cores]
    r4 = run_bass_kernel_spmd(nc4, im, core_ids=cores)
    out = np.stack([r4.results[b]["out"] for b in cores], 0)
    return out.astype(np.float32)


# version 21
# speedup vs baseline: 1.1677x; 1.0706x over previous
import math
import numpy as np
import concourse.bass as bass
import concourse.mybir as mybir
from concourse.bass_utils import run_bass_kernel_spmd

F32 = mybir.dt.float32
BF16 = mybir.dt.bfloat16
AF = mybir.ActivationFunctionType
ALU = mybir.AluOpType
AX = mybir.AxisListType

T = 4096
D = 1024
NT = T // 128
DC = D // 128


class Sem:
    def __init__(self, nc, stack, name):
        self.h = stack.enter_context(nc.semaphore(name))
        self.v = 0

    def inc(self, d=1):
        self.v += d
        return self.v


class Plan:
    ENGINES = ("tensor", "scalar", "vector", "gpsimd", "sync")

    def __init__(self):
        self.q = {e: [] for e in self.ENGINES}

    def add(self, eng, fn):
        self.q[eng].append(fn)

    def run(self, nc):
        with nc.Block() as block:
            @block.tensor
            def _(e):
                for f in self.q["tensor"]:
                    f(e)

            @block.scalar
            def _(e):
                for f in self.q["scalar"]:
                    f(e)

            @block.vector
            def _(e):
                for f in self.q["vector"]:
                    f(e)

            @block.gpsimd
            def _(e):
                for f in self.q["gpsimd"]:
                    f(e)

            @block.sync
            def _(e):
                for f in self.q["sync"]:
                    f(e)


def wait(plan, eng, sem, val):
    plan.add(eng, lambda e, s=sem.h, v=val: e.wait_ge(s, v))


def _ap(x):
    a = getattr(x, "ap", None)
    return x.ap() if callable(a) else x


def pe_mark(plan, mm):
    v = mm.inc()
    last = plan.q["tensor"].pop()
    plan.add("tensor", lambda e, f=last, h=mm.h: f(e).then_inc(h, 1))
    return v


def emit_derive_xT(nc, plan, ctx, x_f32_hbm, xbf_hbm, xT_tiles, done_sem,
                   dma_eng="gpsimd", tr_eng="sync", pre_wait=None):
    cast = ctx["mksem"]("cast")
    x_ap = _ap(x_f32_hbm)
    xbf_ap = _ap(xbf_hbm)
    if pre_wait is not None:
        wait(plan, dma_eng, *pre_wait)
    slab = T // 4
    for i in range(4):
        v = cast.inc(16)
        plan.add(dma_eng, lambda e, i=i, v=v, slab=slab: e.dma_start(
            xbf_ap[i * slab:(i + 1) * slab, :], x_ap[i * slab:(i + 1) * slab, :]
        ).then_inc(cast.h, 16))
    cast_done = cast.v
    wait(plan, tr_eng, cast, cast_done)
    vals = []
    for c in range(DC):
        v = done_sem.inc(16)
        plan.add(tr_eng, lambda e, c=c, v=v: e.dma_start(
            xT_tiles[c][0:128, 0:T], xbf_ap[:, c * 128:(c + 1) * 128], transpose=True
        ).then_inc(done_sem.h, 16))
        vals.append(v)
    return vals


def emit_bias_broadcast(nc, plan, ctx, b_hbm, bbc_sb, ncols, ready_sem):
    brow = ctx["brow_sb"]
    ones1 = ctx["ones1"]
    ld = ctx["mksem"]("brow")
    mm = ctx["mm_sem"]
    v = ld.inc(16)
    plan.add("gpsimd", lambda e, v=v: e.dma_start(
        brow[0:1, 0:ncols], _ap(b_hbm).rearrange("(o d) -> o d", o=1)).then_inc(ld.h, 16))
    wait(plan, "tensor", ld, v)
    wait(plan, "tensor", ctx["init_sem"], ctx["init_ones"])
    ps = ctx["ps"]
    for j in range(ncols // 512):
        plan.add("tensor", lambda e, j=j: e.matmul(
            ps[4][:, 0:512] if j == 0 else ps[5][:, 0:512],
            ones1[0:1, :], brow[0:1, j * 512:(j + 1) * 512],
            start=True, stop=True))
    pe_mark(plan, mm)
    wait(plan, "scalar", mm, mm.v)
    for j in range(ncols // 512):
        rv = ready_sem.inc()
        plan.add("scalar", lambda e, j=j, rv=rv: e.copy(
            bbc_sb[:, j * 512:(j + 1) * 512],
            (ctx["ps"][4] if j == 0 else ctx["ps"][5])[:, 0:512]
        ).then_inc(ready_sem.h, 1))


def emit_ln_epilogue(nc, plan, ctx, y_sb, x1_sb, eps=1e-5):
    lnv = ctx["ln_v"]; lns = ctx["ln_s"]
    par = ctx["ln_par"][0]
    ctx["ln_par"][0] = 1 - par
    st = ctx["ln_st"][par]
    mean = st[:, 12:13]; var = st[:, 13:14]
    lnt = st[:, 14:15]; rs = st[:, 15:16]; nmrs = st[:, 16:17]
    ps_ = ctx.get(f"ln_prev_s{par}", 0)
    pv_ = ctx.get(f"ln_prev_f{par}", 0)
    if ps_:
        wait(plan, "vector", lns, ps_)
    wait(plan, "vector", lnv, max(pv_, lnv.v))
    for half in range(2):
        lnv.inc()
        plan.add("vector", lambda e, half=half: e.bn_stats(
            st[:, 6 * half:6 * half + 6],
            y_sb[:, 512 * half:512 * half + 512]).then_inc(lnv.h, 1))
    wait(plan, "vector", lnv, lnv.v)
    v2 = lnv.inc()
    plan.add("vector", lambda e: e.bn_aggr(st[:, 12:14], st[:, 0:12]).then_inc(lnv.h, 1))
    wait(plan, "scalar", lnv, v2)
    v2s = lns.inc()
    plan.add("scalar", lambda e: e.activation(lnt, var, AF.Ln, bias=ctx["eps_ln"][:]).then_inc(lns.h, 1))
    wait(plan, "scalar", lns, v2s)
    v3 = lns.inc()
    plan.add("scalar", lambda e: e.activation(rs, lnt, AF.Exp, scale=-0.5).then_inc(lns.h, 1))
    wait(plan, "vector", lns, v3)
    vd = lnv.inc()
    plan.add("vector", lambda e: e.scalar_tensor_tensor(
        nmrs, mean, -1.0, rs, ALU.mult, ALU.mult).then_inc(lnv.h, 1))
    wait(plan, "vector", lnv, vd)
    vf = lnv.inc()
    plan.add("vector", lambda e: e.tensor_scalar(
        x1_sb[:], y_sb[:], rs, nmrs, ALU.mult, ALU.add).then_inc(lnv.h, 1))
    ctx[f"ln_prev_s{par}"] = v3
    ctx[f"ln_prev_f{par}"] = vf
    return vf


def mk_common(nc, stack, plan, pool=None):
    ctx = {}
    ctx["_pool"] = pool
    ctx["_off"] = [0]
    ctx["dma_list"] = []

    def salloc(name_, shape, dtype):
        if pool is None:
            return stack.enter_context(nc.sbuf_tensor(name_, shape, dtype))
        nbytes = shape[1] * mybir.dt.size(dtype)
        nbytes = (nbytes + 63) // 64 * 64
        off = ctx["_off"][0]
        ctx["_off"][0] = off + nbytes
        assert ctx["_off"][0] <= pool.shape[1], (
            f"pool overflow: {name_} needs {nbytes}B at {off}, pool {pool.shape[1]}")
        return pool[0:shape[0], off:off + shape[1] * mybir.dt.size(dtype)].bitcast(dtype)

    ctx["salloc"] = salloc
    ctx["pool_reset"] = lambda: ctx["_off"].__setitem__(0, 0)
    ctx["psB"] = [stack.enter_context(nc.psum_tensor(f"psB{i}", [128, 1024], F32))
                  for i in range(4)]
    ctx["ps"] = [ctx["psB"][i // 2][:, (i % 2) * 512:(i % 2) * 512 + 512]
                 for i in range(8)]
    ctx["ones1"] = stack.enter_context(nc.sbuf_tensor("ones1", [1, 128], BF16))
    ctx["ones128"] = stack.enter_context(nc.sbuf_tensor("ones128", [128, 1], BF16))
    ctx["brow_sb"] = stack.enter_context(nc.sbuf_tensor("brow", [1, 1024], BF16))
    ctx["ln_st"] = [stack.enter_context(nc.sbuf_tensor(f"ln_st{i}", [128, 18], F32)) for i in range(2)]
    ctx["ln_par"] = [0]
    ctx["eps_ln"] = stack.enter_context(nc.sbuf_tensor("eps_ln", [128, 1], F32))
    for nm_ in ("ld_sem", "mm_sem", "ln_v", "ln_s", "init_sem"):
        ctx[nm_] = Sem(nc, stack, nm_)
    _n = [0]
    def _mksem(tag="s"):
        _n[0] += 1
        s = Sem(nc, stack, f"{tag}{_n[0]}")
        ctx["dma_list"].append(s)
        return s
    ctx["mksem"] = _mksem
    def _mksem_nobar(tag="s"):
        _n[0] += 1
        return Sem(nc, stack, f"{tag}{_n[0]}")
    ctx["mksem_nobar"] = _mksem_nobar
    iv = ctx["init_sem"].inc(1)
    plan.add("vector", lambda e: e.memset(ctx["ones1"][:], 1.0))
    plan.add("vector", lambda e: e.memset(ctx["ones128"][:], 1.0))
    plan.add("vector", lambda e, v=iv: e.memset(ctx["eps_ln"][:], 1e-5).then_inc(ctx["init_sem"].h, 1))
    ctx["init_ones"] = iv
    return ctx


def build_stage4():
    nc = bass.Bass("TRN2", target_bir_lowering=False)
    x3 = nc.dram_tensor("x3", [T, D], F32, kind="ExternalInput").ap()
    wfr = nc.dram_tensor("wfr", [D, D], F32, kind="ExternalInput").ap()
    wfi = nc.dram_tensor("wfi", [D, D], F32, kind="ExternalInput").ap()
    bias = nc.dram_tensor("bias", [D], F32, kind="ExternalInput").ap()
    fr = nc.dram_tensor("fr", [D, D], F32, kind="ExternalInput").ap()
    fi = nc.dram_tensor("fi", [D, D], F32, kind="ExternalInput").ap()
    fineg = nc.dram_tensor("fineg", [D, D], F32, kind="ExternalInput").ap()
    gr = nc.dram_tensor("gr", [D, D], F32, kind="ExternalInput").ap()
    gineg = nc.dram_tensor("gineg", [D, D], F32, kind="ExternalInput").ap()
    out = nc.dram_tensor("out", [T, D], F32, kind="ExternalOutput").ap()
    x3bf = nc.dram_tensor("x3bf", [T, D], BF16, kind="Internal").ap()

    import contextlib
    with contextlib.ExitStack() as stack:
        plan = Plan()
        ctx = mk_common(nc, stack, plan)
        emit_stage4(nc, stack, plan, ctx, x3, x3bf, wfr, wfi, bias,
                    fr, fi, fineg, gr, gineg, out)
        plan.run(nc)
    return nc


def emit_stage4(nc, stack, plan, ctx, x3, x3bf, wfr, wfi, bias,
                fr, fi, fineg, gr, gineg, out, name="s4"):
    sb = lambda n, shp, dt: ctx["salloc"](name + n, shp, dt)
    cA = [sb(f"cA{j}", [128, 4096], BF16) for j in range(DC)]
    wA = [sb(f"wA{j}", [128, 2048], BF16) for j in range(DC)]
    aT = [sb(f"aT{j}", [128, 2048], BF16) for j in range(DC)]
    mS = [sb(f"mS{j}", [128, 1024], BF16) for j in range(DC)]
    bbc = sb("bbc", [128, 1024], F32)
    ysb = [sb(f"y{j}", [128, 1024], F32) for j in range(2)]
    osb = [sb(f"o{j}", [128, 1024], F32) for j in range(2)]
    xres = [sb(f"xr{j}", [128, 1024], F32) for j in range(2)]

    mm = ctx["mm_sem"]
    ld = ctx["mksem"](name + "ld")
    gld = ctx["mksem"](name + "gld")
    stm = [ctx["mksem"](name + "stA"), ctx["mksem"](name + "stB")]
    ev = Sem(nc, stack, name + "ev")
    xt = Sem(nc, stack, name + "xt")
    bb = Sem(nc, stack, name + "bb")

    def cast_load(dst_tiles, src_ap, col0, sem, eng="gpsimd"):
        for j in range(DC):
            sem.inc(16)
            plan.add(eng, lambda e, j=j: e.dma_start(
                dst_tiles[j][:, col0:col0 + D], src_ap[j * 128:(j + 1) * 128, :]
            ).then_inc(sem.h, 16))

    cast_load(cA, fr, 0, ld)
    cast_load(cA, fi, 1024, ld)
    cast_load(cA, fineg, 2048, ld)
    cast_load(wA, wfr, 0, ld)
    cast_load(wA, wfi, 1024, ld)
    consts_done = ld.v

    emit_bias_broadcast(nc, plan, ctx, bias, bbc, D, bb)
    bb_done = bb.v

    wait(plan, "tensor", ld, consts_done)
    wait(plan, "tensor", ctx["init_sem"], ctx["init_ones"])
    s1_ev = []
    for kt in range(DC):
        for half, (t0, t1) in enumerate(((0, 2048), (1024, 0))):
            for dch in range(2):
                bank = ctx["ps"][(kt * 4 + half * 2 + dch) % 4]
                for j in range(DC):
                    if half == 0:
                        ops = [(wA[j][:, kt * 128:kt * 128 + 128], cA[j][:, dch * 512:dch * 512 + 512]),
                               (wA[j][:, 1024 + kt * 128:1024 + kt * 128 + 128], cA[j][:, 2048 + dch * 512:2048 + dch * 512 + 512])]
                    else:
                        ops = [(wA[j][:, 1024 + kt * 128:1024 + kt * 128 + 128], cA[j][:, dch * 512:dch * 512 + 512]),
                               (wA[j][:, kt * 128:kt * 128 + 128], cA[j][:, 1024 + dch * 512:1024 + dch * 512 + 512])]
                    for oi, (l_, r_) in enumerate(ops):
                        st_ = (j == 0 and oi == 0)
                        sp_ = (j == DC - 1 and oi == 1)
                        plan.add("tensor", lambda e, b=bank, l_=l_, r_=r_, st_=st_, sp_=sp_:
                                 e.matmul(b[:, 0:512], l_, r_, start=st_, stop=sp_))
                pe_mark(plan, mm)
                col0 = half * 1024 + dch * 512
                wait(plan, "scalar", mm, mm.v)
                v = ev.inc()
                plan.add("scalar", lambda e, kt=kt, col0=col0, b=bank, v=v:
                         e.copy(aT[kt][:, col0:col0 + 512], b[:, 0:512]).then_inc(ev.h, 1))
                s1_ev.append(v)
                if len(s1_ev) >= 4:
                    wait(plan, "tensor", ev, s1_ev[-4])
    s1_done = ev.v

    wait(plan, "gpsimd", mm, mm.v)
    cast_load(wA, gr, 0, gld)
    cast_load(wA, gineg, 1024, gld)
    g_all = gld.v

    wait(plan, "sync", mm, mm.v)
    emit_derive_xT(nc, plan, ctx, x3, x3bf, cA, xt, pre_wait=None)
    xt_all = xt.v

    wait(plan, "tensor", gld, g_all)
    wait(plan, "tensor", ev, s1_done)
    s2_ev = []
    for dt_ in range(DC):
        for och in range(2):
            bank = ctx["ps"][(dt_ * 2 + och) % 4]
            n_mm = 0
            for k in range(DC):
                for oi, (l_, r_) in enumerate((
                        (aT[k][:, dt_ * 128:dt_ * 128 + 128], wA[k][:, och * 512:och * 512 + 512]),
                        (aT[k][:, 1024 + dt_ * 128:1024 + dt_ * 128 + 128], wA[k][:, 1024 + och * 512:1024 + och * 512 + 512]))):
                    st_ = (k == 0 and oi == 0)
                    sp_ = (k == DC - 1 and oi == 1)
                    plan.add("tensor", lambda e, b=bank, l_=l_, r_=r_, st_=st_, sp_=sp_:
                             e.matmul(b[:, 0:512], l_, r_, start=st_, stop=sp_))
            pe_mark(plan, mm)
            wait(plan, "scalar", mm, mm.v)
            v = ev.inc()
            plan.add("scalar", lambda e, dt_=dt_, och=och, b=bank, v=v:
                     e.copy(mS[dt_][:, och * 512:och * 512 + 512], b[:, 0:512]).then_inc(ev.h, 1))
            s2_ev.append(v)
            if len(s2_ev) >= 4:
                wait(plan, "tensor", ev, s2_ev[-4])
    m_done = ev.v

    wait(plan, "tensor", ev, m_done)
    wait(plan, "tensor", xt, xt_all)
    wait(plan, "tensor", bb, bb_done)
    res_sem = [ctx["mksem"](name + "rA"), ctx["mksem"](name + "rB")]
    vv_of, ln_of, st_of = {}, {}, {}
    for nt in range(NT):
        pb = nt % 2
        banks = (ctx["ps"][4 + 2 * pb], ctx["ps"][5 + 2 * pb])
        if nt >= 2:
            wait(plan, "sync", ctx["ln_v"], vv_of[nt - 2])
        v = res_sem[pb].inc(16)
        plan.add("sync", lambda e, nt=nt, pb=pb, v=v: e.dma_start(
            xres[pb][:], x3[nt * 128:(nt + 1) * 128, :]).then_inc(res_sem[pb].h, 16))
        if nt >= 2:
            wait(plan, "tensor", ctx["ln_v"], vv_of[nt - 2])
        for j in range(DC):
            st_ = (j == 0)
            sp_ = (j == DC - 1)
            l_ = cA[j][:, nt * 128:(nt + 1) * 128]
            for och in range(2):
                r_ = mS[j][:, och * 512:och * 512 + 512]
                plan.add("tensor", lambda e, b=banks[och], l_=l_, r_=r_, st_=st_, sp_=sp_:
                         e.matmul(b[:, 0:512], l_, r_, start=st_, stop=sp_))
        mm_here = pe_mark(plan, mm)
        wait(plan, "vector", mm, mm_here)
        wait(plan, "vector", res_sem[pb], v)
        vpre = ctx["ln_v"].inc(2)
        for och in range(2):
            plan.add("vector", lambda e, b=banks[och], och=och, pb=pb:
                     e.tensor_add(ysb[pb][:, och * 512:och * 512 + 512], b[:, 0:512],
                                  bbc[:, och * 512:och * 512 + 512]).then_inc(ctx["ln_v"].h, 1))
        wait(plan, "vector", ctx["ln_v"], vpre)
        vv = ctx["ln_v"].inc()
        plan.add("vector", lambda e, pb=pb, v=vv: e.tensor_add(
            ysb[pb][:], ysb[pb][:], xres[pb][:]).then_inc(ctx["ln_v"].h, 1))
        vv_of[nt] = vv
        wait(plan, "scalar", ctx["ln_v"], vv)
        if nt >= 2:
            wait(plan, "vector", stm[pb], st_of[nt - 2])
        lnfin = emit_ln_epilogue(nc, plan, ctx, ysb[pb], osb[pb])
        ln_of[nt] = lnfin
        wait(plan, "gpsimd", ctx["ln_v"], lnfin)
        sv = stm[pb].inc(16)
        st_of[nt] = sv
        plan.add("gpsimd", lambda e, nt=nt, pb=pb: e.dma_start(
            out[nt * 128:(nt + 1) * 128, :], osb[pb][:]).then_inc(stm[pb].h, 16))
    wait(plan, "gpsimd", stm[0], stm[0].v)
    wait(plan, "gpsimd", stm[1], stm[1].v)
    return




def emit_barrier(nc, plan, ctx):
    for eng in Plan.ENGINES:
        wait(plan, eng, ctx["mm_sem"], ctx["mm_sem"].v)
        wait(plan, eng, ctx["ln_s"], ctx["ln_s"].v)
        wait(plan, eng, ctx["ln_v"], ctx["ln_v"].v)
        for s in ctx["dma_list"]:
            if s.v:
                wait(plan, eng, s, s.v)


def build_stage1():
    nc = bass.Bass("TRN2", target_bir_lowering=False)
    x = nc.dram_tensor("x", [T, D], F32, kind="ExternalInput").ap()
    wqkv = nc.dram_tensor("wqkv", [D, 3 * D], F32, kind="ExternalInput").ap()
    wproj = nc.dram_tensor("wproj", [D, D], F32, kind="ExternalInput").ap()
    bproj = nc.dram_tensor("bproj", [D], F32, kind="ExternalInput").ap()
    masks = nc.dram_tensor("masks", [128, 2048], BF16, kind="ExternalInput").ap()
    x1 = nc.dram_tensor("x1", [T, D], F32, kind="ExternalOutput").ap()
    xbf = nc.dram_tensor("xbf", [T, D], BF16, kind="Internal").ap()
    ahbm = nc.dram_tensor("ahbm", [D, T], BF16, kind="Internal").ap()

    import contextlib
    with contextlib.ExitStack() as stack:
        plan = Plan()
        ctx = mk_common(nc, stack, plan)
        emit_stage1(nc, stack, plan, ctx, x, xbf, wqkv, wproj, bproj, masks,
                    ahbm, x1)
        plan.run(nc)
    return nc


def emit_stage1(nc, stack, plan, ctx, x, xbf, wqkv, wproj, bproj, masks_h,
                ahbm, x1, name="s1"):
    sb = lambda n, shp, dt: ctx["salloc"](name + n, shp, dt)
    mksem = ctx["mksem"]
    NQB = T // 512
    arena = [sb(f"ar{c}", [128, T], BF16) for c in range(DC)]
    wg = [sb(f"wg{c}", [128, 768], BF16) for c in range(DC)]
    qTg = [sb(f"qT{p}", [128, T], BF16) for p in range(2)]
    kTg = [sb(f"kT{p}", [128, T], BF16) for p in range(2)]
    vext = sb("vext", [128, NT * 4 * 65], BF16)
    pbuf = sb("pbuf", [128, 2048], BF16)
    cb = sb("cb", [128, 16], F32)
    nbs = sb("nbs", [1, 16], F32)
    nbs_bf = sb("nbsb", [1, 2], BF16)
    msk = sb("msk", [128, 2048], BF16)
    apair = [sb(f"ap{p}", [128, 512], BF16) for p in range(2)]
    rrow = sb("rrow", [1, 1024], BF16)
    rbc = sb("rbc", [128, 512], F32)
    wpj = [sb(f"wp{c}", [128, 1024], BF16) for c in range(DC)]
    bbc = sb("bbc", [128, 1024], F32)
    ysb = [sb(f"y{p}", [128, 1024], F32) for p in range(2)]
    osb = [sb(f"o{p}", [128, 1024], F32) for p in range(2)]
    xres = [sb(f"xr{p}", [128, 1024], F32) for p in range(2)]

    ps = ctx["ps"]
    mm = ctx["mm_sem"]; lnv = ctx["ln_v"]; lns = ctx["ln_s"]
    xt = mksem(name + "xt")
    wld = mksem(name + "wld")
    pld = mksem(name + "pld")
    ast = [mksem(name + f"as{p}") for p in range(2)]
    atl = mksem(name + "atl")
    res2 = [mksem(name + "res2A"), mksem(name + "res2B")]
    stm = [mksem(name + f"st{p}") for p in range(2)]
    bb = Sem(nc, stack, name + "bb")

    emit_derive_xT(nc, plan, ctx, x, xbf, arena, xt)
    xt_all = xt.v
    pld.inc(16)
    plan.add("sync", lambda e: e.dma_start(msk[:], _ap(masks_h)).then_inc(pld.h, 16))
    pld_all = pld.v
    wpld = mksem(name + "wpld")

    def emit_wpj_loads():
        for c in range(DC):
            wpld.inc(16)
            plan.add("gpsimd", lambda e, c=c: e.dma_start(
                wpj[c][:], _ap(wproj)[c * 128:(c + 1) * 128, :]).then_inc(wpld.h, 16))
    emit_bias_broadcast(nc, plan, ctx, bproj, bbc, D, bb)
    bb_done = bb.v
    lnv.inc()
    plan.add("vector", lambda e: e.memset(
        vext[:].rearrange("p (s c) -> p s c", c=65)[:, :, 64:65], 1.0).then_inc(lnv.h, 1))
    vext_ones = lnv.v

    projmark = {}
    tile12 = 0
    st_of = {}
    for g in range(4):
        pf = ctx.pop(name + "pending_fin", None)
        if pf is not None:
            pf()
        if g >= 1:
            wait(plan, "gpsimd", mm, projmark[g - 1])
        for c in range(DC):
            for oi, (src0, w_) in enumerate(((g * 256, 256), (D + g * 256, 256), (2 * D + g * 256, 256))):
                wld.inc(16)
                plan.add("gpsimd", lambda e, c=c, src0=src0, oi=oi: e.dma_start(
                    wg[c][:, oi * 256:(oi + 1) * 256],
                    _ap(wqkv)[c * 128:(c + 1) * 128, src0:src0 + 256]).then_inc(wld.h, 16))
        if g == 0:
            emit_wpj_loads()
        pcc = ctx.get("precast_chunks")
        if pcc:
            pcc.pop(0)()
        wld_here = wld.v
        wait(plan, "tensor", wld, wld_here)
        if g >= 1:
            wait(plan, "tensor", lns, lns.v)
            wait(plan, "tensor", lnv, lnv.v)
        if g == 0:
            wait(plan, "tensor", xt, xt_all)
            wait(plan, "tensor", ctx["init_sem"], ctx["init_ones"])
        evac = {}
        idx = 0
        for role, dstl, coloff, scale in (("q", qTg, 0, 0.125), ("k", kTg, 256, 1.0)):
            for pt in range(2):
                for nsb in range(T // 512):
                    bank = ps[idx % 2]
                    if idx >= 2:
                        wait(plan, "tensor", lns, evac[idx - 2])
                    for c in range(DC):
                        plan.add("tensor", lambda e, c=c, pt=pt, nsb=nsb, coloff=coloff, bank=bank:
                                 e.matmul(bank[:, 0:512],
                                          wg[c][0:128, coloff + pt * 128:coloff + (pt + 1) * 128],
                                          arena[c][0:128, nsb * 512:(nsb + 1) * 512],
                                          start=(c == 0), stop=(c == DC - 1)))
                    mk = pe_mark(plan, mm)
                    wait(plan, "scalar", mm, mk)
                    lns.inc()
                    plan.add("scalar", lambda e, dstl=dstl, pt=pt, nsb=nsb, bank=bank, scale=scale:
                             e.activation(dstl[pt][:, nsb * 512:(nsb + 1) * 512],
                                          bank[:, 0:512], AF.Copy, scale=float(scale)).then_inc(lns.h, 1))
                    evac[idx] = lns.v
                    idx += 1
        vidx = 0
        vevac = {}
        for nt in range(NT):
            bank = ps[2 + nt % 2]
            if vidx >= 2:
                wait(plan, "tensor", lns, vevac[vidx - 2])
            for c in range(DC):
                plan.add("tensor", lambda e, c=c, nt=nt, bank=bank:
                         e.matmul(bank[:, 0:256],
                                  arena[c][0:128, nt * 128:(nt + 1) * 128],
                                  wg[c][0:128, 512:768],
                                  start=(c == 0), stop=(c == DC - 1)))
            mk = pe_mark(plan, mm)
            wait(plan, "scalar", mm, mk)
            if g == 0 and nt == 0:
                wait(plan, "scalar", lnv, vext_ones)
            for hg in range(4):
                lns.inc()
                plan.add("scalar", lambda e, nt=nt, hg=hg, bank=bank: e.copy(
                    vext[:, (nt * 4 + hg) * 65:(nt * 4 + hg) * 65 + 64],
                    bank[:, hg * 64:(hg + 1) * 64]).then_inc(lns.h, 1))
            vevac[vidx] = lns.v
            vidx += 1
        projmark[g] = mm.v
        proj_evac_all = lns.v

        wait(plan, "scalar", lns, proj_evac_all)
        for pt in range(2):
            for sl, srcl in ((0, qTg), (1, kTg)):
                first = True
                for half in range(max(T // 2048, 1)):
                    w_ = min(2048, T)
                    wait(plan, "scalar", mm, ctx.get(name + "bndmm", 0))
                    sqmark = lns.inc()
                    plan.add("scalar", lambda e, pt=pt, srcl=srcl, half=half, w_=w_: e.activation(
                        pbuf[:, 0:w_], srcl[pt][:, half * w_:(half + 1) * w_],
                        AF.Square).then_inc(lns.h, 1))
                    wait(plan, "tensor", lns, sqmark)
                    if first:
                        wait(plan, "tensor", lnv, ctx.get(name + "bndv", 0))
                    nhalves = max(T // 2048, 1)
                    nch = w_ // 512
                    for hh in range(2):
                        for ch in range(nch):
                            last_ = (half == nhalves - 1 and ch == nch - 1)
                            plan.add("tensor", lambda e, hh=hh, ch=ch, first=first, last_=last_:
                                     e.matmul((ps[2] if hh == 0 else ps[7])[0:1, 0:512],
                                              ctx["ones128"][hh * 64:hh * 64 + 64, 0:1],
                                              pbuf[hh * 64:hh * 64 + 64, ch * 512:(ch + 1) * 512],
                                              start=(first and ch == 0), stop=last_))
                    first = False
                    mk = pe_mark(plan, mm)
                    ctx[name + "bndmm"] = mk
                wait(plan, "vector", mm, mk)
                for hh in range(2):
                    h4 = 2 * pt + hh
                    lnv.inc()
                    plan.add("vector", lambda e, h4=h4, sl=sl, hh=hh: e.reduce_max(
                        nbs[0:1, 2 * h4 + sl:2 * h4 + sl + 1],
                        (ps[2] if hh == 0 else ps[7])[0:1, 0:512], axis=AX.X).then_inc(lnv.h, 1))
                ctx[name + "bndv"] = lnv.v
            p_ = 2 * g + pt
            for hh in range(2):
                bidx = 2 * (2 * pt + hh)
                wait(plan, "vector", lnv, ctx[name + "bndv"])
                lnv.inc()
                plan.add("vector", lambda e, bidx=bidx: e.tensor_mul(
                    nbs[0:1, 8 + bidx:9 + bidx], nbs[0:1, bidx:bidx + 1],
                    nbs[0:1, bidx + 1:bidx + 2]).then_inc(lnv.h, 1))
            bidx0 = 2 * (2 * pt)
            wait(plan, "vector", lnv, lnv.v)
            lnv.inc()
            plan.add("vector", lambda e, bidx0=bidx0: e.tensor_max(
                nbs[0:1, 8 + bidx0:9 + bidx0], nbs[0:1, 8 + bidx0:9 + bidx0],
                nbs[0:1, 10 + bidx0:11 + bidx0]).then_inc(lnv.h, 1))
            wait(plan, "scalar", lnv, lnv.v)
            lns.inc()
            plan.add("scalar", lambda e, bidx0=bidx0: e.activation(
                nbs[0:1, 9 + bidx0:10 + bidx0], nbs[0:1, 8 + bidx0:9 + bidx0],
                AF.Ln).then_inc(lns.h, 1))
            wait(plan, "scalar", lns, lns.v)
            lns.inc()
            plan.add("scalar", lambda e, bidx0=bidx0: e.activation(
                nbs[0:1, 8 + bidx0:9 + bidx0], nbs[0:1, 9 + bidx0:10 + bidx0],
                AF.Exp, scale=0.5).then_inc(lns.h, 1))
            wait(plan, "vector", lns, lns.v)
            wait(plan, "vector", mm, mm.v)
            lnv.inc()
            plan.add("vector", lambda e, bidx0=bidx0: e.tensor_scalar_mul(
                nbs_bf[0:1, 0:1], nbs[0:1, 8 + bidx0:9 + bidx0], -1.0).then_inc(lnv.h, 1))
            wait(plan, "tensor", lnv, lnv.v)
            plan.add("tensor", lambda e: e.matmul(
                ps[7][0:128, 0:1], ctx["ones1"][0:1, :], nbs_bf[0:1, 0:1],
                start=True, stop=True))
            mk = pe_mark(plan, mm)
            wait(plan, "scalar", mm, mk)
            lns.inc()
            plan.add("scalar", lambda e, p_=p_: e.copy(
                cb[:, p_:p_ + 1], ps[7][0:128, 0:1]).then_inc(lns.h, 1))
        cb_done = lns.v

        wait(plan, "tensor", lns, cb_done)
        if g == 0:
            wait(plan, "tensor", pld, pld_all)
            wait(plan, "vector", pld, pld_all)
        for pt in range(2):
            pair = 2 * g + pt
            for qb in range(NQB):
                J = 4 * qb + 4
                expm = {}
                avm = {}
                maskm = {}
                pb = tile12 % 2
                wait(plan, "tensor", lns, ctx.get(name + "exp_last", 0))
                jpos = {}

                def emit_S(j):
                    ji = len(jpos)
                    jpos[j] = ji
                    SS = ctx["psB"][2 + ji % 2]
                    if ji >= 2:
                        wait(plan, "tensor", lns, expm[ji - 2])
                    plan.add("tensor", lambda e, pt=pt, j=j, qb=qb, SS=SS:
                             e.matmul(SS[:, 0:512], kTg[pt][0:64, j * 128:(j + 1) * 128],
                                      qTg[pt][0:64, qb * 512:(qb + 1) * 512],
                                      start=True, stop=True, tile_position=(0, 0)))
                    plan.add("tensor", lambda e, pt=pt, j=j, qb=qb, SS=SS:
                             e.matmul(SS[:, 512:1024], kTg[pt][64:128, j * 128:(j + 1) * 128],
                                      qTg[pt][64:128, qb * 512:(qb + 1) * 512],
                                      start=True, stop=True, tile_position=(64, 0)))
                    mmS[j] = pe_mark(plan, mm)

                def emit_expmask(j):
                    ji = jpos[j]
                    SS = ctx["psB"][2 + ji % 2]
                    wait(plan, "scalar", mm, mmS[j])
                    if ji >= 2:
                        wait(plan, "scalar", mm, avm[ji - 2])
                    p_ = 2 * g + pt
                    lns.inc()
                    plan.add("scalar", lambda e, ji=ji, SS=SS, p_=p_: e.activation(
                        pbuf[:, (ji % 2) * 1024:(ji % 2) * 1024 + 1024],
                        SS[:, 0:1024], AF.Exp, bias=cb[:, p_:p_ + 1]).then_inc(lns.h, 1))
                    expm[ji] = lns.v
                    if j >= 4 * qb:
                        r_off = (j - 4 * qb) * 512
                        wait(plan, "vector", lns, expm[ji])
                        for hh in range(2):
                            lnv.inc()
                            plan.add("vector", lambda e, hh=hh, ji=ji, r_off=r_off: e.tensor_mul(
                                pbuf[:, (ji % 2) * 1024 + hh * 512:(ji % 2) * 1024 + hh * 512 + 512],
                                pbuf[:, (ji % 2) * 1024 + hh * 512:(ji % 2) * 1024 + hh * 512 + 512],
                                msk[:, r_off:r_off + 512]).then_inc(lnv.h, 1))
                        maskm[ji] = lnv.v

                def emit_av(j, first, last):
                    ji = jpos[j]
                    if j >= 4 * qb:
                        wait(plan, "tensor", lnv, maskm[ji])
                    else:
                        wait(plan, "tensor", lns, expm[ji])
                    for hh, attb in ((0, ps[0]), (1, ps[1])):
                        hg = 2 * pt + hh
                        plan.add("tensor", lambda e, hh=hh, j=j, ji=ji, hg=hg, attb=attb,
                                 first=first, last=last:
                                 e.matmul(attb[0:65, 0:512],
                                          vext[:, (j * 4 + hg) * 65:(j * 4 + hg) * 65 + 65],
                                          pbuf[:, (ji % 2) * 1024 + hh * 512:
                                               (ji % 2) * 1024 + hh * 512 + 512],
                                          start=first, stop=last))
                    avm[ji] = pe_mark(plan, mm)

                mmS = {}
                jorder = list(range(4 * qb, J)) + list(range(4 * qb))
                emit_S(jorder[0])
                pf = ctx.pop(name + "pending_fin", None)
                if pf is not None:
                    pf()
                for ji, j in enumerate(jorder):
                    emit_expmask(j)
                    if ji + 1 < J:
                        emit_S(jorder[ji + 1])
                    if ji == 0:
                        wait(plan, "tensor", lnv, ctx.get(name + "finp", 0))
                    emit_av(j, first=(ji == 0), last=(ji == J - 1))
                ctx[name + "exp_last"] = expm[J - 1]
                wait(plan, "vector", mm, avm[J - 1])
                wait(plan, "vector", mm, ctx.get(name + "rrm", 0))
                for hh, attb in ((0, ps[0]), (1, ps[1])):
                    lnv.inc()
                    def _recip(e, hh=hh, attb=attb):
                        with nc.allow_low_precision(reason="softmax 1/l to bf16 matmul operand"):
                            return e.reciprocal(rrow[0:1, hh * 512:hh * 512 + 512],
                                                attb[64:65, 0:512])
                    plan.add("vector", lambda e, f=_recip: f(e).then_inc(lnv.h, 1))
                rcopy_done = lnv.v

                def part2(pair=pair, qb=qb, pb=pb, t12=tile12, rcopy_done=rcopy_done):
                    wait(plan, "tensor", lnv, rcopy_done)
                    wait(plan, "tensor", lns, ctx.get(name + "rbs", 0))
                    for hh in range(2):
                        plan.add("tensor", lambda e, hh=hh: e.matmul(
                            ps[2][hh * 64:hh * 64 + 64, 0:512],
                            ctx["ones1"][0:1, hh * 64:hh * 64 + 64],
                            rrow[0:1, hh * 512:hh * 512 + 512],
                            start=True, stop=True))
                    rbm = pe_mark(plan, mm)
                    ctx[name + "rrm"] = rbm
                    wait(plan, "scalar", mm, rbm)
                    wait(plan, "scalar", lnv, ctx.get(name + "rbv", 0))
                    lns.inc()
                    plan.add("scalar", lambda e: e.copy(rbc[:], ps[2][:, 0:512]).then_inc(lns.h, 1))
                    ctx[name + "rbs"] = lns.v
                    wait(plan, "vector", lns, lns.v)
                    if t12 >= 2:
                        wait(plan, "vector", ast[pb], st_of.get(t12 - 2, 0))
                    for hh, attb in ((0, ps[0]), (1, ps[1])):
                        lnv.inc()
                        plan.add("vector", lambda e, hh=hh, attb=attb, pb=pb: e.tensor_mul(
                            apair[pb][hh * 64:hh * 64 + 64, 0:512],
                            attb[0:64, 0:512],
                            rbc[hh * 64:hh * 64 + 64, 0:512]).then_inc(lnv.h, 1))
                    ctx[name + "rbv"] = lnv.v
                    ctx[name + "finp"] = lnv.v
                    wait(plan, "gpsimd", lnv, lnv.v)
                    sv = ast[pb].inc(16)
                    st_of[t12] = sv
                    plan.add("gpsimd", lambda e, pair=pair, qb=qb, pb=pb: e.dma_start(
                        _ap(ahbm)[pair * 128:(pair + 1) * 128, qb * 512:(qb + 1) * 512],
                        apair[pb][:, 0:512]).then_inc(ast[pb].h, 16))

                ctx[name + "pending_fin"] = part2
                tile12 += 1

    pf = ctx.pop(name + "pending_fin", None)
    if pf is not None:
        pf()
    wait(plan, "sync", ast[0], ast[0].v)
    wait(plan, "sync", ast[1], ast[1].v)
    wait(plan, "sync", mm, projmark[3])
    for c in range(DC):
        atl.inc(16)
        plan.add("sync", lambda e, c=c: e.dma_start(
            arena[c][0:128, 0:T], _ap(ahbm)[c * 128:(c + 1) * 128, 0:T]
        ).then_inc(atl.h, 16))
    atl_all = atl.v

    wait(plan, "tensor", atl, atl_all)
    wait(plan, "tensor", wpld, wpld.v)
    wait(plan, "tensor", bb, bb_done)
    vv_of, ln_of2, st_of2 = {}, {}, {}
    for nt in range(NT):
        pb = nt % 2
        banks = (ps[4 + 2 * pb], ps[5 + 2 * pb])
        if nt >= 2:
            wait(plan, "sync", lnv, vv_of[nt - 2])
        rv = res2[pb].inc(16)
        plan.add("sync", lambda e, nt=nt, pb=pb: e.dma_start(
            xres[pb][:], _ap(x)[nt * 128:(nt + 1) * 128, :]).then_inc(res2[pb].h, 16))
        if nt >= 2:
            wait(plan, "tensor", lnv, vv_of[nt - 2])
        for c in range(DC):
            for och in range(2):
                plan.add("tensor", lambda e, c=c, nt=nt, och=och, b=banks[och]:
                         e.matmul(b[:, 0:512], arena[c][0:128, nt * 128:(nt + 1) * 128],
                                  wpj[c][0:128, och * 512:och * 512 + 512],
                                  start=(c == 0), stop=(c == DC - 1)))
        mm_here = pe_mark(plan, mm)
        wait(plan, "vector", mm, mm_here)
        wait(plan, "vector", res2[pb], rv)
        if nt == 0:
            wait(plan, "vector", bb, bb_done)
        vpre = lnv.inc(2)
        for och in range(2):
            plan.add("vector", lambda e, b=banks[och], och=och, pb=pb:
                     e.tensor_add(ysb[pb][:, och * 512:och * 512 + 512], b[:, 0:512],
                                  bbc[:, och * 512:och * 512 + 512]).then_inc(lnv.h, 1))
        wait(plan, "vector", lnv, vpre)
        vv = lnv.inc()
        plan.add("vector", lambda e, pb=pb: e.tensor_add(
            ysb[pb][:], ysb[pb][:], xres[pb][:]).then_inc(lnv.h, 1))
        vv_of[nt] = vv
        wait(plan, "scalar", lnv, vv)
        if nt >= 2:
            wait(plan, "vector", stm[pb], st_of2[nt - 2])
        lnfin = emit_ln_epilogue(nc, plan, ctx, ysb[pb], osb[pb])
        ln_of2[nt] = lnfin
        wait(plan, "gpsimd", lnv, lnfin)
        sv = stm[pb].inc(16)
        st_of2[nt] = sv
        plan.add("gpsimd", lambda e, nt=nt, pb=pb: e.dma_start(
            _ap(x1)[nt * 128:(nt + 1) * 128, :], osb[pb][:]).then_inc(stm[pb].h, 16))
    wait(plan, "gpsimd", stm[0], stm[0].v)
    wait(plan, "gpsimd", stm[1], stm[1].v)
    return



def build_stage2():
    nc = bass.Bass("TRN2", target_bir_lowering=False)
    x1 = nc.dram_tensor("x1", [T, D], F32, kind="ExternalInput").ap()
    wrt = nc.dram_tensor("wrt", [D, 8], F32, kind="ExternalInput").ap()
    w1 = nc.dram_tensor("w1", [8 * 4096, D], F32, kind="ExternalInput").ap()
    w2 = nc.dram_tensor("w2", [8 * D, 2048], F32, kind="ExternalInput").ap()
    x2 = nc.dram_tensor("x2", [T, D], F32, kind="ExternalOutput").ap()
    x1bf = nc.dram_tensor("x1bf", [T, D], BF16, kind="Internal").ap()
    w1bf = nc.dram_tensor("w1bf", [8 * 4096, D], BF16, kind="Internal").ap()
    w2bf = nc.dram_tensor("w2bf", [8 * D, 2048], BF16, kind="Internal").ap()

    import contextlib
    with contextlib.ExitStack() as stack:
        plan = Plan()
        ctx = mk_common(nc, stack, plan)
        emit_stage2(nc, stack, plan, ctx, x1, wrt, w1, w2, x1bf, w1bf,
                    w2bf, x2)
        plan.run(nc)
    return nc


def emit_weight_precasts(nc, plan, ctx, w1, w2, w1bf, w2bf, wqkv, wqkvbf,
                         wout, woutbf):
    mksem = ctx["mksem_nobar"]
    caste = [mksem(f"pce{e}") for e in range(8)]
    c3 = mksem("pc3")
    for e in range(8):
        caste[e].inc(32)
    c3.inc(64)
    c3_final = c3.v

    def chunk(es, with_c3=False):
        def emit():
            for e in es:
                plan.add("gpsimd", lambda e_, e=e: e_.dma_start(
                    _ap(w1bf)[e * 4096:(e + 1) * 4096, :],
                    _ap(w1)[e * 4096:(e + 1) * 4096, :]).then_inc(caste[e].h, 16))
                plan.add("gpsimd", lambda e_, e=e: e_.dma_start(
                    _ap(w2bf)[e * D:(e + 1) * D, :],
                    _ap(w2)[e * D:(e + 1) * D, :]).then_inc(caste[e].h, 16))
            if with_c3:
                for i in range(3):
                    plan.add("gpsimd", lambda e_, i=i: e_.dma_start(
                        _ap(wqkvbf)[i * D:(i + 1) * D, :],
                        _ap(wqkv)[i * D:(i + 1) * D, :]).then_inc(c3.h, 16))
                plan.add("gpsimd", lambda e_: e_.dma_start(
                    _ap(woutbf), _ap(wout)).then_inc(c3.h, 16))
        return emit

    chunks = [chunk([0, 1]), chunk([2, 3]), chunk([4, 5]), chunk([6, 7], True)]
    return {"caste": caste, "c3": (c3, c3_final), "chunks": chunks}


def emit_stage2(nc, stack, plan, ctx, x1, wrt, w1, w2, x1bf, w1bf, w2bf,
                x2, name="s2", ext_casts=None):
    E = 8
    QT = min(1024, T)
    NQ = T // QT
    NTQ = QT // 128
    NT2 = QT // 256
    sb = lambda n, shp, dt: ctx["salloc"](name + n, shp, dt)
    mksem = ctx["mksem"]
    x1T = [[sb(f"xT{p}_{c}", [128, QT], BF16) for c in range(DC)] for p in range(2)]
    w1tv = [[sb(f"w1v{b}_{c}", [128, 1024], BF16) for c in range(DC)] for b in range(2)]
    w1tg = [[sb(f"w1g{b}_{c}", [128, 1024], BF16) for c in range(DC)] for b in range(2)]
    w2t = [[sb(f"w2t{b}_{f}", [128, 1024], BF16) for f in range(DC)] for b in range(2)]
    wrT = sb("wrT", [128, 64], BF16)
    acta = [[sb(f"act{p}_{f}", [128, 512], BF16) for f in range(DC)] for p in range(2)]
    sg = [sb(f"sg{p}", [128, 512], F32) for p in range(2)]
    rwq = sb("rwq", [128, 8 * NTQ], F32)
    lg = sb("lg", [128, 8 * NTQ], F32)
    rt = sb("rt", [128, 40], F32)
    yac = [sb(f"ya{t}", [128, 1024], F32) for t in range(NTQ)]
    osb = [sb(f"o{p}", [128, 1024], F32) for p in range(2)]
    xres = [sb("xr0", [128, 1024], F32)]

    ps = ctx["ps"]
    mm = ctx["mm_sem"]; lnv = ctx["ln_v"]; lns = ctx["ln_s"]
    cast = mksem(name + "cast")
    caste = [mksem(name + f"ce{e}") for e in range(E)]
    wrld = mksem(name + "wrld")
    xtq = mksem(name + "xtq")
    wld = [mksem(name + f"wld{b}") for b in range(2)]
    res = mksem(name + "res")
    stm = [mksem(name + f"st{p}") for p in range(2)]

    def vop(fn):
        lnv.inc()
        plan.add("vector", lambda e_, fn=fn: fn(e_).then_inc(lnv.h, 1))
        wait(plan, "vector", lnv, lnv.v)

    slab = T // 4
    castsl = [mksem(name + f"cs{i}") for i in range(4)]
    for i in range(4):
        castsl[i].inc(16)
        plan.add("gpsimd", lambda e_, i=i, slab=slab: e_.dma_start(
            x1bf[i * slab:(i + 1) * slab, :], x1[i * slab:(i + 1) * slab, :]).then_inc(castsl[i].h, 16))
    cast_all = cast.v
    if ext_casts is not None:
        caste = ext_casts["caste"]
    else:
        for e in range(E):
            caste[e].inc(32)
            plan.add("gpsimd", lambda e_, e=e: e_.dma_start(
                w1bf[e * 4096:(e + 1) * 4096, :], w1[e * 4096:(e + 1) * 4096, :]).then_inc(caste[e].h, 16))
            plan.add("gpsimd", lambda e_, e=e: e_.dma_start(
                w2bf[e * D:(e + 1) * D, :], w2[e * D:(e + 1) * D, :]).then_inc(caste[e].h, 16))

    for c in range(DC):
        wrld.inc(16)
        plan.add("gpsimd", lambda e_, c=c: e_.dma_start(
            wrT[:, c * 8:(c + 1) * 8], _ap(wrt)[c * 128:(c + 1) * 128, :]
        ).then_inc(wrld.h, 16))
    wrld_all = wrld.v

    wait(plan, "tensor", ctx["init_sem"], ctx["init_ones"])
    q_hend = {}
    buffree = [0, 0]
    actm_h = {}
    yst_h = {}
    lastY = [0, 0]
    gfs = 0
    gy = 0
    rnd = 0
    st_of = {}
    tile_base = 0

    def emit_x1T_loads(q):
        if q >= NQ:
            return
        qb2 = q % 2
        if q >= 2:
            wait(plan, "sync", mm, q_hend.get(q - 2, 0))
        nslab = min(4, -(-((q + 1) * QT * 4) // T))
        for i_ in range(nslab):
            wait(plan, "sync", castsl[i_], 16)
        xtq.inc(16 * DC)
        for c in range(DC):
            plan.add("sync", lambda e_, q=q, c=c, qb2=qb2: e_.dma_start(
                x1T[qb2][c][0:128, 0:QT],
                x1bf[q * QT:(q + 1) * QT, c * 128:(c + 1) * 128],
                transpose=True).then_inc(xtq.h, 16))
        return xtq.v

    def emit_router(q, xv):
        qb2 = q % 2
        wait(plan, "tensor", xtq, xv)
        if q == 0:
            wait(plan, "tensor", wrld, wrld_all)
        wait(plan, "tensor", lnv, actm_h.get(gfs - 1, 0))
        wait(plan, "scalar", lnv, ctx.get(name + "rtv", 0))
        for ntq in range(NTQ):
            for c in range(DC):
                plan.add("tensor", lambda e_, ntq=ntq, c=c, qb2=qb2: e_.matmul(
                    ps[0][:, ntq * 8:(ntq + 1) * 8],
                    x1T[qb2][c][0:128, ntq * 128:(ntq + 1) * 128],
                    wrT[:, c * 8:(c + 1) * 8],
                    start=(ntq == 0 and c == 0), stop=(ntq == NTQ - 1 and c == DC - 1)))
        mk = pe_mark(plan, mm)
        wait(plan, "scalar", mm, mk)
        lns.inc()
        plan.add("scalar", lambda e_: e_.copy(lg[:], ps[0][:, 0:8 * NTQ]).then_inc(lns.h, 1))
        lg_done = lns.v
        ctx[name + "lgd"] = lg_done
        wait(plan, "vector", lns, lg_done)
        for ntq in range(NTQ):
            lgs = lg[:, ntq * 8:(ntq + 1) * 8]
            m1 = rt[:, 0:1]; eq1 = rt[:, 1:9]; mskd = rt[:, 9:17]
            m2 = rt[:, 17:18]; eq2 = rt[:, 24:32]; dd = rt[:, 18:19]
            ex = rt[:, 19:20]; tt = rt[:, 20:21]; w1v = rt[:, 21:22]
            w2v = rt[:, 22:23]; t2 = rt[:, 32:40]
            rws = rwq[:, ntq * 8:(ntq + 1) * 8]

            def vop(fn):
                lnv.inc()
                plan.add("vector", lambda e_, fn=fn: fn(e_).then_inc(lnv.h, 1))
                wait(plan, "vector", lnv, lnv.v)
            vop(lambda e_, lgs=lgs: e_.reduce_max(m1, lgs, axis=AX.X))
            vop(lambda e_, lgs=lgs: e_.tensor_single_scalar(eq1, lgs, m1, ALU.is_equal))
            vop(lambda e_, lgs=lgs: e_.scalar_tensor_tensor(mskd, eq1, -1e30, lgs, ALU.mult, ALU.add))
            vop(lambda e_: e_.reduce_max(m2, mskd, axis=AX.X))
            vop(lambda e_: e_.tensor_single_scalar(eq2, mskd, m2, ALU.is_equal))
            vop(lambda e_: e_.tensor_sub(dd, m2, m1))
            wait(plan, "scalar", lnv, lnv.v)
            lns.inc()
            plan.add("scalar", lambda e_, dd=dd, ex=ex: e_.activation(
                ex, dd, AF.Exp).then_inc(lns.h, 1))
            wait(plan, "vector", lns, lns.v)
            vop(lambda e_: e_.tensor_scalar_add(tt, ex, 1.0))
            vop(lambda e_: e_.reciprocal(w1v, tt))
            vop(lambda e_: e_.tensor_scalar(w2v, w1v, -1.0, 1.0, ALU.mult, ALU.add))
            vop(lambda e_: e_.tensor_single_scalar(t2, eq1, w1v, ALU.mult))
            vop(lambda e_, rws=rws: e_.scalar_tensor_tensor(rws, eq2, w2v, t2, ALU.mult, ALU.add))
            ctx[name + "rtv"] = lnv.v
        return lg_done

    def emit_loads(q, e, fh):
        b = (e * 2 + fh) % 2
        wait(plan, "sync", caste[e], caste[e].v)
        wait(plan, "sync", mm, buffree[b])
        wld[b].inc(16 * 3 * DC)
        for c in range(DC):
            plan.add("sync", lambda e_, e=e, fh=fh, c=c, b=b: e_.dma_start(
                w1tv[b][c][:], w1bf[e * 4096 + fh * 1024:e * 4096 + (fh + 1) * 1024,
                                   c * 128:(c + 1) * 128],
                transpose=True).then_inc(wld[b].h, 16))
            plan.add("sync", lambda e_, e=e, fh=fh, c=c, b=b: e_.dma_start(
                w1tg[b][c][:], w1bf[e * 4096 + 2048 + fh * 1024:
                                    e * 4096 + 2048 + (fh + 1) * 1024,
                                    c * 128:(c + 1) * 128],
                transpose=True).then_inc(wld[b].h, 16))
            plan.add("sync", lambda e_, e=e, fh=fh, c=c, b=b: e_.dma_start(
                w2t[b][c][:], w2bf[e * D:(e + 1) * D,
                                   fh * 1024 + c * 128:fh * 1024 + (c + 1) * 128],
                transpose=True).then_inc(wld[b].h, 16))
        return wld[b].v

    def emit_H(q, e, fh, ntw, wv, rw_done, qfirst):
        nonlocal gfs, rnd
        qb2 = q % 2
        b = (e * 2 + fh) % 2
        par = rnd % 2
        if ntw == 0:
            wait(plan, "tensor", wld[b], wv)
        if qfirst and ntw == 0:
            wait(plan, "tensor", lns, rw_done)
        for fs in range(DC):
            pv = ps[0 + 2 * (fs % 2)]
            pg = ps[1 + 2 * (fs % 2)]
            if gfs >= 2:
                wait(plan, "tensor", lnv, actm_h.get(gfs - 2, 0))
            for c in range(DC):
                plan.add("tensor", lambda e_, c=c, fs=fs, ntw=ntw, b=b, pv=pv, qb2=qb2:
                         e_.matmul(pv[:, 0:512],
                                   w1tv[b][c][0:128, fs * 128:(fs + 1) * 128],
                                   x1T[qb2][c][0:128, ntw * 512:(ntw + 1) * 512],
                                   start=(c == 0), stop=(c == DC - 1)))
            for c in range(DC):
                plan.add("tensor", lambda e_, c=c, fs=fs, ntw=ntw, b=b, pg=pg, qb2=qb2:
                         e_.matmul(pg[:, 0:512],
                                   w1tg[b][c][0:128, fs * 128:(fs + 1) * 128],
                                   x1T[qb2][c][0:128, ntw * 512:(ntw + 1) * 512],
                                   start=(c == 0), stop=(c == DC - 1)))
            hmk = pe_mark(plan, mm)
            wait(plan, "scalar", mm, hmk)
            if gfs >= 2:
                wait(plan, "scalar", lnv, actm_h.get(gfs - 2, 0))
            lns.inc()
            plan.add("scalar", lambda e_, fs=fs, pg=pg: e_.activation(
                sg[fs % 2][:], pg[:, 0:512], AF.Silu).then_inc(lns.h, 1))
            smk = lns.v
            wait(plan, "vector", lns, smk)
            if lastY[par]:
                wait(plan, "vector", mm, lastY[par])
            lnv.inc()
            plan.add("vector", lambda e_, fs=fs, pv=pv, par=par: e_.tensor_mul(
                acta[par][fs][:], pv[:, 0:512], sg[fs % 2][:]).then_inc(lnv.h, 1))
            actm_h[gfs] = lnv.v
            gfs += 1
        rnd += 1
        return par, gfs - 1

    def make_Y(q, e, fh, ntw, par, gfs_last, qfirst):
        b = (e * 2 + fh) % 2

        def emit_Y():
            nonlocal gy
            wait(plan, "tensor", lnv, actm_h[gfs_last])
            first = qfirst
            for nsub in range(4):
                tile = ntw * 4 + nsub
                pairb = (ps[4], ps[5]) if nsub % 2 == 0 else (ps[6], ps[7])
                if gy >= 2:
                    wait(plan, "tensor", lnv, yst_h.get(gy - 2, 0))
                for fs in range(DC):
                    for cc in range(2):
                        plan.add("tensor", lambda e_, fs=fs, nsub=nsub, cc=cc, b=b, par=par, yb=pairb[cc]:
                                 e_.matmul(yb[:, 0:512],
                                           acta[par][fs][0:128, nsub * 128:(nsub + 1) * 128],
                                           w2t[b][fs][0:128, cc * 512:(cc + 1) * 512],
                                           start=(fs == 0), stop=(fs == DC - 1)))
                ymk = pe_mark(plan, mm)
                wait(plan, "vector", mm, ymk)
                for cc in range(2):
                    lnv.inc()
                    if first:
                        plan.add("vector", lambda e_, tile=tile, cc=cc, e=e, yb=pairb[cc]:
                                 e_.tensor_single_scalar(
                                     yac[tile][:, cc * 512:(cc + 1) * 512],
                                     yb[:, 0:512],
                                     rwq[:, tile * 8 + e:tile * 8 + e + 1],
                                     ALU.mult).then_inc(lnv.h, 1))
                    else:
                        plan.add("vector", lambda e_, tile=tile, cc=cc, e=e, yb=pairb[cc]:
                                 e_.scalar_tensor_tensor(
                                     yac[tile][:, cc * 512:(cc + 1) * 512],
                                     yb[:, 0:512],
                                     rwq[:, tile * 8 + e:tile * 8 + e + 1],
                                     yac[tile][:, cc * 512:(cc + 1) * 512],
                                     ALU.mult, ALU.add).then_inc(lnv.h, 1))
                yst_h[gy] = lnv.v
                gy += 1
            lastY[par] = mm.v
            buffree[b] = mm.v
        return emit_Y

    def make_finalize(q):
        def one_tile(ntq):
            def emit_fin_tile():
                t = q * NTQ + ntq
                pb = t % 2
                rv = res.inc(16)
                plan.add("gpsimd", lambda e_, t=t: e_.dma_start(
                    xres[0][:], x1[t * 128:(t + 1) * 128, :]).then_inc(res.h, 16))
                wait(plan, "vector", res, rv)
                lnv.inc()
                plan.add("vector", lambda e_, ntq=ntq: e_.tensor_add(
                    yac[ntq][:], yac[ntq][:], xres[0][:]).then_inc(lnv.h, 1))
                vv = lnv.v
                wait(plan, "scalar", lnv, vv)
                if t >= 2:
                    wait(plan, "vector", stm[pb], st_of[t - 2])
                lnfin = emit_ln_epilogue(nc, plan, ctx, yac[ntq], osb[pb])
                wait(plan, "gpsimd", lnv, lnfin)
                sv = stm[pb].inc(16)
                st_of[t] = sv
                plan.add("gpsimd", lambda e_, t=t, pb=pb: e_.dma_start(
                    x2[t * 128:(t + 1) * 128, :], osb[pb][:]).then_inc(stm[pb].h, 16))
            return emit_fin_tile
        return [one_tile(ntq) for ntq in range(NTQ)]

    xv = emit_x1T_loads(0)
    xv_next = None
    for q in range(NQ):
        xv_next = emit_x1T_loads(q + 1)
        rw_done = emit_router(q, xv)
        xv = xv_next
        pending_Y = None
        ef = [(e, fh) for e in range(E) for fh in range(2)]
        if q % 2 == 1:
            ef = ef[::-1]
        fin = make_finalize(q)
        for efi, (e, fh) in enumerate(ef):
            b = (e * 2 + fh) % 2
            if ctx.get(name + f"buf{b}") != (e, fh):
                emit_loads(q, e, fh)
                ctx[name + f"buf{b}"] = (e, fh)
            wv = wld[b].v
            last_ef = (efi == len(ef) - 1)
            for ntw in range(QT // 512):
                par, gl = emit_H(q, e, fh, ntw, wv, rw_done, efi == 0)
                if pending_Y is not None:
                    pending_Y()
                if last_ef and ntw == 1 and QT // 512 > 1:
                    for ntq_ in range(NTQ // 2):
                        fin[ntq_]()
                pending_Y = make_Y(q, e, fh, ntw, par, gl, efi == 0)
        pending_Y()
        start_ = NTQ // 2 if QT // 512 > 1 else 0
        for ntq_ in range(start_, NTQ):
            fin[ntq_]()
        q_hend[q] = mm.v
    wait(plan, "gpsimd", stm[0], stm[0].v)
    wait(plan, "gpsimd", stm[1], stm[1].v)
    return


def _run_stage2(x1, Wr, W1, W2):
    nc = build_stage2()
    in_maps = [dict(x1=np.ascontiguousarray(x1[b]), wrt=np.ascontiguousarray(Wr.T),
                    w1=np.ascontiguousarray(W1.reshape(8 * 4096, D)),
                    w2=np.ascontiguousarray(W2.reshape(8 * D, 2048)))
               for b in range(x1.shape[0])]
    res = run_bass_kernel_spmd(nc, in_maps, core_ids=list(range(x1.shape[0])))
    return np.stack([r["x2"] for r in res.results], 0)


def build_stage3():
    nc = bass.Bass("TRN2", target_bir_lowering=False)
    x2 = nc.dram_tensor("x2", [T, D], F32, kind="ExternalInput").ap()
    wqkv = nc.dram_tensor("wqkv", [3 * D, D], F32, kind="ExternalInput").ap()
    wout = nc.dram_tensor("wout", [D, D], F32, kind="ExternalInput").ap()
    bout = nc.dram_tensor("bout", [D], F32, kind="ExternalInput").ap()
    x3 = nc.dram_tensor("x3", [T, D], F32, kind="ExternalOutput").ap()
    x2bf = nc.dram_tensor("x2bf", [T, D], BF16, kind="Internal").ap()
    wqkvbf = nc.dram_tensor("wqkvbf", [3 * D, D], BF16, kind="Internal").ap()
    woutbf = nc.dram_tensor("woutbf", [D, D], BF16, kind="Internal").ap()

    import contextlib
    with contextlib.ExitStack() as stack:
        plan = Plan()
        ctx = mk_common(nc, stack, plan)
        emit_stage3(nc, stack, plan, ctx, x2, x2bf, wqkv, wqkvbf, wout, woutbf,
                    bout, x3)
        plan.run(nc)
    return nc


def emit_stage3(nc, stack, plan, ctx, x2, x2bf, wqkv, wqkvbf, wout, woutbf,
                bout, x3, name="s3", ext_casts=None):
    EPS = 1e-6
    sb = lambda n, shp, dt: ctx["salloc"](name + n, shp, dt)
    mksem = ctx["mksem"]
    NS = T // 512
    kvw = [sb(f"kvw{c}", [128, 2048], BF16) for c in range(DC)]
    woT = [sb(f"wo{c}", [128, 1024], BF16) for c in range(DC)]
    kvacc = [sb(f"kva{c}", [128, 1024], F32) for c in range(DC)]
    kvbf = [sb(f"kvb{c}", [128, 1024], BF16) for c in range(DC)]
    x2Ta = [sb(f"x2a{p}", [128, 1024], BF16) for p in range(2)]
    x2Tb = [sb(f"x2b{p}", [128, 4096], BF16) for p in range(2)]
    kbf = [sb(f"kbf{i}", [128, 1024], BF16) for i in range(4)]
    vbf = [sb(f"vbf{i}", [128, 1024], BF16) for i in range(4)]
    qT = [[sb(f"qT{p}_{c}", [128, 512], BF16) for c in range(DC)] for p in range(2)]
    aT = [sb(f"aT{c}", [128, 512], BF16) for c in range(DC)]
    ksumT = sb("ksumT", [128, 8], BF16)
    zsb = sb("z", [128, 8], F32)
    bbc = sb("bbc", [128, 1024], F32)
    ysb = [sb(f"y{p}", [128, 1024], F32) for p in range(2)]
    osb = [sb(f"o{p}", [128, 1024], F32) for p in range(2)]
    xres = [sb(f"xr{p}", [128, 1024], F32) for p in range(2)]

    ps = ctx["ps"]
    mm = ctx["mm_sem"]; lnv = ctx["ln_v"]; lns = ctx["ln_s"]
    cast = mksem(name + "cast")
    wld = mksem(name + "wld")
    qld = mksem(name + "qld")
    xta = [mksem(name + f"xta{p}") for p in range(2)]
    xtb = [mksem(name + f"xtb{p}") for p in range(2)]
    res = mksem(name + "res")
    stm = [mksem(name + f"st{p}") for p in range(2)]
    bb = Sem(nc, stack, name + "bb")

    slab = T // 4
    castsl = [mksem(name + f"cs{i}") for i in range(4)]
    for i in range(4):
        castsl[i].inc(16)
        plan.add("gpsimd", lambda e, i=i, slab=slab: e.dma_start(
            x2bf[i * slab:(i + 1) * slab, :], x2[i * slab:(i + 1) * slab, :]).then_inc(castsl[i].h, 16))
    if ext_casts is None:
        for i in range(3):
            cast.inc(16)
            plan.add("gpsimd", lambda e, i=i: e.dma_start(
                wqkvbf[i * D:(i + 1) * D, :], wqkv[i * D:(i + 1) * D, :]).then_inc(cast.h, 16))
        cast.inc(16)
        plan.add("gpsimd", lambda e: e.dma_start(woutbf, wout).then_inc(cast.h, 16))
        wsem, wval = cast, cast.v
    else:
        wsem, wval = ext_casts["c3"]
    cast_all = cast.v

    wait(plan, "sync", wsem, wval)
    for c in range(DC):
        wld.inc(32)
        plan.add("sync", lambda e, c=c: e.dma_start(
            kvw[c][0:128, 0:2048], wqkvbf[D:3 * D, c * 128:(c + 1) * 128],
            transpose=True).then_inc(wld.h, 16))
        plan.add("sync", lambda e, c=c: e.dma_start(
            woT[c][:], woutbf[:, c * 128:(c + 1) * 128], transpose=True).then_inc(wld.h, 16))
    wld_all = wld.v

    emit_bias_broadcast(nc, plan, ctx, bout, bbc, D, bb)
    bb_done = bb.v

    for c in range(DC):
        lnv.inc()
        plan.add("vector", lambda e, c=c: e.memset(kvacc[c][:], 0.0).then_inc(lnv.h, 1))
    kvz_done = lnv.v

    wait(plan, "tensor", wld, wld_all)
    wait(plan, "tensor", ctx["init_sem"], ctx["init_ones"])
    xta_free = {}
    evac_of = {}
    eps_of = {}
    kvmm_of = {}
    kvadd_of = {}
    projmm_of = {}
    NG = NT // 4
    for g in range(NG):
        for i in range(4):
            nt = 4 * g + i
            pb = nt % 2
            if nt >= 2:
                wait(plan, "sync", mm, xta_free[nt - 2])
            if nt % (NT // 4) == 0:
                wait(plan, "sync", castsl[nt // (NT // 4)], 16)
            xv = xta[pb].inc(16 * DC)
            for c in range(DC):
                plan.add("sync", lambda e, nt=nt, c=c, pb=pb: e.dma_start(
                    x2Ta[pb][0:128, c * 128:(c + 1) * 128],
                    x2bf[nt * 128:(nt + 1) * 128, c * 128:(c + 1) * 128],
                    transpose=True).then_inc(xta[pb].h, 16))
            wait(plan, "tensor", xta[pb], xv)
            if nt >= 1:
                wait(plan, "tensor", lns, evac_of[nt - 1])
            if g >= 1 and i == 0:
                wait(plan, "tensor", lnv, kvadd_of[g - 1])
            for c in range(DC):
                st_ = (c == 0)
                sp_ = (c == DC - 1)
                for tgt, col in ((ps[0], 0), (ps[1], 512), (ps[2], 1024), (ps[3], 1536)):
                    plan.add("tensor", lambda e, c=c, st_=st_, sp_=sp_, pb=pb, tgt=tgt, col=col:
                             e.matmul(tgt[:, 0:512],
                                      x2Ta[pb][0:128, c * 128:(c + 1) * 128],
                                      kvw[c][0:128, col:col + 512],
                                      start=st_, stop=sp_))
            mmk = pe_mark(plan, mm)
            xta_free[nt] = mmk
            projmm_of[nt] = mmk
            wait(plan, "scalar", mm, mmk)
            if g >= 1:
                wait(plan, "scalar", mm, kvmm_of[g - 1])
            for cc in range(2):
                lns.inc()
                plan.add("scalar", lambda e, cc=cc, i=i: e.activation(
                    kbf[i][:, cc * 512:cc * 512 + 512], ps[0 + cc][:, 0:512],
                    AF.Relu).then_inc(lns.h, 1))
                lns.inc()
                plan.add("scalar", lambda e, cc=cc, i=i: e.copy(
                    vbf[i][:, cc * 512:cc * 512 + 512], ps[2 + cc][:, 0:512]).then_inc(lns.h, 1))
            evac_of[nt] = lns.v
            wait(plan, "vector", lns, evac_of[nt])
            lnv.inc()
            plan.add("vector", lambda e, i=i: e.tensor_scalar_add(
                kbf[i][:], kbf[i][:], EPS).then_inc(lnv.h, 1))
            eps_of[nt] = lnv.v
        wait(plan, "tensor", lnv, eps_of[4 * g + 3])
        if g == 0:
            wait(plan, "tensor", lnv, kvz_done)
        kvadd_prev = kvadd_of.get(g - 1, 0)
        add_targets = []
        for dc_ in range(DC):
            pair = (ps[4], ps[5]) if dc_ % 2 == 0 else (ps[2], ps[3])
            if dc_ >= 2:
                wait(plan, "tensor", lnv, add_targets[dc_ - 2])
            for i in range(4):
                nt = 4 * g + i
                st_ = (i == 0)
                sp_ = (i == 3)
                for eh in range(2):
                    plan.add("tensor", lambda e, dc_=dc_, i=i, eh=eh, st_=st_, sp_=sp_, pair=pair:
                             e.matmul(pair[eh][:, 0:512],
                                      kbf[i][0:128, dc_ * 128:(dc_ + 1) * 128],
                                      vbf[i][0:128, eh * 512:eh * 512 + 512],
                                      start=st_, stop=sp_))
                plan.add("tensor", lambda e, dc_=dc_, i=i, g=g, st_=st_, sp_=sp_:
                         e.matmul(ps[7][0:128, dc_:dc_ + 1],
                                  kbf[i][0:128, dc_ * 128:(dc_ + 1) * 128],
                                  ctx["ones128"][0:128, 0:1],
                                  start=(g == 0 and i == 0), stop=(g == NG - 1 and i == 3)))
            mmv = pe_mark(plan, mm)
            wait(plan, "vector", mm, mmv)
            for eh in range(2):
                lnv.inc()
                plan.add("vector", lambda e, dc_=dc_, eh=eh, pair=pair: e.tensor_add(
                    kvacc[dc_][:, eh * 512:eh * 512 + 512],
                    kvacc[dc_][:, eh * 512:eh * 512 + 512],
                    pair[eh][:, 0:512]).then_inc(lnv.h, 1))
            add_targets.append(lnv.v)
        kvmm_of[g] = mm.v
        kvadd_of[g] = lnv.v

    wait(plan, "scalar", lnv, kvadd_of[NG - 1])
    wait(plan, "scalar", mm, mm.v)
    for c in range(DC):
        lns.inc()
        plan.add("scalar", lambda e, c=c: e.copy(kvbf[c][:], kvacc[c][:]).then_inc(lns.h, 1))
    lns.inc()
    plan.add("scalar", lambda e: e.copy(ksumT[:], ps[7][0:128, 0:8]).then_inc(lns.h, 1))
    kvbf_done = lns.v

    wait(plan, "sync", mm, kvmm_of[NG - 1])
    for c in range(DC):
        qld.inc(16)
        plan.add("sync", lambda e, c=c: e.dma_start(
            kvw[c][0:128, 0:1024], wqkvbf[0:D, c * 128:(c + 1) * 128],
            transpose=True).then_inc(qld.h, 16))
    qld_all = qld.v

    wait(plan, "tensor", qld, qld_all)
    wait(plan, "tensor", lns, kvbf_done)
    qproj_free = {}
    ln_of = {}
    st_of = {}
    tile_idx = 0
    z_done_of = {}
    qready_of = {}
    bprojmm = {0: 0}

    def emit_A(ns):
        if ns >= NS:
            return
        pb = ns % 2
        if ns >= 2:
            wait(plan, "sync", mm, qproj_free[ns - 2])
        for i_ in range(min(4, -(-((ns + 1) * 512 * 4) // T))):
            wait(plan, "sync", castsl[i_], 16)
        xv = xtb[pb].inc(16 * DC)
        for c in range(DC):
            plan.add("sync", lambda e, ns=ns, c=c, pb=pb: e.dma_start(
                x2Tb[pb][0:128, c * 512:(c + 1) * 512],
                x2bf[ns * 512:(ns + 1) * 512, c * 128:(c + 1) * 128],
                transpose=True).then_inc(xtb[pb].h, 16))
        wait(plan, "tensor", xtb[pb], xv)
        wait(plan, "tensor", lns, ctx.get(name + "qrelu", 0))
        qevac = {}
        for dq in range(DC):
            bank = ps[dq % 2]
            if dq >= 2:
                wait(plan, "tensor", lnv, qevac[dq - 2])
            for c in range(DC):
                plan.add("tensor", lambda e, dq=dq, c=c, pb=pb, bank=bank:
                         e.matmul(bank[:, 0:512],
                                  kvw[c][0:128, dq * 128:(dq + 1) * 128],
                                  x2Tb[pb][0:128, c * 512:(c + 1) * 512],
                                  start=(c == 0), stop=(c == DC - 1)))
            mmq = pe_mark(plan, mm)
            wait(plan, "scalar", mm, mmq)
            wait(plan, "scalar", mm, bprojmm.get(ns - 1, 0))
            lns.inc()
            plan.add("scalar", lambda e, dq=dq, pb=pb, bank=bank: e.activation(
                qT[pb][dq][:], bank[:, 0:512], AF.Relu).then_inc(lns.h, 1))
            wait(plan, "vector", lns, lns.v)
            lnv.inc()
            plan.add("vector", lambda e, dq=dq, pb=pb: e.tensor_scalar_add(
                qT[pb][dq][:], qT[pb][dq][:], EPS).then_inc(lnv.h, 1))
            qevac[dq] = lnv.v
        ctx[name + "qrelu"] = lns.v
        qproj_free[ns] = mm.v
        qready_of[ns] = lnv.v

    def emit_B(ns):
        nonlocal tile_idx
        pb = ns % 2
        wait(plan, "tensor", lnv, qready_of[ns])
        wait(plan, "tensor", lnv, ln_of.get(tile_idx - 1, 0))
        for nsub in range(4):
            for c in range(DC):
                plan.add("tensor", lambda e, nsub=nsub, c=c, pb=pb:
                         e.matmul(ps[2][0:128, nsub:nsub + 1],
                                  qT[pb][c][0:128, nsub * 128:(nsub + 1) * 128],
                                  ksumT[0:128, c:c + 1],
                                  start=(c == 0), stop=(c == DC - 1)))
        mmz = pe_mark(plan, mm)
        wait(plan, "vector", mm, mmz)
        wait(plan, "vector", lnv, lnv.v)
        lnv.inc()
        plan.add("vector", lambda e: e.tensor_scalar_add(
            zsb[:, 0:4], ps[2][0:128, 0:4], EPS).then_inc(lnv.h, 1))
        wait(plan, "vector", lnv, lnv.v)
        lnv.inc()
        plan.add("vector", lambda e: e.reciprocal(
            zsb[:, 4:8], zsb[:, 0:4]).then_inc(lnv.h, 1))
        z_done = lnv.v
        z_done_of[ns] = z_done
        aevac = {}
        for et in range(DC):
            bank = ps[3 + et % 2]
            if et >= 2:
                wait(plan, "tensor", lns, aevac[et - 2])
            for c in range(DC):
                plan.add("tensor", lambda e, et=et, c=c, pb=pb, bank=bank:
                         e.matmul(bank[:, 0:512],
                                  kvbf[c][0:128, et * 128:(et + 1) * 128],
                                  qT[pb][c][0:128, 0:512],
                                  start=(c == 0), stop=(c == DC - 1)))
            mma = pe_mark(plan, mm)
            wait(plan, "scalar", mm, mma)
            wait(plan, "scalar", mm, bprojmm.get(ns, 0))
            lns.inc()
            plan.add("scalar", lambda e, et=et, bank=bank: e.copy(
                aT[et][:], bank[:, 0:512]).then_inc(lns.h, 1))
            aevac[et] = lns.v
        a_ready = lns.v
        wait(plan, "tensor", lns, a_ready)
        for nsub in range(4):
            pb2 = tile_idx % 2
            pbanks = (ps[5], ps[6]) if nsub % 2 == 0 else (ps[7], ps[2])
            if nsub % 2 == 1:
                wait(plan, "tensor", lnv, z_done)
            if tile_idx >= 2:
                wait(plan, "tensor", lnv, ln_of[tile_idx - 2])
            for et in range(DC):
                st_ = (et == 0)
                sp_ = (et == DC - 1)
                for cc in range(2):
                    plan.add("tensor", lambda e, nsub=nsub, cc=cc, et=et, st_=st_, sp_=sp_, pbanks=pbanks:
                             e.matmul(pbanks[cc][:, 0:512],
                                      aT[et][0:128, nsub * 128:(nsub + 1) * 128],
                                      woT[et][0:128, cc * 512:cc * 512 + 512],
                                      start=st_, stop=sp_))
            mmp = pe_mark(plan, mm)
            rv = res.inc(16)
            plan.add("gpsimd", lambda e, ns=ns, nsub=nsub, pb2=pb2: e.dma_start(
                xres[pb2][:], x2[ns * 512 + nsub * 128:ns * 512 + (nsub + 1) * 128, :]
            ).then_inc(res.h, 16))
            wait(plan, "vector", mm, mmp)
            wait(plan, "vector", bb, bb_done)
            wait(plan, "vector", lnv, z_done)
            wait(plan, "vector", res, rv)
            for cc in range(2):
                lnv.inc()
                plan.add("vector", lambda e, cc=cc, nsub=nsub, pb2=pb2, pbanks=pbanks:
                         e.scalar_tensor_tensor(
                             ysb[pb2][:, cc * 512:cc * 512 + 512], pbanks[cc][:, 0:512],
                             zsb[:, 4 + nsub:5 + nsub], bbc[:, cc * 512:cc * 512 + 512],
                             ALU.mult, ALU.add).then_inc(lnv.h, 1))
            wait(plan, "vector", lnv, lnv.v)
            vv = lnv.inc()
            plan.add("vector", lambda e, pb2=pb2: e.tensor_add(
                ysb[pb2][:], ysb[pb2][:], xres[pb2][:]).then_inc(lnv.h, 1))
            wait(plan, "scalar", lnv, vv)
            if tile_idx >= 2:
                wait(plan, "vector", stm[pb2], st_of[tile_idx - 2])
            lnfin = emit_ln_epilogue(nc, plan, ctx, ysb[pb2], osb[pb2])
            ln_of[tile_idx] = lnfin
            wait(plan, "gpsimd", lnv, lnfin)
            sv = stm[pb2].inc(16)
            st_of[tile_idx] = sv
            plan.add("gpsimd", lambda e, ns=ns, nsub=nsub, pb2=pb2: e.dma_start(
                x3[ns * 512 + nsub * 128:ns * 512 + (nsub + 1) * 128, :],
                osb[pb2][:]).then_inc(stm[pb2].h, 16))
            tile_idx += 1
        bprojmm[ns + 1] = mm.v

    emit_A(0)
    for ns in range(NS):
        emit_A(ns + 1)
        emit_B(ns)
    wait(plan, "gpsimd", stm[0], stm[0].v)
    wait(plan, "gpsimd", stm[1], stm[1].v)
    return


    wait(plan, "gpsimd", stm[1], stm[1].v)
    return


def _dft_consts():
    j = np.arange(D)
    ang = 2.0 * np.pi * np.outer(j, j) / D
    fr = np.cos(ang).astype(np.float32)
    fi = (-np.sin(ang)).astype(np.float32)
    gr = (np.cos(ang) / D).astype(np.float32)
    gi = (np.sin(ang) / D).astype(np.float32)
    return fr, fi, gr, gi


def _run_stage4(x3, wfr, wfi, bias):
    fr, fi, gr, gi = _dft_consts()
    nc = build_stage4()
    in_maps = []
    for b in range(x3.shape[0]):
        in_maps.append(dict(x3=np.ascontiguousarray(x3[b]), wfr=wfr, wfi=wfi,
                            bias=bias, fr=fr, fi=fi, fineg=-fi, gr=gr, gineg=-gi))
    res = run_bass_kernel_spmd(nc, in_maps, core_ids=list(range(x3.shape[0])))
    return np.stack([r["out"] for r in res.results], 0)


def _mk_masks():
    import ml_dtypes
    m = np.zeros((128, 2048), dtype=np.float32)
    for ri, r in enumerate((0, 128, 256, 384)):
        k = np.arange(128)[:, None]
        q = np.arange(512)[None, :]
        m[:, ri * 512:(ri + 1) * 512] = ((k + r) <= q).astype(np.float32)
    return m.astype(ml_dtypes.bfloat16)


def _run_stage1(x, wqkv, wproj, bproj):
    nc = build_stage1()
    masks = _mk_masks()
    in_maps = [dict(x=np.ascontiguousarray(x[b]), wqkv=wqkv, wproj=wproj,
                    bproj=bproj, masks=masks) for b in range(x.shape[0])]
    res = run_bass_kernel_spmd(nc, in_maps, core_ids=list(range(x.shape[0])))
    return np.stack([r["x1"] for r in res.results], 0)


def _run_stage3(x2, wqkv, wout, bout):
    nc = build_stage3()
    in_maps = [dict(x2=np.ascontiguousarray(x2[b]), wqkv=wqkv, wout=wout, bout=bout)
               for b in range(x2.shape[0])]
    res = run_bass_kernel_spmd(nc, in_maps, core_ids=list(range(x2.shape[0])))
    return np.stack([r["x3"] for r in res.results], 0)


POOL_BYTES = 201728


def build_fused():
    nc = bass.Bass("TRN2", target_bir_lowering=False)
    x = nc.dram_tensor("x", [T, D], F32, kind="ExternalInput").ap()
    wqkv0 = nc.dram_tensor("wqkv0", [D, 3 * D], F32, kind="ExternalInput").ap()
    wproj0 = nc.dram_tensor("wproj0", [D, D], F32, kind="ExternalInput").ap()
    bproj0 = nc.dram_tensor("bproj0", [D], F32, kind="ExternalInput").ap()
    masks = nc.dram_tensor("masks", [128, 2048], BF16, kind="ExternalInput").ap()
    wrt = nc.dram_tensor("wrt", [D, 8], F32, kind="ExternalInput").ap()
    w1 = nc.dram_tensor("w1", [8 * 4096, D], F32, kind="ExternalInput").ap()
    w2 = nc.dram_tensor("w2", [8 * D, 2048], F32, kind="ExternalInput").ap()
    wqkv1 = nc.dram_tensor("wqkv1", [3 * D, D], F32, kind="ExternalInput").ap()
    wout1 = nc.dram_tensor("wout1", [D, D], F32, kind="ExternalInput").ap()
    bout1 = nc.dram_tensor("bout1", [D], F32, kind="ExternalInput").ap()
    wfr = nc.dram_tensor("wfr", [D, D], F32, kind="ExternalInput").ap()
    wfi = nc.dram_tensor("wfi", [D, D], F32, kind="ExternalInput").ap()
    fbias = nc.dram_tensor("fbias", [D], F32, kind="ExternalInput").ap()
    fr = nc.dram_tensor("fr", [D, D], F32, kind="ExternalInput").ap()
    fi = nc.dram_tensor("fi", [D, D], F32, kind="ExternalInput").ap()
    fineg = nc.dram_tensor("fineg", [D, D], F32, kind="ExternalInput").ap()
    gr = nc.dram_tensor("gr", [D, D], F32, kind="ExternalInput").ap()
    gineg = nc.dram_tensor("gineg", [D, D], F32, kind="ExternalInput").ap()
    out = nc.dram_tensor("out", [T, D], F32, kind="ExternalOutput").ap()
    x1 = nc.dram_tensor("x1", [T, D], F32, kind="Internal").ap()
    x2 = nc.dram_tensor("x2", [T, D], F32, kind="Internal").ap()
    x3 = nc.dram_tensor("x3", [T, D], F32, kind="Internal").ap()
    xbf = nc.dram_tensor("xbf", [T, D], BF16, kind="Internal").ap()
    ahbm = nc.dram_tensor("ahbm", [D, T], BF16, kind="Internal").ap()
    x1bf = nc.dram_tensor("x1bf", [T, D], BF16, kind="Internal").ap()
    w1bf = nc.dram_tensor("w1bf", [8 * 4096, D], BF16, kind="Internal").ap()
    w2bf = nc.dram_tensor("w2bf", [8 * D, 2048], BF16, kind="Internal").ap()
    x2bf = nc.dram_tensor("x2bf", [T, D], BF16, kind="Internal").ap()
    wqkvbf = nc.dram_tensor("wqkvbf", [3 * D, D], BF16, kind="Internal").ap()
    woutbf = nc.dram_tensor("woutbf", [D, D], BF16, kind="Internal").ap()
    x3bf = nc.dram_tensor("x3bf", [T, D], BF16, kind="Internal").ap()

    import contextlib
    with contextlib.ExitStack() as stack:
        pool = stack.enter_context(nc.sbuf_tensor("pool", [128, POOL_BYTES], mybir.dt.uint8))
        plan = Plan()
        ctx = mk_common(nc, stack, plan, pool=pool)
        ext = emit_weight_precasts(nc, plan, ctx, w1, w2, w1bf, w2bf,
                                   wqkv1, wqkvbf, wout1, woutbf)
        ctx["precast_chunks"] = list(ext["chunks"])
        emit_stage1(nc, stack, plan, ctx, x, xbf, wqkv0, wproj0, bproj0,
                    masks, ahbm, x1)
        assert not ctx["precast_chunks"]
        emit_barrier(nc, plan, ctx)
        ctx["pool_reset"]()
        emit_stage2(nc, stack, plan, ctx, x1, wrt, w1, w2, x1bf, w1bf, w2bf,
                    x2, ext_casts=ext)
        emit_barrier(nc, plan, ctx)
        ctx["pool_reset"]()
        emit_stage3(nc, stack, plan, ctx, x2, x2bf, wqkv1, wqkvbf, wout1,
                    woutbf, bout1, x3, ext_casts=ext)
        emit_barrier(nc, plan, ctx)
        ctx["pool_reset"]()
        emit_stage4(nc, stack, plan, ctx, x3, x3bf, wfr, wfi, fbias,
                    fr, fi, fineg, gr, gineg, out)
        plan.run(nc)
    return nc


def kernel(L0_attn_x, L0_attn_w_qkv, L0_attn_w_proj, L0_attn_b_proj,
           L0_ffn_Wr, L0_ffn_W1, L0_ffn_W2,
           L1_attn_W_qkv, L1_attn_W_out, L1_attn_b_out,
           L1_ffn_w_freq_real, L1_ffn_w_freq_imag, L1_ffn_bias):
    B = L0_attn_x.shape[0]
    cores = list(range(B))
    x = np.ascontiguousarray(L0_attn_x, dtype=np.float32)
    masks = _mk_masks()
    fr, fi, gr, gi = _dft_consts()
    w1r = np.ascontiguousarray(L0_ffn_W1.reshape(8 * 4096, D))
    w2r = np.ascontiguousarray(L0_ffn_W2.reshape(8 * D, 2048))
    wrt = np.ascontiguousarray(L0_ffn_Wr.T)

    nc = build_fused()
    im = [dict(x=np.ascontiguousarray(x[b]), wqkv0=L0_attn_w_qkv,
               wproj0=L0_attn_w_proj, bproj0=L0_attn_b_proj, masks=masks,
               wrt=wrt, w1=w1r, w2=w2r,
               wqkv1=L1_attn_W_qkv, wout1=L1_attn_W_out, bout1=L1_attn_b_out,
               wfr=L1_ffn_w_freq_real, wfi=L1_ffn_w_freq_imag,
               fbias=L1_ffn_bias, fr=fr, fi=fi, fineg=-fi, gr=gr, gineg=-gi)
          for b in cores]
    res = run_bass_kernel_spmd(nc, im, core_ids=cores)
    out = np.stack([res.results[b]["out"] for b in cores], 0)
    return out.astype(np.float32)


def kernel_unfused(L0_attn_x, L0_attn_w_qkv, L0_attn_w_proj, L0_attn_b_proj,
           L0_ffn_Wr, L0_ffn_W1, L0_ffn_W2,
           L1_attn_W_qkv, L1_attn_W_out, L1_attn_b_out,
           L1_ffn_w_freq_real, L1_ffn_w_freq_imag, L1_ffn_bias):
    B = L0_attn_x.shape[0]
    cores = list(range(B))
    x = np.ascontiguousarray(L0_attn_x, dtype=np.float32)
    masks = _mk_masks()
    fr, fi, gr, gi = _dft_consts()

    nc1 = build_stage1()
    im = [dict(x=np.ascontiguousarray(x[b]), wqkv=L0_attn_w_qkv,
               wproj=L0_attn_w_proj, bproj=L0_attn_b_proj, masks=masks)
          for b in cores]
    r1 = run_bass_kernel_spmd(nc1, im, core_ids=cores)
    x1 = [r1.results[b]["x1"] for b in cores]

    nc2 = build_stage2()
    w1r = np.ascontiguousarray(L0_ffn_W1.reshape(8 * 4096, D))
    w2r = np.ascontiguousarray(L0_ffn_W2.reshape(8 * D, 2048))
    wrt = np.ascontiguousarray(L0_ffn_Wr.T)
    im = [dict(x1=x1[b], wrt=wrt, w1=w1r, w2=w2r) for b in cores]
    r2 = run_bass_kernel_spmd(nc2, im, core_ids=cores)
    x2 = [r2.results[b]["x2"] for b in cores]

    nc3 = build_stage3()
    im = [dict(x2=x2[b], wqkv=L1_attn_W_qkv, wout=L1_attn_W_out,
               bout=L1_attn_b_out) for b in cores]
    r3 = run_bass_kernel_spmd(nc3, im, core_ids=cores)
    x3 = [r3.results[b]["x3"] for b in cores]

    nc4 = build_stage4()
    im = [dict(x3=x3[b], wfr=L1_ffn_w_freq_real, wfi=L1_ffn_w_freq_imag,
               bias=L1_ffn_bias, fr=fr, fi=fi, fineg=-fi, gr=gr, gineg=-gi)
          for b in cores]
    r4 = run_bass_kernel_spmd(nc4, im, core_ids=cores)
    out = np.stack([r4.results[b]["out"] for b in cores], 0)
    return out.astype(np.float32)


# version 22
# speedup vs baseline: 1.1769x; 1.0079x over previous
import math
import numpy as np
import concourse.bass as bass
import concourse.mybir as mybir
from concourse.bass_utils import run_bass_kernel_spmd

F32 = mybir.dt.float32
BF16 = mybir.dt.bfloat16
AF = mybir.ActivationFunctionType
ALU = mybir.AluOpType
AX = mybir.AxisListType

T = 4096
D = 1024
NT = T // 128
DC = D // 128


class Sem:
    def __init__(self, nc, stack, name):
        self.h = stack.enter_context(nc.semaphore(name))
        self.v = 0

    def inc(self, d=1):
        self.v += d
        return self.v


class Plan:
    ENGINES = ("tensor", "scalar", "vector", "gpsimd", "sync")

    def __init__(self):
        self.q = {e: [] for e in self.ENGINES}

    def add(self, eng, fn):
        self.q[eng].append(fn)

    def run(self, nc):
        with nc.Block() as block:
            @block.tensor
            def _(e):
                for f in self.q["tensor"]:
                    f(e)

            @block.scalar
            def _(e):
                for f in self.q["scalar"]:
                    f(e)

            @block.vector
            def _(e):
                for f in self.q["vector"]:
                    f(e)

            @block.gpsimd
            def _(e):
                for f in self.q["gpsimd"]:
                    f(e)

            @block.sync
            def _(e):
                for f in self.q["sync"]:
                    f(e)


def wait(plan, eng, sem, val):
    plan.add(eng, lambda e, s=sem.h, v=val: e.wait_ge(s, v))


def _ap(x):
    a = getattr(x, "ap", None)
    return x.ap() if callable(a) else x


def pe_mark(plan, mm):
    v = mm.inc()
    last = plan.q["tensor"].pop()
    plan.add("tensor", lambda e, f=last, h=mm.h: f(e).then_inc(h, 1))
    return v


def emit_derive_xT(nc, plan, ctx, x_f32_hbm, xbf_hbm, xT_tiles, done_sem,
                   dma_eng="gpsimd", tr_eng="sync", pre_wait=None):
    cast = ctx["mksem"]("cast")
    x_ap = _ap(x_f32_hbm)
    xbf_ap = _ap(xbf_hbm)
    if pre_wait is not None:
        wait(plan, dma_eng, *pre_wait)
    slab = T // 4
    for i in range(4):
        v = cast.inc(16)
        plan.add(dma_eng, lambda e, i=i, v=v, slab=slab: e.dma_start(
            xbf_ap[i * slab:(i + 1) * slab, :], x_ap[i * slab:(i + 1) * slab, :]
        ).then_inc(cast.h, 16))
    cast_done = cast.v
    wait(plan, tr_eng, cast, cast_done)
    vals = []
    for c in range(DC):
        v = done_sem.inc(16)
        plan.add(tr_eng, lambda e, c=c, v=v: e.dma_start(
            xT_tiles[c][0:128, 0:T], xbf_ap[:, c * 128:(c + 1) * 128], transpose=True
        ).then_inc(done_sem.h, 16))
        vals.append(v)
    return vals


def emit_bias_broadcast(nc, plan, ctx, b_hbm, bbc_sb, ncols, ready_sem):
    brow = ctx["brow_sb"]
    ones1 = ctx["ones1"]
    ld = ctx["mksem"]("brow")
    mm = ctx["mm_sem"]
    v = ld.inc(16)
    plan.add("gpsimd", lambda e, v=v: e.dma_start(
        brow[0:1, 0:ncols], _ap(b_hbm).rearrange("(o d) -> o d", o=1)).then_inc(ld.h, 16))
    wait(plan, "tensor", ld, v)
    wait(plan, "tensor", ctx["init_sem"], ctx["init_ones"])
    ps = ctx["ps"]
    for j in range(ncols // 512):
        plan.add("tensor", lambda e, j=j: e.matmul(
            ps[4][:, 0:512] if j == 0 else ps[5][:, 0:512],
            ones1[0:1, :], brow[0:1, j * 512:(j + 1) * 512],
            start=True, stop=True))
    pe_mark(plan, mm)
    wait(plan, "scalar", mm, mm.v)
    for j in range(ncols // 512):
        rv = ready_sem.inc()
        plan.add("scalar", lambda e, j=j, rv=rv: e.copy(
            bbc_sb[:, j * 512:(j + 1) * 512],
            (ctx["ps"][4] if j == 0 else ctx["ps"][5])[:, 0:512]
        ).then_inc(ready_sem.h, 1))


def emit_ln_epilogue(nc, plan, ctx, y_sb, x1_sb, eps=1e-5):
    lnv = ctx["ln_v"]; lns = ctx["ln_s"]
    par = ctx["ln_par"][0]
    ctx["ln_par"][0] = 1 - par
    st = ctx["ln_st"][par]
    mean = st[:, 12:13]; var = st[:, 13:14]
    lnt = st[:, 14:15]; rs = st[:, 15:16]; nmrs = st[:, 16:17]
    ps_ = ctx.get(f"ln_prev_s{par}", 0)
    pv_ = ctx.get(f"ln_prev_f{par}", 0)
    if ps_:
        wait(plan, "vector", lns, ps_)
    wait(plan, "vector", lnv, max(pv_, lnv.v))
    for half in range(2):
        lnv.inc()
        plan.add("vector", lambda e, half=half: e.bn_stats(
            st[:, 6 * half:6 * half + 6],
            y_sb[:, 512 * half:512 * half + 512]).then_inc(lnv.h, 1))
    wait(plan, "vector", lnv, lnv.v)
    v2 = lnv.inc()
    plan.add("vector", lambda e: e.bn_aggr(st[:, 12:14], st[:, 0:12]).then_inc(lnv.h, 1))
    wait(plan, "scalar", lnv, v2)
    v2s = lns.inc()
    plan.add("scalar", lambda e: e.activation(lnt, var, AF.Ln, bias=ctx["eps_ln"][:]).then_inc(lns.h, 1))
    wait(plan, "scalar", lns, v2s)
    v3 = lns.inc()
    plan.add("scalar", lambda e: e.activation(rs, lnt, AF.Exp, scale=-0.5).then_inc(lns.h, 1))
    wait(plan, "vector", lns, v3)
    vd = lnv.inc()
    plan.add("vector", lambda e: e.scalar_tensor_tensor(
        nmrs, mean, -1.0, rs, ALU.mult, ALU.mult).then_inc(lnv.h, 1))
    wait(plan, "vector", lnv, vd)
    vf = lnv.inc()
    plan.add("vector", lambda e: e.tensor_scalar(
        x1_sb[:], y_sb[:], rs, nmrs, ALU.mult, ALU.add).then_inc(lnv.h, 1))
    ctx[f"ln_prev_s{par}"] = v3
    ctx[f"ln_prev_f{par}"] = vf
    return vf


def mk_common(nc, stack, plan, pool=None):
    ctx = {}
    ctx["_pool"] = pool
    ctx["_off"] = [0]
    ctx["dma_list"] = []

    def salloc(name_, shape, dtype):
        if pool is None:
            return stack.enter_context(nc.sbuf_tensor(name_, shape, dtype))
        nbytes = shape[1] * mybir.dt.size(dtype)
        nbytes = (nbytes + 63) // 64 * 64
        off = ctx["_off"][0]
        ctx["_off"][0] = off + nbytes
        assert ctx["_off"][0] <= pool.shape[1], (
            f"pool overflow: {name_} needs {nbytes}B at {off}, pool {pool.shape[1]}")
        return pool[0:shape[0], off:off + shape[1] * mybir.dt.size(dtype)].bitcast(dtype)

    ctx["salloc"] = salloc
    ctx["pool_reset"] = lambda: ctx["_off"].__setitem__(0, 0)
    ctx["psB"] = [stack.enter_context(nc.psum_tensor(f"psB{i}", [128, 1024], F32))
                  for i in range(4)]
    ctx["ps"] = [ctx["psB"][i // 2][:, (i % 2) * 512:(i % 2) * 512 + 512]
                 for i in range(8)]
    ctx["ones1"] = stack.enter_context(nc.sbuf_tensor("ones1", [1, 128], BF16))
    ctx["ones128"] = stack.enter_context(nc.sbuf_tensor("ones128", [128, 1], BF16))
    ctx["brow_sb"] = stack.enter_context(nc.sbuf_tensor("brow", [1, 1024], BF16))
    ctx["ln_st"] = [stack.enter_context(nc.sbuf_tensor(f"ln_st{i}", [128, 18], F32)) for i in range(2)]
    ctx["ln_par"] = [0]
    ctx["eps_ln"] = stack.enter_context(nc.sbuf_tensor("eps_ln", [128, 1], F32))
    for nm_ in ("ld_sem", "mm_sem", "ln_v", "ln_s", "init_sem"):
        ctx[nm_] = Sem(nc, stack, nm_)
    _n = [0]
    def _mksem(tag="s"):
        _n[0] += 1
        s = Sem(nc, stack, f"{tag}{_n[0]}")
        ctx["dma_list"].append(s)
        return s
    ctx["mksem"] = _mksem
    def _mksem_nobar(tag="s"):
        _n[0] += 1
        return Sem(nc, stack, f"{tag}{_n[0]}")
    ctx["mksem_nobar"] = _mksem_nobar
    iv = ctx["init_sem"].inc(1)
    plan.add("vector", lambda e: e.memset(ctx["ones1"][:], 1.0))
    plan.add("vector", lambda e: e.memset(ctx["ones128"][:], 1.0))
    plan.add("vector", lambda e, v=iv: e.memset(ctx["eps_ln"][:], 1e-5).then_inc(ctx["init_sem"].h, 1))
    ctx["init_ones"] = iv
    return ctx


def build_stage4():
    nc = bass.Bass("TRN2", target_bir_lowering=False)
    x3 = nc.dram_tensor("x3", [T, D], F32, kind="ExternalInput").ap()
    wfr = nc.dram_tensor("wfr", [D, D], F32, kind="ExternalInput").ap()
    wfi = nc.dram_tensor("wfi", [D, D], F32, kind="ExternalInput").ap()
    bias = nc.dram_tensor("bias", [D], F32, kind="ExternalInput").ap()
    fr = nc.dram_tensor("fr", [D, D], F32, kind="ExternalInput").ap()
    fi = nc.dram_tensor("fi", [D, D], F32, kind="ExternalInput").ap()
    fineg = nc.dram_tensor("fineg", [D, D], F32, kind="ExternalInput").ap()
    gr = nc.dram_tensor("gr", [D, D], F32, kind="ExternalInput").ap()
    gineg = nc.dram_tensor("gineg", [D, D], F32, kind="ExternalInput").ap()
    out = nc.dram_tensor("out", [T, D], F32, kind="ExternalOutput").ap()
    x3bf = nc.dram_tensor("x3bf", [T, D], BF16, kind="Internal").ap()

    import contextlib
    with contextlib.ExitStack() as stack:
        plan = Plan()
        ctx = mk_common(nc, stack, plan)
        emit_stage4(nc, stack, plan, ctx, x3, x3bf, wfr, wfi, bias,
                    fr, fi, fineg, gr, gineg, out)
        plan.run(nc)
    return nc


def emit_stage4(nc, stack, plan, ctx, x3, x3bf, wfr, wfi, bias,
                fr, fi, fineg, gr, gineg, out, name="s4"):
    sb = lambda n, shp, dt: ctx["salloc"](name + n, shp, dt)
    cA = [sb(f"cA{j}", [128, 4096], BF16) for j in range(DC)]
    wA = [sb(f"wA{j}", [128, 2048], BF16) for j in range(DC)]
    aT = [sb(f"aT{j}", [128, 2048], BF16) for j in range(DC)]
    mS = [sb(f"mS{j}", [128, 1024], BF16) for j in range(DC)]
    bbc = sb("bbc", [128, 1024], F32)
    ysb = [sb(f"y{j}", [128, 1024], F32) for j in range(2)]
    osb = [sb(f"o{j}", [128, 1024], F32) for j in range(2)]
    xres = [sb(f"xr{j}", [128, 1024], F32) for j in range(2)]

    mm = ctx["mm_sem"]
    ld = ctx["mksem"](name + "ld")
    gld = ctx["mksem"](name + "gld")
    stm = [ctx["mksem"](name + "stA"), ctx["mksem"](name + "stB")]
    ev = Sem(nc, stack, name + "ev")
    xt = Sem(nc, stack, name + "xt")
    bb = Sem(nc, stack, name + "bb")

    def cast_load(dst_tiles, src_ap, col0, sem, eng="gpsimd"):
        for j in range(DC):
            sem.inc(16)
            plan.add(eng, lambda e, j=j: e.dma_start(
                dst_tiles[j][:, col0:col0 + D], src_ap[j * 128:(j + 1) * 128, :]
            ).then_inc(sem.h, 16))

    cast_load(cA, fr, 0, ld)
    cast_load(cA, fi, 1024, ld)
    cast_load(cA, fineg, 2048, ld)
    cast_load(wA, wfr, 0, ld)
    cast_load(wA, wfi, 1024, ld)
    consts_done = ld.v

    emit_bias_broadcast(nc, plan, ctx, bias, bbc, D, bb)
    bb_done = bb.v

    wait(plan, "tensor", ld, consts_done)
    wait(plan, "tensor", ctx["init_sem"], ctx["init_ones"])
    s1_ev = []
    for kt in range(DC):
        for half, (t0, t1) in enumerate(((0, 2048), (1024, 0))):
            for dch in range(2):
                bank = ctx["ps"][(kt * 4 + half * 2 + dch) % 4]
                for j in range(DC):
                    if half == 0:
                        ops = [(wA[j][:, kt * 128:kt * 128 + 128], cA[j][:, dch * 512:dch * 512 + 512]),
                               (wA[j][:, 1024 + kt * 128:1024 + kt * 128 + 128], cA[j][:, 2048 + dch * 512:2048 + dch * 512 + 512])]
                    else:
                        ops = [(wA[j][:, 1024 + kt * 128:1024 + kt * 128 + 128], cA[j][:, dch * 512:dch * 512 + 512]),
                               (wA[j][:, kt * 128:kt * 128 + 128], cA[j][:, 1024 + dch * 512:1024 + dch * 512 + 512])]
                    for oi, (l_, r_) in enumerate(ops):
                        st_ = (j == 0 and oi == 0)
                        sp_ = (j == DC - 1 and oi == 1)
                        plan.add("tensor", lambda e, b=bank, l_=l_, r_=r_, st_=st_, sp_=sp_:
                                 e.matmul(b[:, 0:512], l_, r_, start=st_, stop=sp_))
                pe_mark(plan, mm)
                col0 = half * 1024 + dch * 512
                wait(plan, "scalar", mm, mm.v)
                v = ev.inc()
                plan.add("scalar", lambda e, kt=kt, col0=col0, b=bank, v=v:
                         e.copy(aT[kt][:, col0:col0 + 512], b[:, 0:512]).then_inc(ev.h, 1))
                s1_ev.append(v)
                if len(s1_ev) >= 4:
                    wait(plan, "tensor", ev, s1_ev[-4])
    s1_done = ev.v

    wait(plan, "gpsimd", mm, mm.v)
    cast_load(wA, gr, 0, gld)
    cast_load(wA, gineg, 1024, gld)
    g_all = gld.v

    wait(plan, "sync", mm, mm.v)
    emit_derive_xT(nc, plan, ctx, x3, x3bf, cA, xt, pre_wait=None)
    xt_all = xt.v

    wait(plan, "tensor", gld, g_all)
    wait(plan, "tensor", ev, s1_done)
    s2_ev = []
    for dt_ in range(DC):
        for och in range(2):
            bank = ctx["ps"][(dt_ * 2 + och) % 4]
            n_mm = 0
            for k in range(DC):
                for oi, (l_, r_) in enumerate((
                        (aT[k][:, dt_ * 128:dt_ * 128 + 128], wA[k][:, och * 512:och * 512 + 512]),
                        (aT[k][:, 1024 + dt_ * 128:1024 + dt_ * 128 + 128], wA[k][:, 1024 + och * 512:1024 + och * 512 + 512]))):
                    st_ = (k == 0 and oi == 0)
                    sp_ = (k == DC - 1 and oi == 1)
                    plan.add("tensor", lambda e, b=bank, l_=l_, r_=r_, st_=st_, sp_=sp_:
                             e.matmul(b[:, 0:512], l_, r_, start=st_, stop=sp_))
            pe_mark(plan, mm)
            wait(plan, "scalar", mm, mm.v)
            v = ev.inc()
            plan.add("scalar", lambda e, dt_=dt_, och=och, b=bank, v=v:
                     e.copy(mS[dt_][:, och * 512:och * 512 + 512], b[:, 0:512]).then_inc(ev.h, 1))
            s2_ev.append(v)
            if len(s2_ev) >= 4:
                wait(plan, "tensor", ev, s2_ev[-4])
    m_done = ev.v

    wait(plan, "tensor", ev, m_done)
    wait(plan, "tensor", xt, xt_all)
    wait(plan, "tensor", bb, bb_done)
    res_sem = [ctx["mksem"](name + "rA"), ctx["mksem"](name + "rB")]
    vv_of, ln_of, st_of = {}, {}, {}
    for nt in range(NT):
        pb = nt % 2
        banks = (ctx["ps"][4 + 2 * pb], ctx["ps"][5 + 2 * pb])
        if nt >= 2:
            wait(plan, "sync", ctx["ln_v"], vv_of[nt - 2])
        v = res_sem[pb].inc(16)
        plan.add("sync", lambda e, nt=nt, pb=pb, v=v: e.dma_start(
            xres[pb][:], x3[nt * 128:(nt + 1) * 128, :]).then_inc(res_sem[pb].h, 16))
        if nt >= 2:
            wait(plan, "tensor", ctx["ln_v"], vv_of[nt - 2])
        for j in range(DC):
            st_ = (j == 0)
            sp_ = (j == DC - 1)
            l_ = cA[j][:, nt * 128:(nt + 1) * 128]
            for och in range(2):
                r_ = mS[j][:, och * 512:och * 512 + 512]
                plan.add("tensor", lambda e, b=banks[och], l_=l_, r_=r_, st_=st_, sp_=sp_:
                         e.matmul(b[:, 0:512], l_, r_, start=st_, stop=sp_))
        mm_here = pe_mark(plan, mm)
        wait(plan, "vector", mm, mm_here)
        wait(plan, "vector", res_sem[pb], v)
        vpre = ctx["ln_v"].inc(2)
        for och in range(2):
            plan.add("vector", lambda e, b=banks[och], och=och, pb=pb:
                     e.tensor_add(ysb[pb][:, och * 512:och * 512 + 512], b[:, 0:512],
                                  bbc[:, och * 512:och * 512 + 512]).then_inc(ctx["ln_v"].h, 1))
        wait(plan, "vector", ctx["ln_v"], vpre)
        vv = ctx["ln_v"].inc()
        plan.add("vector", lambda e, pb=pb, v=vv: e.tensor_add(
            ysb[pb][:], ysb[pb][:], xres[pb][:]).then_inc(ctx["ln_v"].h, 1))
        vv_of[nt] = vv
        wait(plan, "scalar", ctx["ln_v"], vv)
        if nt >= 2:
            wait(plan, "vector", stm[pb], st_of[nt - 2])
        lnfin = emit_ln_epilogue(nc, plan, ctx, ysb[pb], osb[pb])
        ln_of[nt] = lnfin
        wait(plan, "gpsimd", ctx["ln_v"], lnfin)
        sv = stm[pb].inc(16)
        st_of[nt] = sv
        plan.add("gpsimd", lambda e, nt=nt, pb=pb: e.dma_start(
            out[nt * 128:(nt + 1) * 128, :], osb[pb][:]).then_inc(stm[pb].h, 16))
    wait(plan, "gpsimd", stm[0], stm[0].v)
    wait(plan, "gpsimd", stm[1], stm[1].v)
    return




def emit_barrier(nc, plan, ctx):
    for eng in Plan.ENGINES:
        wait(plan, eng, ctx["mm_sem"], ctx["mm_sem"].v)
        wait(plan, eng, ctx["ln_s"], ctx["ln_s"].v)
        wait(plan, eng, ctx["ln_v"], ctx["ln_v"].v)
        for s in ctx["dma_list"]:
            if s.v:
                wait(plan, eng, s, s.v)


def build_stage1():
    nc = bass.Bass("TRN2", target_bir_lowering=False)
    x = nc.dram_tensor("x", [T, D], F32, kind="ExternalInput").ap()
    wqkv = nc.dram_tensor("wqkv", [D, 3 * D], F32, kind="ExternalInput").ap()
    wproj = nc.dram_tensor("wproj", [D, D], F32, kind="ExternalInput").ap()
    bproj = nc.dram_tensor("bproj", [D], F32, kind="ExternalInput").ap()
    masks = nc.dram_tensor("masks", [128, 2048], BF16, kind="ExternalInput").ap()
    x1 = nc.dram_tensor("x1", [T, D], F32, kind="ExternalOutput").ap()
    xbf = nc.dram_tensor("xbf", [T, D], BF16, kind="Internal").ap()
    ahbm = nc.dram_tensor("ahbm", [D, T], BF16, kind="Internal").ap()
    rhbm = nc.dram_tensor("rhbm", [64, 1024], BF16, kind="Internal").ap()

    import contextlib
    with contextlib.ExitStack() as stack:
        plan = Plan()
        ctx = mk_common(nc, stack, plan)
        emit_stage1(nc, stack, plan, ctx, x, xbf, wqkv, wproj, bproj, masks,
                    ahbm, x1, rhbm)
        plan.run(nc)
    return nc


def emit_stage1(nc, stack, plan, ctx, x, xbf, wqkv, wproj, bproj, masks_h,
                ahbm, x1, rhbm=None, name="s1"):
    sb = lambda n, shp, dt: ctx["salloc"](name + n, shp, dt)
    mksem = ctx["mksem"]
    NQB = T // 512
    arena = [sb(f"ar{c}", [128, T], BF16) for c in range(DC)]
    wg = [sb(f"wg{c}", [128, 768], BF16) for c in range(DC)]
    qTg = [sb(f"qT{p}", [128, T], BF16) for p in range(2)]
    kTg = [sb(f"kT{p}", [128, T], BF16) for p in range(2)]
    vext = sb("vext", [128, NT * 4 * 65], BF16)
    pbuf = sb("pbuf", [128, 2048], BF16)
    cb = sb("cb", [128, 16], F32)
    nbs = sb("nbs", [1, 16], F32)
    nbs_bf = sb("nbsb", [1, 2], BF16)
    msk = sb("msk", [128, 2048], BF16)
    apair = [sb(f"ap{p}", [128, 512], BF16) for p in range(2)]
    rrow = sb("rrow", [1, 2048], BF16)
    rbc = [sb(f"rbc{p}", [128, 512], BF16) for p in range(2)]
    rstg = [sb(f"rstg{p}", [1, 1024], BF16) for p in range(2)]
    wpj = [sb(f"wp{c}", [128, 1024], BF16) for c in range(DC)]
    bbc = sb("bbc", [128, 1024], F32)
    ysb = [sb(f"y{p}", [128, 1024], F32) for p in range(2)]
    osb = [sb(f"o{p}", [128, 1024], F32) for p in range(2)]
    xres = [sb(f"xr{p}", [128, 1024], F32) for p in range(2)]

    ps = ctx["ps"]
    mm = ctx["mm_sem"]; lnv = ctx["ln_v"]; lns = ctx["ln_s"]
    xt = mksem(name + "xt")
    wld = mksem(name + "wld")
    pld = mksem(name + "pld")
    ast = [mksem(name + f"as{p}") for p in range(2)]
    atl = mksem(name + "atl")
    res2 = [mksem(name + "res2A"), mksem(name + "res2B")]
    stm = [mksem(name + f"st{p}") for p in range(2)]
    bb = Sem(nc, stack, name + "bb")

    emit_derive_xT(nc, plan, ctx, x, xbf, arena, xt)
    xt_all = xt.v
    pld.inc(16)
    plan.add("sync", lambda e: e.dma_start(msk[:], _ap(masks_h)).then_inc(pld.h, 16))
    pld_all = pld.v
    wpld = mksem(name + "wpld")

    def emit_wpj_loads():
        for c in range(DC):
            wpld.inc(16)
            plan.add("gpsimd", lambda e, c=c: e.dma_start(
                wpj[c][:], _ap(wproj)[c * 128:(c + 1) * 128, :]).then_inc(wpld.h, 16))
    emit_bias_broadcast(nc, plan, ctx, bproj, bbc, D, bb)
    bb_done = bb.v
    lnv.inc()
    plan.add("vector", lambda e: e.memset(
        vext[:].rearrange("p (s c) -> p s c", c=65)[:, :, 64:65], 1.0).then_inc(lnv.h, 1))
    vext_ones = lnv.v

    projmark = {}
    tile12 = 0
    st_of = {}
    for g in range(4):
        if g >= 1:
            wait(plan, "gpsimd", mm, projmark[g - 1])
        for c in range(DC):
            for oi, (src0, w_) in enumerate(((g * 256, 256), (D + g * 256, 256), (2 * D + g * 256, 256))):
                wld.inc(16)
                plan.add("gpsimd", lambda e, c=c, src0=src0, oi=oi: e.dma_start(
                    wg[c][:, oi * 256:(oi + 1) * 256],
                    _ap(wqkv)[c * 128:(c + 1) * 128, src0:src0 + 256]).then_inc(wld.h, 16))
        if g == 0:
            emit_wpj_loads()
        pcc = ctx.get("precast_chunks")
        if pcc:
            pcc.pop(0)()
        wld_here = wld.v
        wait(plan, "tensor", wld, wld_here)
        if g >= 1:
            wait(plan, "tensor", lns, lns.v)
            wait(plan, "tensor", lnv, lnv.v)
        if g == 0:
            wait(plan, "tensor", xt, xt_all)
            wait(plan, "tensor", ctx["init_sem"], ctx["init_ones"])
        evac = {}
        idx = 0
        for role, dstl, coloff, scale in (("q", qTg, 0, 0.125), ("k", kTg, 256, 1.0)):
            for pt in range(2):
                for nsb in range(T // 512):
                    bank = ps[idx % 2]
                    if idx >= 2:
                        wait(plan, "tensor", lns, evac[idx - 2])
                    for c in range(DC):
                        plan.add("tensor", lambda e, c=c, pt=pt, nsb=nsb, coloff=coloff, bank=bank:
                                 e.matmul(bank[:, 0:512],
                                          wg[c][0:128, coloff + pt * 128:coloff + (pt + 1) * 128],
                                          arena[c][0:128, nsb * 512:(nsb + 1) * 512],
                                          start=(c == 0), stop=(c == DC - 1)))
                    mk = pe_mark(plan, mm)
                    wait(plan, "scalar", mm, mk)
                    lns.inc()
                    plan.add("scalar", lambda e, dstl=dstl, pt=pt, nsb=nsb, bank=bank, scale=scale:
                             e.activation(dstl[pt][:, nsb * 512:(nsb + 1) * 512],
                                          bank[:, 0:512], AF.Copy, scale=float(scale)).then_inc(lns.h, 1))
                    evac[idx] = lns.v
                    idx += 1
        vidx = 0
        vevac = {}
        for nt in range(NT):
            bank = ps[2 + nt % 2]
            if vidx >= 2:
                wait(plan, "tensor", lns, vevac[vidx - 2])
            for c in range(DC):
                plan.add("tensor", lambda e, c=c, nt=nt, bank=bank:
                         e.matmul(bank[:, 0:256],
                                  arena[c][0:128, nt * 128:(nt + 1) * 128],
                                  wg[c][0:128, 512:768],
                                  start=(c == 0), stop=(c == DC - 1)))
            mk = pe_mark(plan, mm)
            wait(plan, "scalar", mm, mk)
            if g == 0 and nt == 0:
                wait(plan, "scalar", lnv, vext_ones)
            for hg in range(4):
                lns.inc()
                plan.add("scalar", lambda e, nt=nt, hg=hg, bank=bank: e.copy(
                    vext[:, (nt * 4 + hg) * 65:(nt * 4 + hg) * 65 + 64],
                    bank[:, hg * 64:(hg + 1) * 64]).then_inc(lns.h, 1))
            vevac[vidx] = lns.v
            vidx += 1
        projmark[g] = mm.v
        proj_evac_all = lns.v

        wait(plan, "scalar", lns, proj_evac_all)
        for pt in range(2):
            for sl, srcl in ((0, qTg), (1, kTg)):
                first = True
                for half in range(max(T // 2048, 1)):
                    w_ = min(2048, T)
                    wait(plan, "scalar", mm, ctx.get(name + "bndmm", 0))
                    sqmark = lns.inc()
                    plan.add("scalar", lambda e, pt=pt, srcl=srcl, half=half, w_=w_: e.activation(
                        pbuf[:, 0:w_], srcl[pt][:, half * w_:(half + 1) * w_],
                        AF.Square).then_inc(lns.h, 1))
                    wait(plan, "tensor", lns, sqmark)
                    if first:
                        wait(plan, "tensor", lnv, ctx.get(name + "bndv", 0))
                    nhalves = max(T // 2048, 1)
                    nch = w_ // 512
                    for hh in range(2):
                        for ch in range(nch):
                            last_ = (half == nhalves - 1 and ch == nch - 1)
                            plan.add("tensor", lambda e, hh=hh, ch=ch, first=first, last_=last_:
                                     e.matmul((ps[2] if hh == 0 else ps[7])[0:1, 0:512],
                                              ctx["ones128"][hh * 64:hh * 64 + 64, 0:1],
                                              pbuf[hh * 64:hh * 64 + 64, ch * 512:(ch + 1) * 512],
                                              start=(first and ch == 0), stop=last_))
                    first = False
                    mk = pe_mark(plan, mm)
                    ctx[name + "bndmm"] = mk
                wait(plan, "vector", mm, mk)
                for hh in range(2):
                    h4 = 2 * pt + hh
                    lnv.inc()
                    plan.add("vector", lambda e, h4=h4, sl=sl, hh=hh: e.reduce_max(
                        nbs[0:1, 2 * h4 + sl:2 * h4 + sl + 1],
                        (ps[2] if hh == 0 else ps[7])[0:1, 0:512], axis=AX.X).then_inc(lnv.h, 1))
                ctx[name + "bndv"] = lnv.v
            p_ = 2 * g + pt
            for hh in range(2):
                bidx = 2 * (2 * pt + hh)
                wait(plan, "vector", lnv, ctx[name + "bndv"])
                lnv.inc()
                plan.add("vector", lambda e, bidx=bidx: e.tensor_mul(
                    nbs[0:1, 8 + bidx:9 + bidx], nbs[0:1, bidx:bidx + 1],
                    nbs[0:1, bidx + 1:bidx + 2]).then_inc(lnv.h, 1))
            bidx0 = 2 * (2 * pt)
            wait(plan, "vector", lnv, lnv.v)
            lnv.inc()
            plan.add("vector", lambda e, bidx0=bidx0: e.tensor_max(
                nbs[0:1, 8 + bidx0:9 + bidx0], nbs[0:1, 8 + bidx0:9 + bidx0],
                nbs[0:1, 10 + bidx0:11 + bidx0]).then_inc(lnv.h, 1))
            wait(plan, "scalar", lnv, lnv.v)
            lns.inc()
            plan.add("scalar", lambda e, bidx0=bidx0: e.activation(
                nbs[0:1, 9 + bidx0:10 + bidx0], nbs[0:1, 8 + bidx0:9 + bidx0],
                AF.Ln).then_inc(lns.h, 1))
            wait(plan, "scalar", lns, lns.v)
            lns.inc()
            plan.add("scalar", lambda e, bidx0=bidx0: e.activation(
                nbs[0:1, 8 + bidx0:9 + bidx0], nbs[0:1, 9 + bidx0:10 + bidx0],
                AF.Exp, scale=0.5).then_inc(lns.h, 1))
            wait(plan, "vector", lns, lns.v)
            wait(plan, "vector", mm, mm.v)
            lnv.inc()
            plan.add("vector", lambda e, bidx0=bidx0: e.tensor_scalar_mul(
                nbs_bf[0:1, 0:1], nbs[0:1, 8 + bidx0:9 + bidx0], -1.0).then_inc(lnv.h, 1))
            wait(plan, "tensor", lnv, lnv.v)
            plan.add("tensor", lambda e: e.matmul(
                ps[7][0:128, 0:1], ctx["ones1"][0:1, :], nbs_bf[0:1, 0:1],
                start=True, stop=True))
            mk = pe_mark(plan, mm)
            wait(plan, "scalar", mm, mk)
            lns.inc()
            plan.add("scalar", lambda e, p_=p_: e.copy(
                cb[:, p_:p_ + 1], ps[7][0:128, 0:1]).then_inc(lns.h, 1))
        cb_done = lns.v

        wait(plan, "tensor", lns, cb_done)
        if g == 0:
            wait(plan, "tensor", pld, pld_all)
            wait(plan, "vector", pld, pld_all)
        for pt in range(2):
            pair = 2 * g + pt
            for qb in range(NQB):
                J = 4 * qb + 4
                expm = {}
                avm = {}
                maskm = {}
                pb = tile12 % 2
                wait(plan, "tensor", lns, ctx.get(name + "exp_last", 0))
                jpos = {}

                def emit_S(j):
                    ji = len(jpos)
                    jpos[j] = ji
                    SS = ctx["psB"][2 + ji % 2]
                    if ji >= 2:
                        wait(plan, "tensor", lns, expm[ji - 2])
                    plan.add("tensor", lambda e, pt=pt, j=j, qb=qb, SS=SS:
                             e.matmul(SS[:, 0:512], kTg[pt][0:64, j * 128:(j + 1) * 128],
                                      qTg[pt][0:64, qb * 512:(qb + 1) * 512],
                                      start=True, stop=True, tile_position=(0, 0)))
                    plan.add("tensor", lambda e, pt=pt, j=j, qb=qb, SS=SS:
                             e.matmul(SS[:, 512:1024], kTg[pt][64:128, j * 128:(j + 1) * 128],
                                      qTg[pt][64:128, qb * 512:(qb + 1) * 512],
                                      start=True, stop=True, tile_position=(64, 0)))
                    mmS[j] = pe_mark(plan, mm)

                def emit_expmask(j):
                    ji = jpos[j]
                    SS = ctx["psB"][2 + ji % 2]
                    wait(plan, "scalar", mm, mmS[j])
                    if ji >= 2:
                        wait(plan, "scalar", mm, avm[ji - 2])
                    p_ = 2 * g + pt
                    lns.inc()
                    plan.add("scalar", lambda e, ji=ji, SS=SS, p_=p_: e.activation(
                        pbuf[:, (ji % 2) * 1024:(ji % 2) * 1024 + 1024],
                        SS[:, 0:1024], AF.Exp, bias=cb[:, p_:p_ + 1]).then_inc(lns.h, 1))
                    expm[ji] = lns.v
                    if j >= 4 * qb:
                        r_off = (j - 4 * qb) * 512
                        wait(plan, "vector", lns, expm[ji])
                        for hh in range(2):
                            lnv.inc()
                            plan.add("vector", lambda e, hh=hh, ji=ji, r_off=r_off: e.tensor_mul(
                                pbuf[:, (ji % 2) * 1024 + hh * 512:(ji % 2) * 1024 + hh * 512 + 512],
                                pbuf[:, (ji % 2) * 1024 + hh * 512:(ji % 2) * 1024 + hh * 512 + 512],
                                msk[:, r_off:r_off + 512]).then_inc(lnv.h, 1))
                        maskm[ji] = lnv.v

                def emit_av(j, first, last):
                    ji = jpos[j]
                    if j >= 4 * qb:
                        wait(plan, "tensor", lnv, maskm[ji])
                    else:
                        wait(plan, "tensor", lns, expm[ji])
                    for hh, attb in ((0, ps[0]), (1, ps[1])):
                        hg = 2 * pt + hh
                        plan.add("tensor", lambda e, hh=hh, j=j, ji=ji, hg=hg, attb=attb,
                                 first=first, last=last:
                                 e.matmul(attb[0:65, 0:512],
                                          vext[:, (j * 4 + hg) * 65:(j * 4 + hg) * 65 + 65],
                                          pbuf[:, (ji % 2) * 1024 + hh * 512:
                                               (ji % 2) * 1024 + hh * 512 + 512],
                                          start=first, stop=last))
                    avm[ji] = pe_mark(plan, mm)

                mmS = {}
                jorder = list(range(4 * qb, J)) + list(range(4 * qb))
                emit_S(jorder[0])
                for ji, j in enumerate(jorder):
                    emit_expmask(j)
                    if ji + 1 < J:
                        emit_S(jorder[ji + 1])
                    if ji == 0:
                        wait(plan, "tensor", lns, ctx.get(name + "finp", 0))
                    emit_av(j, first=(ji == 0), last=(ji == J - 1))
                ctx[name + "exp_last"] = expm[J - 1]
                wait(plan, "vector", mm, avm[J - 1])
                if tile12 >= 2:
                    wait(plan, "vector", ast[pb], st_of.get(tile12 - 2, 0))
                for hh, attb in ((0, ps[0]), (1, ps[1])):
                    lnv.inc()
                    def _recip(e, hh=hh, attb=attb, t2=tile12 % 2):
                        with nc.allow_low_precision(reason="softmax 1/l to bf16 operand"):
                            return e.reciprocal(rrow[0:1, t2 * 1024 + hh * 512:
                                                     t2 * 1024 + hh * 512 + 512],
                                                attb[64:65, 0:512])
                    plan.add("vector", lambda e, f=_recip: f(e).then_inc(lnv.h, 1))
                r_done = lnv.v
                wait(plan, "scalar", mm, avm[J - 1])
                wait(plan, "scalar", ast[pb], st_of.get(tile12 - 2, 0))
                for hh, attb in ((0, ps[0]), (1, ps[1])):
                    lns.inc()
                    plan.add("scalar", lambda e, hh=hh, attb=attb, pb=pb: e.copy(
                        apair[pb][hh * 64:hh * 64 + 64, 0:512],
                        attb[0:64, 0:512]).then_inc(lns.h, 1))
                ctx[name + "finp"] = lns.v
                ev_done = lns.v
                wait(plan, "gpsimd", lns, ev_done)
                wait(plan, "gpsimd", lnv, r_done)
                ast[pb].inc(32)
                sv = ast[pb].v
                st_of[tile12] = sv
                plan.add("gpsimd", lambda e, pair=pair, qb=qb, pb=pb, h=ast[pb].h: e.dma_start(
                    _ap(ahbm)[pair * 128:(pair + 1) * 128, qb * 512:(qb + 1) * 512],
                    apair[pb][:, 0:512]).then_inc(h, 16))
                plan.add("gpsimd", lambda e, pair=pair, qb=qb, t2=tile12 % 2, h=ast[pb].h: e.dma_start(
                    _ap(rhbm)[pair * 8 + qb:pair * 8 + qb + 1, :],
                    rrow[0:1, t2 * 1024:t2 * 1024 + 1024]).then_inc(h, 16))
                tile12 += 1

    wait(plan, "sync", ast[0], ast[0].v)
    wait(plan, "sync", ast[1], ast[1].v)
    wait(plan, "sync", mm, projmark[3])
    for c in range(DC):
        atl.inc(16)
        plan.add("sync", lambda e, c=c: e.dma_start(
            arena[c][0:128, 0:T], _ap(ahbm)[c * 128:(c + 1) * 128, 0:T]
        ).then_inc(atl.h, 16))
    atl_all = atl.v

    rsem = [mksem(name + "rgA"), mksem(name + "rgB")]
    wait(plan, "tensor", atl, atl_all)
    normmark = {}
    for c in range(DC):
        for qb in range(NQB):
            t = c * NQB + qb
            t2 = t % 2
            if t >= 2:
                wait(plan, "sync", mm, normmark.get(t - 2, 0))
            rv = rsem[t2].inc(16)
            plan.add("sync", lambda e, c=c, qb=qb, t2=t2: e.dma_start(
                rstg[t2][0:1, :], _ap(rhbm)[c * 8 + qb:c * 8 + qb + 1, :]).then_inc(rsem[t2].h, 16))
            wait(plan, "tensor", rsem[t2], rv)
            if t >= 2:
                wait(plan, "tensor", lns, ctx.get(name + "rbcv", 0))
            for hh in range(2):
                plan.add("tensor", lambda e, hh=hh, t2=t2: e.matmul(
                    ps[2 + t2][hh * 64:hh * 64 + 64, 0:512],
                    ctx["ones1"][0:1, hh * 64:hh * 64 + 64],
                    rstg[t2][0:1, hh * 512:hh * 512 + 512],
                    start=True, stop=True))
            bm = pe_mark(plan, mm)
            normmark[t] = bm
            wait(plan, "scalar", mm, bm)
            wait(plan, "scalar", lnv, ctx.get(name + "rmul", 0))
            lns.inc()
            plan.add("scalar", lambda e, t2=t2: e.copy(
                rbc[t2][:], ps[2 + t2][:, 0:512]).then_inc(lns.h, 1))
            ctx[name + "rbcv"] = lns.v
            wait(plan, "vector", lns, lns.v)
            lnv.inc()
            plan.add("vector", lambda e, c=c, qb=qb, t2=t2: e.tensor_mul(
                arena[c][0:128, qb * 512:(qb + 1) * 512],
                arena[c][0:128, qb * 512:(qb + 1) * 512],
                rbc[t2][:]).then_inc(lnv.h, 1))
            ctx[name + "rmul"] = lnv.v
    norm_done = lnv.v
    wait(plan, "tensor", lnv, norm_done)
    wait(plan, "tensor", wpld, wpld.v)
    wait(plan, "tensor", bb, bb_done)
    vv_of, ln_of2, st_of2 = {}, {}, {}
    for nt in range(NT):
        pb = nt % 2
        banks = (ps[4 + 2 * pb], ps[5 + 2 * pb])
        if nt >= 2:
            wait(plan, "sync", lnv, vv_of[nt - 2])
        rv = res2[pb].inc(16)
        plan.add("sync", lambda e, nt=nt, pb=pb: e.dma_start(
            xres[pb][:], _ap(x)[nt * 128:(nt + 1) * 128, :]).then_inc(res2[pb].h, 16))
        if nt >= 2:
            wait(plan, "tensor", lnv, vv_of[nt - 2])
        for c in range(DC):
            for och in range(2):
                plan.add("tensor", lambda e, c=c, nt=nt, och=och, b=banks[och]:
                         e.matmul(b[:, 0:512], arena[c][0:128, nt * 128:(nt + 1) * 128],
                                  wpj[c][0:128, och * 512:och * 512 + 512],
                                  start=(c == 0), stop=(c == DC - 1)))
        mm_here = pe_mark(plan, mm)
        wait(plan, "vector", mm, mm_here)
        wait(plan, "vector", res2[pb], rv)
        if nt == 0:
            wait(plan, "vector", bb, bb_done)
        vpre = lnv.inc(2)
        for och in range(2):
            plan.add("vector", lambda e, b=banks[och], och=och, pb=pb:
                     e.tensor_add(ysb[pb][:, och * 512:och * 512 + 512], b[:, 0:512],
                                  bbc[:, och * 512:och * 512 + 512]).then_inc(lnv.h, 1))
        wait(plan, "vector", lnv, vpre)
        vv = lnv.inc()
        plan.add("vector", lambda e, pb=pb: e.tensor_add(
            ysb[pb][:], ysb[pb][:], xres[pb][:]).then_inc(lnv.h, 1))
        vv_of[nt] = vv
        wait(plan, "scalar", lnv, vv)
        if nt >= 2:
            wait(plan, "vector", stm[pb], st_of2[nt - 2])
        lnfin = emit_ln_epilogue(nc, plan, ctx, ysb[pb], osb[pb])
        ln_of2[nt] = lnfin
        wait(plan, "gpsimd", lnv, lnfin)
        sv = stm[pb].inc(16)
        st_of2[nt] = sv
        plan.add("gpsimd", lambda e, nt=nt, pb=pb: e.dma_start(
            _ap(x1)[nt * 128:(nt + 1) * 128, :], osb[pb][:]).then_inc(stm[pb].h, 16))
    wait(plan, "gpsimd", stm[0], stm[0].v)
    wait(plan, "gpsimd", stm[1], stm[1].v)
    return



def build_stage2():
    nc = bass.Bass("TRN2", target_bir_lowering=False)
    x1 = nc.dram_tensor("x1", [T, D], F32, kind="ExternalInput").ap()
    wrt = nc.dram_tensor("wrt", [D, 8], F32, kind="ExternalInput").ap()
    w1 = nc.dram_tensor("w1", [8 * 4096, D], F32, kind="ExternalInput").ap()
    w2 = nc.dram_tensor("w2", [8 * D, 2048], F32, kind="ExternalInput").ap()
    x2 = nc.dram_tensor("x2", [T, D], F32, kind="ExternalOutput").ap()
    x1bf = nc.dram_tensor("x1bf", [T, D], BF16, kind="Internal").ap()
    w1bf = nc.dram_tensor("w1bf", [8 * 4096, D], BF16, kind="Internal").ap()
    w2bf = nc.dram_tensor("w2bf", [8 * D, 2048], BF16, kind="Internal").ap()

    import contextlib
    with contextlib.ExitStack() as stack:
        plan = Plan()
        ctx = mk_common(nc, stack, plan)
        emit_stage2(nc, stack, plan, ctx, x1, wrt, w1, w2, x1bf, w1bf,
                    w2bf, x2)
        plan.run(nc)
    return nc


def emit_weight_precasts(nc, plan, ctx, w1, w2, w1bf, w2bf, wqkv, wqkvbf,
                         wout, woutbf):
    mksem = ctx["mksem_nobar"]
    caste = [mksem(f"pce{e}") for e in range(8)]
    c3 = mksem("pc3")
    for e in range(8):
        caste[e].inc(32)
    c3.inc(64)
    c3_final = c3.v

    def chunk(es, with_c3=False):
        def emit():
            for e in es:
                plan.add("gpsimd", lambda e_, e=e: e_.dma_start(
                    _ap(w1bf)[e * 4096:(e + 1) * 4096, :],
                    _ap(w1)[e * 4096:(e + 1) * 4096, :]).then_inc(caste[e].h, 16))
                plan.add("gpsimd", lambda e_, e=e: e_.dma_start(
                    _ap(w2bf)[e * D:(e + 1) * D, :],
                    _ap(w2)[e * D:(e + 1) * D, :]).then_inc(caste[e].h, 16))
            if with_c3:
                for i in range(3):
                    plan.add("gpsimd", lambda e_, i=i: e_.dma_start(
                        _ap(wqkvbf)[i * D:(i + 1) * D, :],
                        _ap(wqkv)[i * D:(i + 1) * D, :]).then_inc(c3.h, 16))
                plan.add("gpsimd", lambda e_: e_.dma_start(
                    _ap(woutbf), _ap(wout)).then_inc(c3.h, 16))
        return emit

    chunks = [chunk([0, 1]), chunk([2, 3]), chunk([4, 5]), chunk([6, 7], True)]
    return {"caste": caste, "c3": (c3, c3_final), "chunks": chunks}


def emit_stage2(nc, stack, plan, ctx, x1, wrt, w1, w2, x1bf, w1bf, w2bf,
                x2, name="s2", ext_casts=None):
    E = 8
    QT = min(1024, T)
    NQ = T // QT
    NTQ = QT // 128
    NT2 = QT // 256
    sb = lambda n, shp, dt: ctx["salloc"](name + n, shp, dt)
    mksem = ctx["mksem"]
    x1T = [[sb(f"xT{p}_{c}", [128, QT], BF16) for c in range(DC)] for p in range(2)]
    w1tv = [[sb(f"w1v{b}_{c}", [128, 1024], BF16) for c in range(DC)] for b in range(2)]
    w1tg = [[sb(f"w1g{b}_{c}", [128, 1024], BF16) for c in range(DC)] for b in range(2)]
    w2t = [[sb(f"w2t{b}_{f}", [128, 1024], BF16) for f in range(DC)] for b in range(2)]
    wrT = sb("wrT", [128, 64], BF16)
    acta = [[sb(f"act{p}_{f}", [128, 512], BF16) for f in range(DC)] for p in range(2)]
    sg = [sb(f"sg{p}", [128, 512], F32) for p in range(2)]
    rwq = sb("rwq", [128, 8 * NTQ], F32)
    lg = sb("lg", [128, 8 * NTQ], F32)
    rt = sb("rt", [128, 40], F32)
    yac = [sb(f"ya{t}", [128, 1024], F32) for t in range(NTQ)]
    osb = [sb(f"o{p}", [128, 1024], F32) for p in range(2)]
    xres = [sb("xr0", [128, 1024], F32)]

    ps = ctx["ps"]
    mm = ctx["mm_sem"]; lnv = ctx["ln_v"]; lns = ctx["ln_s"]
    cast = mksem(name + "cast")
    caste = [mksem(name + f"ce{e}") for e in range(E)]
    wrld = mksem(name + "wrld")
    xtq = mksem(name + "xtq")
    wld = [mksem(name + f"wld{b}") for b in range(2)]
    res = mksem(name + "res")
    stm = [mksem(name + f"st{p}") for p in range(2)]

    def vop(fn):
        lnv.inc()
        plan.add("vector", lambda e_, fn=fn: fn(e_).then_inc(lnv.h, 1))
        wait(plan, "vector", lnv, lnv.v)

    slab = T // 4
    castsl = [mksem(name + f"cs{i}") for i in range(4)]
    for i in range(4):
        castsl[i].inc(16)
        plan.add("gpsimd", lambda e_, i=i, slab=slab: e_.dma_start(
            x1bf[i * slab:(i + 1) * slab, :], x1[i * slab:(i + 1) * slab, :]).then_inc(castsl[i].h, 16))
    cast_all = cast.v
    if ext_casts is not None:
        caste = ext_casts["caste"]
    else:
        for e in range(E):
            caste[e].inc(32)
            plan.add("gpsimd", lambda e_, e=e: e_.dma_start(
                w1bf[e * 4096:(e + 1) * 4096, :], w1[e * 4096:(e + 1) * 4096, :]).then_inc(caste[e].h, 16))
            plan.add("gpsimd", lambda e_, e=e: e_.dma_start(
                w2bf[e * D:(e + 1) * D, :], w2[e * D:(e + 1) * D, :]).then_inc(caste[e].h, 16))

    for c in range(DC):
        wrld.inc(16)
        plan.add("gpsimd", lambda e_, c=c: e_.dma_start(
            wrT[:, c * 8:(c + 1) * 8], _ap(wrt)[c * 128:(c + 1) * 128, :]
        ).then_inc(wrld.h, 16))
    wrld_all = wrld.v

    wait(plan, "tensor", ctx["init_sem"], ctx["init_ones"])
    q_hend = {}
    buffree = [0, 0]
    actm_h = {}
    yst_h = {}
    lastY = [0, 0]
    gfs = 0
    gy = 0
    rnd = 0
    st_of = {}
    tile_base = 0

    def emit_x1T_loads(q):
        if q >= NQ:
            return
        qb2 = q % 2
        if q >= 2:
            wait(plan, "sync", mm, q_hend.get(q - 2, 0))
        nslab = min(4, -(-((q + 1) * QT * 4) // T))
        for i_ in range(nslab):
            wait(plan, "sync", castsl[i_], 16)
        xtq.inc(16 * DC)
        for c in range(DC):
            plan.add("sync", lambda e_, q=q, c=c, qb2=qb2: e_.dma_start(
                x1T[qb2][c][0:128, 0:QT],
                x1bf[q * QT:(q + 1) * QT, c * 128:(c + 1) * 128],
                transpose=True).then_inc(xtq.h, 16))
        return xtq.v

    def emit_router(q, xv):
        qb2 = q % 2
        wait(plan, "tensor", xtq, xv)
        if q == 0:
            wait(plan, "tensor", wrld, wrld_all)
        wait(plan, "tensor", lnv, actm_h.get(gfs - 1, 0))
        wait(plan, "scalar", lnv, ctx.get(name + "rtv", 0))
        for ntq in range(NTQ):
            for c in range(DC):
                plan.add("tensor", lambda e_, ntq=ntq, c=c, qb2=qb2: e_.matmul(
                    ps[0][:, ntq * 8:(ntq + 1) * 8],
                    x1T[qb2][c][0:128, ntq * 128:(ntq + 1) * 128],
                    wrT[:, c * 8:(c + 1) * 8],
                    start=(ntq == 0 and c == 0), stop=(ntq == NTQ - 1 and c == DC - 1)))
        mk = pe_mark(plan, mm)
        wait(plan, "scalar", mm, mk)
        lns.inc()
        plan.add("scalar", lambda e_: e_.copy(lg[:], ps[0][:, 0:8 * NTQ]).then_inc(lns.h, 1))
        lg_done = lns.v
        ctx[name + "lgd"] = lg_done
        wait(plan, "vector", lns, lg_done)
        for ntq in range(NTQ):
            lgs = lg[:, ntq * 8:(ntq + 1) * 8]
            m1 = rt[:, 0:1]; eq1 = rt[:, 1:9]; mskd = rt[:, 9:17]
            m2 = rt[:, 17:18]; eq2 = rt[:, 24:32]; dd = rt[:, 18:19]
            ex = rt[:, 19:20]; tt = rt[:, 20:21]; w1v = rt[:, 21:22]
            w2v = rt[:, 22:23]; t2 = rt[:, 32:40]
            rws = rwq[:, ntq * 8:(ntq + 1) * 8]

            def vop(fn):
                lnv.inc()
                plan.add("vector", lambda e_, fn=fn: fn(e_).then_inc(lnv.h, 1))
                wait(plan, "vector", lnv, lnv.v)
            vop(lambda e_, lgs=lgs: e_.reduce_max(m1, lgs, axis=AX.X))
            vop(lambda e_, lgs=lgs: e_.tensor_single_scalar(eq1, lgs, m1, ALU.is_equal))
            vop(lambda e_, lgs=lgs: e_.scalar_tensor_tensor(mskd, eq1, -1e30, lgs, ALU.mult, ALU.add))
            vop(lambda e_: e_.reduce_max(m2, mskd, axis=AX.X))
            vop(lambda e_: e_.tensor_single_scalar(eq2, mskd, m2, ALU.is_equal))
            vop(lambda e_: e_.tensor_sub(dd, m2, m1))
            wait(plan, "scalar", lnv, lnv.v)
            lns.inc()
            plan.add("scalar", lambda e_, dd=dd, ex=ex: e_.activation(
                ex, dd, AF.Exp).then_inc(lns.h, 1))
            wait(plan, "vector", lns, lns.v)
            vop(lambda e_: e_.tensor_scalar_add(tt, ex, 1.0))
            vop(lambda e_: e_.reciprocal(w1v, tt))
            vop(lambda e_: e_.tensor_scalar(w2v, w1v, -1.0, 1.0, ALU.mult, ALU.add))
            vop(lambda e_: e_.tensor_single_scalar(t2, eq1, w1v, ALU.mult))
            vop(lambda e_, rws=rws: e_.scalar_tensor_tensor(rws, eq2, w2v, t2, ALU.mult, ALU.add))
            ctx[name + "rtv"] = lnv.v
        return lg_done

    def emit_loads(q, e, fh):
        b = (e * 2 + fh) % 2
        wait(plan, "sync", caste[e], caste[e].v)
        wait(plan, "sync", mm, buffree[b])
        wld[b].inc(16 * 3 * DC)
        for c in range(DC):
            plan.add("sync", lambda e_, e=e, fh=fh, c=c, b=b: e_.dma_start(
                w1tv[b][c][:], w1bf[e * 4096 + fh * 1024:e * 4096 + (fh + 1) * 1024,
                                   c * 128:(c + 1) * 128],
                transpose=True).then_inc(wld[b].h, 16))
            plan.add("sync", lambda e_, e=e, fh=fh, c=c, b=b: e_.dma_start(
                w1tg[b][c][:], w1bf[e * 4096 + 2048 + fh * 1024:
                                    e * 4096 + 2048 + (fh + 1) * 1024,
                                    c * 128:(c + 1) * 128],
                transpose=True).then_inc(wld[b].h, 16))
            plan.add("sync", lambda e_, e=e, fh=fh, c=c, b=b: e_.dma_start(
                w2t[b][c][:], w2bf[e * D:(e + 1) * D,
                                   fh * 1024 + c * 128:fh * 1024 + (c + 1) * 128],
                transpose=True).then_inc(wld[b].h, 16))
        return wld[b].v

    def emit_H(q, e, fh, ntw, wv, rw_done, qfirst):
        nonlocal gfs, rnd
        qb2 = q % 2
        b = (e * 2 + fh) % 2
        par = rnd % 2
        if ntw == 0:
            wait(plan, "tensor", wld[b], wv)
        if qfirst and ntw == 0:
            wait(plan, "tensor", lns, rw_done)
        for fs in range(DC):
            pv = ps[0 + 2 * (fs % 2)]
            pg = ps[1 + 2 * (fs % 2)]
            if gfs >= 2:
                wait(plan, "tensor", lnv, actm_h.get(gfs - 2, 0))
            for c in range(DC):
                plan.add("tensor", lambda e_, c=c, fs=fs, ntw=ntw, b=b, pv=pv, qb2=qb2:
                         e_.matmul(pv[:, 0:512],
                                   w1tv[b][c][0:128, fs * 128:(fs + 1) * 128],
                                   x1T[qb2][c][0:128, ntw * 512:(ntw + 1) * 512],
                                   start=(c == 0), stop=(c == DC - 1)))
            for c in range(DC):
                plan.add("tensor", lambda e_, c=c, fs=fs, ntw=ntw, b=b, pg=pg, qb2=qb2:
                         e_.matmul(pg[:, 0:512],
                                   w1tg[b][c][0:128, fs * 128:(fs + 1) * 128],
                                   x1T[qb2][c][0:128, ntw * 512:(ntw + 1) * 512],
                                   start=(c == 0), stop=(c == DC - 1)))
            hmk = pe_mark(plan, mm)
            wait(plan, "scalar", mm, hmk)
            if gfs >= 2:
                wait(plan, "scalar", lnv, actm_h.get(gfs - 2, 0))
            lns.inc()
            plan.add("scalar", lambda e_, fs=fs, pg=pg: e_.activation(
                sg[fs % 2][:], pg[:, 0:512], AF.Silu).then_inc(lns.h, 1))
            smk = lns.v
            wait(plan, "vector", lns, smk)
            if lastY[par]:
                wait(plan, "vector", mm, lastY[par])
            lnv.inc()
            plan.add("vector", lambda e_, fs=fs, pv=pv, par=par: e_.tensor_mul(
                acta[par][fs][:], pv[:, 0:512], sg[fs % 2][:]).then_inc(lnv.h, 1))
            actm_h[gfs] = lnv.v
            gfs += 1
        rnd += 1
        return par, gfs - 1

    def make_Y(q, e, fh, ntw, par, gfs_last, qfirst):
        b = (e * 2 + fh) % 2

        def emit_Y():
            nonlocal gy
            wait(plan, "tensor", lnv, actm_h[gfs_last])
            first = qfirst
            for nsub in range(4):
                tile = ntw * 4 + nsub
                pairb = (ps[4], ps[5]) if nsub % 2 == 0 else (ps[6], ps[7])
                if gy >= 2:
                    wait(plan, "tensor", lnv, yst_h.get(gy - 2, 0))
                for fs in range(DC):
                    for cc in range(2):
                        plan.add("tensor", lambda e_, fs=fs, nsub=nsub, cc=cc, b=b, par=par, yb=pairb[cc]:
                                 e_.matmul(yb[:, 0:512],
                                           acta[par][fs][0:128, nsub * 128:(nsub + 1) * 128],
                                           w2t[b][fs][0:128, cc * 512:(cc + 1) * 512],
                                           start=(fs == 0), stop=(fs == DC - 1)))
                ymk = pe_mark(plan, mm)
                wait(plan, "vector", mm, ymk)
                for cc in range(2):
                    lnv.inc()
                    if first:
                        plan.add("vector", lambda e_, tile=tile, cc=cc, e=e, yb=pairb[cc]:
                                 e_.tensor_single_scalar(
                                     yac[tile][:, cc * 512:(cc + 1) * 512],
                                     yb[:, 0:512],
                                     rwq[:, tile * 8 + e:tile * 8 + e + 1],
                                     ALU.mult).then_inc(lnv.h, 1))
                    else:
                        plan.add("vector", lambda e_, tile=tile, cc=cc, e=e, yb=pairb[cc]:
                                 e_.scalar_tensor_tensor(
                                     yac[tile][:, cc * 512:(cc + 1) * 512],
                                     yb[:, 0:512],
                                     rwq[:, tile * 8 + e:tile * 8 + e + 1],
                                     yac[tile][:, cc * 512:(cc + 1) * 512],
                                     ALU.mult, ALU.add).then_inc(lnv.h, 1))
                yst_h[gy] = lnv.v
                gy += 1
            lastY[par] = mm.v
            buffree[b] = mm.v
        return emit_Y

    def make_finalize(q):
        def one_tile(ntq):
            def emit_fin_tile():
                t = q * NTQ + ntq
                pb = t % 2
                rv = res.inc(16)
                plan.add("gpsimd", lambda e_, t=t: e_.dma_start(
                    xres[0][:], x1[t * 128:(t + 1) * 128, :]).then_inc(res.h, 16))
                wait(plan, "vector", res, rv)
                lnv.inc()
                plan.add("vector", lambda e_, ntq=ntq: e_.tensor_add(
                    yac[ntq][:], yac[ntq][:], xres[0][:]).then_inc(lnv.h, 1))
                vv = lnv.v
                wait(plan, "scalar", lnv, vv)
                if t >= 2:
                    wait(plan, "vector", stm[pb], st_of[t - 2])
                lnfin = emit_ln_epilogue(nc, plan, ctx, yac[ntq], osb[pb])
                wait(plan, "gpsimd", lnv, lnfin)
                sv = stm[pb].inc(16)
                st_of[t] = sv
                plan.add("gpsimd", lambda e_, t=t, pb=pb: e_.dma_start(
                    x2[t * 128:(t + 1) * 128, :], osb[pb][:]).then_inc(stm[pb].h, 16))
            return emit_fin_tile
        return [one_tile(ntq) for ntq in range(NTQ)]

    xv = emit_x1T_loads(0)
    xv_next = None
    for q in range(NQ):
        xv_next = emit_x1T_loads(q + 1)
        rw_done = emit_router(q, xv)
        xv = xv_next
        pending_Y = None
        ef = [(e, fh) for e in range(E) for fh in range(2)]
        if q % 2 == 1:
            ef = ef[::-1]
        fin = make_finalize(q)
        for efi, (e, fh) in enumerate(ef):
            b = (e * 2 + fh) % 2
            if ctx.get(name + f"buf{b}") != (e, fh):
                emit_loads(q, e, fh)
                ctx[name + f"buf{b}"] = (e, fh)
            wv = wld[b].v
            last_ef = (efi == len(ef) - 1)
            for ntw in range(QT // 512):
                par, gl = emit_H(q, e, fh, ntw, wv, rw_done, efi == 0)
                if pending_Y is not None:
                    pending_Y()
                if last_ef and ntw == 1 and QT // 512 > 1:
                    for ntq_ in range(NTQ // 2):
                        fin[ntq_]()
                pending_Y = make_Y(q, e, fh, ntw, par, gl, efi == 0)
        pending_Y()
        start_ = NTQ // 2 if QT // 512 > 1 else 0
        for ntq_ in range(start_, NTQ):
            fin[ntq_]()
        q_hend[q] = mm.v
    wait(plan, "gpsimd", stm[0], stm[0].v)
    wait(plan, "gpsimd", stm[1], stm[1].v)
    return


def _run_stage2(x1, Wr, W1, W2):
    nc = build_stage2()
    in_maps = [dict(x1=np.ascontiguousarray(x1[b]), wrt=np.ascontiguousarray(Wr.T),
                    w1=np.ascontiguousarray(W1.reshape(8 * 4096, D)),
                    w2=np.ascontiguousarray(W2.reshape(8 * D, 2048)))
               for b in range(x1.shape[0])]
    res = run_bass_kernel_spmd(nc, in_maps, core_ids=list(range(x1.shape[0])))
    return np.stack([r["x2"] for r in res.results], 0)


def build_stage3():
    nc = bass.Bass("TRN2", target_bir_lowering=False)
    x2 = nc.dram_tensor("x2", [T, D], F32, kind="ExternalInput").ap()
    wqkv = nc.dram_tensor("wqkv", [3 * D, D], F32, kind="ExternalInput").ap()
    wout = nc.dram_tensor("wout", [D, D], F32, kind="ExternalInput").ap()
    bout = nc.dram_tensor("bout", [D], F32, kind="ExternalInput").ap()
    x3 = nc.dram_tensor("x3", [T, D], F32, kind="ExternalOutput").ap()
    x2bf = nc.dram_tensor("x2bf", [T, D], BF16, kind="Internal").ap()
    wqkvbf = nc.dram_tensor("wqkvbf", [3 * D, D], BF16, kind="Internal").ap()
    woutbf = nc.dram_tensor("woutbf", [D, D], BF16, kind="Internal").ap()

    import contextlib
    with contextlib.ExitStack() as stack:
        plan = Plan()
        ctx = mk_common(nc, stack, plan)
        emit_stage3(nc, stack, plan, ctx, x2, x2bf, wqkv, wqkvbf, wout, woutbf,
                    bout, x3)
        plan.run(nc)
    return nc


def emit_stage3(nc, stack, plan, ctx, x2, x2bf, wqkv, wqkvbf, wout, woutbf,
                bout, x3, name="s3", ext_casts=None):
    EPS = 1e-6
    sb = lambda n, shp, dt: ctx["salloc"](name + n, shp, dt)
    mksem = ctx["mksem"]
    NS = T // 512
    kvw = [sb(f"kvw{c}", [128, 2048], BF16) for c in range(DC)]
    woT = [sb(f"wo{c}", [128, 1024], BF16) for c in range(DC)]
    kvacc = [sb(f"kva{c}", [128, 1024], F32) for c in range(DC)]
    kvbf = [sb(f"kvb{c}", [128, 1024], BF16) for c in range(DC)]
    x2Ta = [sb(f"x2a{p}", [128, 1024], BF16) for p in range(2)]
    x2Tb = [sb(f"x2b{p}", [128, 4096], BF16) for p in range(2)]
    kbf = [sb(f"kbf{i}", [128, 1024], BF16) for i in range(4)]
    vbf = [sb(f"vbf{i}", [128, 1024], BF16) for i in range(4)]
    qT = [[sb(f"qT{p}_{c}", [128, 512], BF16) for c in range(DC)] for p in range(2)]
    aT = [sb(f"aT{c}", [128, 512], BF16) for c in range(DC)]
    ksumT = sb("ksumT", [128, 8], BF16)
    zsb = sb("z", [128, 8], F32)
    bbc = sb("bbc", [128, 1024], F32)
    ysb = [sb(f"y{p}", [128, 1024], F32) for p in range(2)]
    osb = [sb(f"o{p}", [128, 1024], F32) for p in range(2)]
    xres = [sb(f"xr{p}", [128, 1024], F32) for p in range(2)]

    ps = ctx["ps"]
    mm = ctx["mm_sem"]; lnv = ctx["ln_v"]; lns = ctx["ln_s"]
    cast = mksem(name + "cast")
    wld = mksem(name + "wld")
    qld = mksem(name + "qld")
    xta = [mksem(name + f"xta{p}") for p in range(2)]
    xtb = [mksem(name + f"xtb{p}") for p in range(2)]
    res = mksem(name + "res")
    stm = [mksem(name + f"st{p}") for p in range(2)]
    bb = Sem(nc, stack, name + "bb")

    slab = T // 4
    castsl = [mksem(name + f"cs{i}") for i in range(4)]
    for i in range(4):
        castsl[i].inc(16)
        plan.add("gpsimd", lambda e, i=i, slab=slab: e.dma_start(
            x2bf[i * slab:(i + 1) * slab, :], x2[i * slab:(i + 1) * slab, :]).then_inc(castsl[i].h, 16))
    if ext_casts is None:
        for i in range(3):
            cast.inc(16)
            plan.add("gpsimd", lambda e, i=i: e.dma_start(
                wqkvbf[i * D:(i + 1) * D, :], wqkv[i * D:(i + 1) * D, :]).then_inc(cast.h, 16))
        cast.inc(16)
        plan.add("gpsimd", lambda e: e.dma_start(woutbf, wout).then_inc(cast.h, 16))
        wsem, wval = cast, cast.v
    else:
        wsem, wval = ext_casts["c3"]
    cast_all = cast.v

    wait(plan, "sync", wsem, wval)
    for c in range(DC):
        wld.inc(32)
        plan.add("sync", lambda e, c=c: e.dma_start(
            kvw[c][0:128, 0:2048], wqkvbf[D:3 * D, c * 128:(c + 1) * 128],
            transpose=True).then_inc(wld.h, 16))
        plan.add("sync", lambda e, c=c: e.dma_start(
            woT[c][:], woutbf[:, c * 128:(c + 1) * 128], transpose=True).then_inc(wld.h, 16))
    wld_all = wld.v

    emit_bias_broadcast(nc, plan, ctx, bout, bbc, D, bb)
    bb_done = bb.v

    for c in range(DC):
        lnv.inc()
        plan.add("vector", lambda e, c=c: e.memset(kvacc[c][:], 0.0).then_inc(lnv.h, 1))
    kvz_done = lnv.v

    wait(plan, "tensor", wld, wld_all)
    wait(plan, "tensor", ctx["init_sem"], ctx["init_ones"])
    xta_free = {}
    evac_of = {}
    eps_of = {}
    kvmm_of = {}
    kvadd_of = {}
    projmm_of = {}
    NG = NT // 4
    for g in range(NG):
        for i in range(4):
            nt = 4 * g + i
            pb = nt % 2
            if nt >= 2:
                wait(plan, "sync", mm, xta_free[nt - 2])
            if nt % (NT // 4) == 0:
                wait(plan, "sync", castsl[nt // (NT // 4)], 16)
            xv = xta[pb].inc(16 * DC)
            for c in range(DC):
                plan.add("sync", lambda e, nt=nt, c=c, pb=pb: e.dma_start(
                    x2Ta[pb][0:128, c * 128:(c + 1) * 128],
                    x2bf[nt * 128:(nt + 1) * 128, c * 128:(c + 1) * 128],
                    transpose=True).then_inc(xta[pb].h, 16))
            wait(plan, "tensor", xta[pb], xv)
            if nt >= 1:
                wait(plan, "tensor", lns, evac_of[nt - 1])
            if g >= 1 and i == 0:
                wait(plan, "tensor", lnv, kvadd_of[g - 1])
            for c in range(DC):
                st_ = (c == 0)
                sp_ = (c == DC - 1)
                for tgt, col in ((ps[0], 0), (ps[1], 512), (ps[2], 1024), (ps[3], 1536)):
                    plan.add("tensor", lambda e, c=c, st_=st_, sp_=sp_, pb=pb, tgt=tgt, col=col:
                             e.matmul(tgt[:, 0:512],
                                      x2Ta[pb][0:128, c * 128:(c + 1) * 128],
                                      kvw[c][0:128, col:col + 512],
                                      start=st_, stop=sp_))
            mmk = pe_mark(plan, mm)
            xta_free[nt] = mmk
            projmm_of[nt] = mmk
            wait(plan, "scalar", mm, mmk)
            if g >= 1:
                wait(plan, "scalar", mm, kvmm_of[g - 1])
            for cc in range(2):
                lns.inc()
                plan.add("scalar", lambda e, cc=cc, i=i: e.activation(
                    kbf[i][:, cc * 512:cc * 512 + 512], ps[0 + cc][:, 0:512],
                    AF.Relu).then_inc(lns.h, 1))
                lns.inc()
                plan.add("scalar", lambda e, cc=cc, i=i: e.copy(
                    vbf[i][:, cc * 512:cc * 512 + 512], ps[2 + cc][:, 0:512]).then_inc(lns.h, 1))
            evac_of[nt] = lns.v
            wait(plan, "vector", lns, evac_of[nt])
            lnv.inc()
            plan.add("vector", lambda e, i=i: e.tensor_scalar_add(
                kbf[i][:], kbf[i][:], EPS).then_inc(lnv.h, 1))
            eps_of[nt] = lnv.v
        wait(plan, "tensor", lnv, eps_of[4 * g + 3])
        if g == 0:
            wait(plan, "tensor", lnv, kvz_done)
        kvadd_prev = kvadd_of.get(g - 1, 0)
        add_targets = []
        for dc_ in range(DC):
            pair = (ps[4], ps[5]) if dc_ % 2 == 0 else (ps[2], ps[3])
            if dc_ >= 2:
                wait(plan, "tensor", lnv, add_targets[dc_ - 2])
            for i in range(4):
                nt = 4 * g + i
                st_ = (i == 0)
                sp_ = (i == 3)
                for eh in range(2):
                    plan.add("tensor", lambda e, dc_=dc_, i=i, eh=eh, st_=st_, sp_=sp_, pair=pair:
                             e.matmul(pair[eh][:, 0:512],
                                      kbf[i][0:128, dc_ * 128:(dc_ + 1) * 128],
                                      vbf[i][0:128, eh * 512:eh * 512 + 512],
                                      start=st_, stop=sp_))
                plan.add("tensor", lambda e, dc_=dc_, i=i, g=g, st_=st_, sp_=sp_:
                         e.matmul(ps[7][0:128, dc_:dc_ + 1],
                                  kbf[i][0:128, dc_ * 128:(dc_ + 1) * 128],
                                  ctx["ones128"][0:128, 0:1],
                                  start=(g == 0 and i == 0), stop=(g == NG - 1 and i == 3)))
            mmv = pe_mark(plan, mm)
            wait(plan, "vector", mm, mmv)
            for eh in range(2):
                lnv.inc()
                plan.add("vector", lambda e, dc_=dc_, eh=eh, pair=pair: e.tensor_add(
                    kvacc[dc_][:, eh * 512:eh * 512 + 512],
                    kvacc[dc_][:, eh * 512:eh * 512 + 512],
                    pair[eh][:, 0:512]).then_inc(lnv.h, 1))
            add_targets.append(lnv.v)
        kvmm_of[g] = mm.v
        kvadd_of[g] = lnv.v

    wait(plan, "scalar", lnv, kvadd_of[NG - 1])
    wait(plan, "scalar", mm, mm.v)
    for c in range(DC):
        lns.inc()
        plan.add("scalar", lambda e, c=c: e.copy(kvbf[c][:], kvacc[c][:]).then_inc(lns.h, 1))
    lns.inc()
    plan.add("scalar", lambda e: e.copy(ksumT[:], ps[7][0:128, 0:8]).then_inc(lns.h, 1))
    kvbf_done = lns.v

    wait(plan, "sync", mm, kvmm_of[NG - 1])
    for c in range(DC):
        qld.inc(16)
        plan.add("sync", lambda e, c=c: e.dma_start(
            kvw[c][0:128, 0:1024], wqkvbf[0:D, c * 128:(c + 1) * 128],
            transpose=True).then_inc(qld.h, 16))
    qld_all = qld.v

    wait(plan, "tensor", qld, qld_all)
    wait(plan, "tensor", lns, kvbf_done)
    qproj_free = {}
    ln_of = {}
    st_of = {}
    tile_idx = 0
    z_done_of = {}
    qready_of = {}
    bprojmm = {0: 0}

    def emit_A(ns):
        if ns >= NS:
            return
        pb = ns % 2
        if ns >= 2:
            wait(plan, "sync", mm, qproj_free[ns - 2])
        for i_ in range(min(4, -(-((ns + 1) * 512 * 4) // T))):
            wait(plan, "sync", castsl[i_], 16)
        xv = xtb[pb].inc(16 * DC)
        for c in range(DC):
            plan.add("sync", lambda e, ns=ns, c=c, pb=pb: e.dma_start(
                x2Tb[pb][0:128, c * 512:(c + 1) * 512],
                x2bf[ns * 512:(ns + 1) * 512, c * 128:(c + 1) * 128],
                transpose=True).then_inc(xtb[pb].h, 16))
        wait(plan, "tensor", xtb[pb], xv)
        wait(plan, "tensor", lns, ctx.get(name + "qrelu", 0))
        qevac = {}
        for dq in range(DC):
            bank = ps[dq % 2]
            if dq >= 2:
                wait(plan, "tensor", lnv, qevac[dq - 2])
            for c in range(DC):
                plan.add("tensor", lambda e, dq=dq, c=c, pb=pb, bank=bank:
                         e.matmul(bank[:, 0:512],
                                  kvw[c][0:128, dq * 128:(dq + 1) * 128],
                                  x2Tb[pb][0:128, c * 512:(c + 1) * 512],
                                  start=(c == 0), stop=(c == DC - 1)))
            mmq = pe_mark(plan, mm)
            wait(plan, "scalar", mm, mmq)
            wait(plan, "scalar", mm, bprojmm.get(ns - 1, 0))
            lns.inc()
            plan.add("scalar", lambda e, dq=dq, pb=pb, bank=bank: e.activation(
                qT[pb][dq][:], bank[:, 0:512], AF.Relu).then_inc(lns.h, 1))
            wait(plan, "vector", lns, lns.v)
            lnv.inc()
            plan.add("vector", lambda e, dq=dq, pb=pb: e.tensor_scalar_add(
                qT[pb][dq][:], qT[pb][dq][:], EPS).then_inc(lnv.h, 1))
            qevac[dq] = lnv.v
        ctx[name + "qrelu"] = lns.v
        qproj_free[ns] = mm.v
        qready_of[ns] = lnv.v

    def emit_B(ns):
        nonlocal tile_idx
        pb = ns % 2
        wait(plan, "tensor", lnv, qready_of[ns])
        wait(plan, "tensor", lnv, ln_of.get(tile_idx - 1, 0))
        for nsub in range(4):
            for c in range(DC):
                plan.add("tensor", lambda e, nsub=nsub, c=c, pb=pb:
                         e.matmul(ps[2][0:128, nsub:nsub + 1],
                                  qT[pb][c][0:128, nsub * 128:(nsub + 1) * 128],
                                  ksumT[0:128, c:c + 1],
                                  start=(c == 0), stop=(c == DC - 1)))
        mmz = pe_mark(plan, mm)
        wait(plan, "vector", mm, mmz)
        wait(plan, "vector", lnv, lnv.v)
        lnv.inc()
        plan.add("vector", lambda e: e.tensor_scalar_add(
            zsb[:, 0:4], ps[2][0:128, 0:4], EPS).then_inc(lnv.h, 1))
        wait(plan, "vector", lnv, lnv.v)
        lnv.inc()
        plan.add("vector", lambda e: e.reciprocal(
            zsb[:, 4:8], zsb[:, 0:4]).then_inc(lnv.h, 1))
        z_done = lnv.v
        z_done_of[ns] = z_done
        aevac = {}
        for et in range(DC):
            bank = ps[3 + et % 2]
            if et >= 2:
                wait(plan, "tensor", lns, aevac[et - 2])
            for c in range(DC):
                plan.add("tensor", lambda e, et=et, c=c, pb=pb, bank=bank:
                         e.matmul(bank[:, 0:512],
                                  kvbf[c][0:128, et * 128:(et + 1) * 128],
                                  qT[pb][c][0:128, 0:512],
                                  start=(c == 0), stop=(c == DC - 1)))
            mma = pe_mark(plan, mm)
            wait(plan, "scalar", mm, mma)
            wait(plan, "scalar", mm, bprojmm.get(ns, 0))
            lns.inc()
            plan.add("scalar", lambda e, et=et, bank=bank: e.copy(
                aT[et][:], bank[:, 0:512]).then_inc(lns.h, 1))
            aevac[et] = lns.v
        a_ready = lns.v
        wait(plan, "tensor", lns, a_ready)
        for nsub in range(4):
            pb2 = tile_idx % 2
            pbanks = (ps[5], ps[6]) if nsub % 2 == 0 else (ps[7], ps[2])
            if nsub % 2 == 1:
                wait(plan, "tensor", lnv, z_done)
            if tile_idx >= 2:
                wait(plan, "tensor", lnv, ln_of[tile_idx - 2])
            for et in range(DC):
                st_ = (et == 0)
                sp_ = (et == DC - 1)
                for cc in range(2):
                    plan.add("tensor", lambda e, nsub=nsub, cc=cc, et=et, st_=st_, sp_=sp_, pbanks=pbanks:
                             e.matmul(pbanks[cc][:, 0:512],
                                      aT[et][0:128, nsub * 128:(nsub + 1) * 128],
                                      woT[et][0:128, cc * 512:cc * 512 + 512],
                                      start=st_, stop=sp_))
            mmp = pe_mark(plan, mm)
            rv = res.inc(16)
            plan.add("gpsimd", lambda e, ns=ns, nsub=nsub, pb2=pb2: e.dma_start(
                xres[pb2][:], x2[ns * 512 + nsub * 128:ns * 512 + (nsub + 1) * 128, :]
            ).then_inc(res.h, 16))
            wait(plan, "vector", mm, mmp)
            wait(plan, "vector", bb, bb_done)
            wait(plan, "vector", lnv, z_done)
            wait(plan, "vector", res, rv)
            for cc in range(2):
                lnv.inc()
                plan.add("vector", lambda e, cc=cc, nsub=nsub, pb2=pb2, pbanks=pbanks:
                         e.scalar_tensor_tensor(
                             ysb[pb2][:, cc * 512:cc * 512 + 512], pbanks[cc][:, 0:512],
                             zsb[:, 4 + nsub:5 + nsub], bbc[:, cc * 512:cc * 512 + 512],
                             ALU.mult, ALU.add).then_inc(lnv.h, 1))
            wait(plan, "vector", lnv, lnv.v)
            vv = lnv.inc()
            plan.add("vector", lambda e, pb2=pb2: e.tensor_add(
                ysb[pb2][:], ysb[pb2][:], xres[pb2][:]).then_inc(lnv.h, 1))
            wait(plan, "scalar", lnv, vv)
            if tile_idx >= 2:
                wait(plan, "vector", stm[pb2], st_of[tile_idx - 2])
            lnfin = emit_ln_epilogue(nc, plan, ctx, ysb[pb2], osb[pb2])
            ln_of[tile_idx] = lnfin
            wait(plan, "gpsimd", lnv, lnfin)
            sv = stm[pb2].inc(16)
            st_of[tile_idx] = sv
            plan.add("gpsimd", lambda e, ns=ns, nsub=nsub, pb2=pb2: e.dma_start(
                x3[ns * 512 + nsub * 128:ns * 512 + (nsub + 1) * 128, :],
                osb[pb2][:]).then_inc(stm[pb2].h, 16))
            tile_idx += 1
        bprojmm[ns + 1] = mm.v

    emit_A(0)
    for ns in range(NS):
        emit_A(ns + 1)
        emit_B(ns)
    wait(plan, "gpsimd", stm[0], stm[0].v)
    wait(plan, "gpsimd", stm[1], stm[1].v)
    return


    wait(plan, "gpsimd", stm[1], stm[1].v)
    return


def _dft_consts():
    j = np.arange(D)
    ang = 2.0 * np.pi * np.outer(j, j) / D
    fr = np.cos(ang).astype(np.float32)
    fi = (-np.sin(ang)).astype(np.float32)
    gr = (np.cos(ang) / D).astype(np.float32)
    gi = (np.sin(ang) / D).astype(np.float32)
    return fr, fi, gr, gi


def _run_stage4(x3, wfr, wfi, bias):
    fr, fi, gr, gi = _dft_consts()
    nc = build_stage4()
    in_maps = []
    for b in range(x3.shape[0]):
        in_maps.append(dict(x3=np.ascontiguousarray(x3[b]), wfr=wfr, wfi=wfi,
                            bias=bias, fr=fr, fi=fi, fineg=-fi, gr=gr, gineg=-gi))
    res = run_bass_kernel_spmd(nc, in_maps, core_ids=list(range(x3.shape[0])))
    return np.stack([r["out"] for r in res.results], 0)


def _mk_masks():
    import ml_dtypes
    m = np.zeros((128, 2048), dtype=np.float32)
    for ri, r in enumerate((0, 128, 256, 384)):
        k = np.arange(128)[:, None]
        q = np.arange(512)[None, :]
        m[:, ri * 512:(ri + 1) * 512] = ((k + r) <= q).astype(np.float32)
    return m.astype(ml_dtypes.bfloat16)


def _run_stage1(x, wqkv, wproj, bproj):
    nc = build_stage1()
    masks = _mk_masks()
    in_maps = [dict(x=np.ascontiguousarray(x[b]), wqkv=wqkv, wproj=wproj,
                    bproj=bproj, masks=masks) for b in range(x.shape[0])]
    res = run_bass_kernel_spmd(nc, in_maps, core_ids=list(range(x.shape[0])))
    return np.stack([r["x1"] for r in res.results], 0)


def _run_stage3(x2, wqkv, wout, bout):
    nc = build_stage3()
    in_maps = [dict(x2=np.ascontiguousarray(x2[b]), wqkv=wqkv, wout=wout, bout=bout)
               for b in range(x2.shape[0])]
    res = run_bass_kernel_spmd(nc, in_maps, core_ids=list(range(x2.shape[0])))
    return np.stack([r["x3"] for r in res.results], 0)


POOL_BYTES = 201728


def build_fused():
    nc = bass.Bass("TRN2", target_bir_lowering=False)
    x = nc.dram_tensor("x", [T, D], F32, kind="ExternalInput").ap()
    wqkv0 = nc.dram_tensor("wqkv0", [D, 3 * D], F32, kind="ExternalInput").ap()
    wproj0 = nc.dram_tensor("wproj0", [D, D], F32, kind="ExternalInput").ap()
    bproj0 = nc.dram_tensor("bproj0", [D], F32, kind="ExternalInput").ap()
    masks = nc.dram_tensor("masks", [128, 2048], BF16, kind="ExternalInput").ap()
    wrt = nc.dram_tensor("wrt", [D, 8], F32, kind="ExternalInput").ap()
    w1 = nc.dram_tensor("w1", [8 * 4096, D], F32, kind="ExternalInput").ap()
    w2 = nc.dram_tensor("w2", [8 * D, 2048], F32, kind="ExternalInput").ap()
    wqkv1 = nc.dram_tensor("wqkv1", [3 * D, D], F32, kind="ExternalInput").ap()
    wout1 = nc.dram_tensor("wout1", [D, D], F32, kind="ExternalInput").ap()
    bout1 = nc.dram_tensor("bout1", [D], F32, kind="ExternalInput").ap()
    wfr = nc.dram_tensor("wfr", [D, D], F32, kind="ExternalInput").ap()
    wfi = nc.dram_tensor("wfi", [D, D], F32, kind="ExternalInput").ap()
    fbias = nc.dram_tensor("fbias", [D], F32, kind="ExternalInput").ap()
    fr = nc.dram_tensor("fr", [D, D], F32, kind="ExternalInput").ap()
    fi = nc.dram_tensor("fi", [D, D], F32, kind="ExternalInput").ap()
    fineg = nc.dram_tensor("fineg", [D, D], F32, kind="ExternalInput").ap()
    gr = nc.dram_tensor("gr", [D, D], F32, kind="ExternalInput").ap()
    gineg = nc.dram_tensor("gineg", [D, D], F32, kind="ExternalInput").ap()
    out = nc.dram_tensor("out", [T, D], F32, kind="ExternalOutput").ap()
    x1 = nc.dram_tensor("x1", [T, D], F32, kind="Internal").ap()
    x2 = nc.dram_tensor("x2", [T, D], F32, kind="Internal").ap()
    x3 = nc.dram_tensor("x3", [T, D], F32, kind="Internal").ap()
    xbf = nc.dram_tensor("xbf", [T, D], BF16, kind="Internal").ap()
    ahbm = nc.dram_tensor("ahbm", [D, T], BF16, kind="Internal").ap()
    rhbm = nc.dram_tensor("rhbm", [64, 1024], BF16, kind="Internal").ap()
    x1bf = nc.dram_tensor("x1bf", [T, D], BF16, kind="Internal").ap()
    w1bf = nc.dram_tensor("w1bf", [8 * 4096, D], BF16, kind="Internal").ap()
    w2bf = nc.dram_tensor("w2bf", [8 * D, 2048], BF16, kind="Internal").ap()
    x2bf = nc.dram_tensor("x2bf", [T, D], BF16, kind="Internal").ap()
    wqkvbf = nc.dram_tensor("wqkvbf", [3 * D, D], BF16, kind="Internal").ap()
    woutbf = nc.dram_tensor("woutbf", [D, D], BF16, kind="Internal").ap()
    x3bf = nc.dram_tensor("x3bf", [T, D], BF16, kind="Internal").ap()

    import contextlib
    with contextlib.ExitStack() as stack:
        pool = stack.enter_context(nc.sbuf_tensor("pool", [128, POOL_BYTES], mybir.dt.uint8))
        plan = Plan()
        ctx = mk_common(nc, stack, plan, pool=pool)
        ext = emit_weight_precasts(nc, plan, ctx, w1, w2, w1bf, w2bf,
                                   wqkv1, wqkvbf, wout1, woutbf)
        ctx["precast_chunks"] = list(ext["chunks"])
        emit_stage1(nc, stack, plan, ctx, x, xbf, wqkv0, wproj0, bproj0,
                    masks, ahbm, x1, rhbm)
        assert not ctx["precast_chunks"]
        emit_barrier(nc, plan, ctx)
        ctx["pool_reset"]()
        emit_stage2(nc, stack, plan, ctx, x1, wrt, w1, w2, x1bf, w1bf, w2bf,
                    x2, ext_casts=ext)
        emit_barrier(nc, plan, ctx)
        ctx["pool_reset"]()
        emit_stage3(nc, stack, plan, ctx, x2, x2bf, wqkv1, wqkvbf, wout1,
                    woutbf, bout1, x3, ext_casts=ext)
        emit_barrier(nc, plan, ctx)
        ctx["pool_reset"]()
        emit_stage4(nc, stack, plan, ctx, x3, x3bf, wfr, wfi, fbias,
                    fr, fi, fineg, gr, gineg, out)
        plan.run(nc)
    return nc


def kernel(L0_attn_x, L0_attn_w_qkv, L0_attn_w_proj, L0_attn_b_proj,
           L0_ffn_Wr, L0_ffn_W1, L0_ffn_W2,
           L1_attn_W_qkv, L1_attn_W_out, L1_attn_b_out,
           L1_ffn_w_freq_real, L1_ffn_w_freq_imag, L1_ffn_bias):
    B = L0_attn_x.shape[0]
    cores = list(range(B))
    x = np.ascontiguousarray(L0_attn_x, dtype=np.float32)
    masks = _mk_masks()
    fr, fi, gr, gi = _dft_consts()
    w1r = np.ascontiguousarray(L0_ffn_W1.reshape(8 * 4096, D))
    w2r = np.ascontiguousarray(L0_ffn_W2.reshape(8 * D, 2048))
    wrt = np.ascontiguousarray(L0_ffn_Wr.T)

    nc = build_fused()
    im = [dict(x=np.ascontiguousarray(x[b]), wqkv0=L0_attn_w_qkv,
               wproj0=L0_attn_w_proj, bproj0=L0_attn_b_proj, masks=masks,
               wrt=wrt, w1=w1r, w2=w2r,
               wqkv1=L1_attn_W_qkv, wout1=L1_attn_W_out, bout1=L1_attn_b_out,
               wfr=L1_ffn_w_freq_real, wfi=L1_ffn_w_freq_imag,
               fbias=L1_ffn_bias, fr=fr, fi=fi, fineg=-fi, gr=gr, gineg=-gi)
          for b in cores]
    res = run_bass_kernel_spmd(nc, im, core_ids=cores)
    out = np.stack([res.results[b]["out"] for b in cores], 0)
    return out.astype(np.float32)


def kernel_unfused(L0_attn_x, L0_attn_w_qkv, L0_attn_w_proj, L0_attn_b_proj,
           L0_ffn_Wr, L0_ffn_W1, L0_ffn_W2,
           L1_attn_W_qkv, L1_attn_W_out, L1_attn_b_out,
           L1_ffn_w_freq_real, L1_ffn_w_freq_imag, L1_ffn_bias):
    B = L0_attn_x.shape[0]
    cores = list(range(B))
    x = np.ascontiguousarray(L0_attn_x, dtype=np.float32)
    masks = _mk_masks()
    fr, fi, gr, gi = _dft_consts()

    nc1 = build_stage1()
    im = [dict(x=np.ascontiguousarray(x[b]), wqkv=L0_attn_w_qkv,
               wproj=L0_attn_w_proj, bproj=L0_attn_b_proj, masks=masks)
          for b in cores]
    r1 = run_bass_kernel_spmd(nc1, im, core_ids=cores)
    x1 = [r1.results[b]["x1"] for b in cores]

    nc2 = build_stage2()
    w1r = np.ascontiguousarray(L0_ffn_W1.reshape(8 * 4096, D))
    w2r = np.ascontiguousarray(L0_ffn_W2.reshape(8 * D, 2048))
    wrt = np.ascontiguousarray(L0_ffn_Wr.T)
    im = [dict(x1=x1[b], wrt=wrt, w1=w1r, w2=w2r) for b in cores]
    r2 = run_bass_kernel_spmd(nc2, im, core_ids=cores)
    x2 = [r2.results[b]["x2"] for b in cores]

    nc3 = build_stage3()
    im = [dict(x2=x2[b], wqkv=L1_attn_W_qkv, wout=L1_attn_W_out,
               bout=L1_attn_b_out) for b in cores]
    r3 = run_bass_kernel_spmd(nc3, im, core_ids=cores)
    x3 = [r3.results[b]["x3"] for b in cores]

    nc4 = build_stage4()
    im = [dict(x3=x3[b], wfr=L1_ffn_w_freq_real, wfi=L1_ffn_w_freq_imag,
               bias=L1_ffn_bias, fr=fr, fi=fi, fineg=-fi, gr=gr, gineg=-gi)
          for b in cores]
    r4 = run_bass_kernel_spmd(nc4, im, core_ids=cores)
    out = np.stack([r4.results[b]["out"] for b in cores], 0)
    return out.astype(np.float32)
